# version 08
# baseline (speedup 1.0000x reference)
import math
import numpy as np
from contextlib import ExitStack

import concourse.bass as bass
import concourse.mybir as mybir
from concourse.bass_utils import run_bass_kernel_spmd

F32 = mybir.dt.float32
BF16 = mybir.dt.bfloat16
ALU = mybir.AluOpType
AF = mybir.ActivationFunctionType
AX = mybir.AxisListType.X

T = 1024
D = 2048
NT = 8
NO = 16
NG = 4
WO_NC = 8
FF_NC = 4
EPS = 1e-5
SCALE = (D // 16) ** -0.5
LNS = math.log(SCALE)
BIG = 1.0e30

DEBUG = False
_dbg_out = {}


class S:

    def __init__(self, nc, ctx, name):
        self.h = ctx.enter_context(nc.semaphore(name))
        self.n = 0

    def inc(self, inst, k=1):
        inst.then_inc(self.h, k)
        self.n += k
        return self.n

    def wait(self, eng, val=None):
        v = self.n if val is None else val
        if v > 0:
            eng.wait_ge(self.h, v)

    def prebump(self, k=1):
        self.n += k
        return self.n


def build_program():
    nc = bass.Bass("TRN2", target_bir_lowering=False, debug=False)

    x_d = nc.dram_tensor("x", [T, D], F32, kind="ExternalInput").ap()
    wq_d = nc.dram_tensor("wq", [NO, 128, NO, 128], F32, kind="ExternalInput").ap()
    wk_d = nc.dram_tensor("wk", [NO, 128, NO, 128], F32, kind="ExternalInput").ap()
    wv_d = nc.dram_tensor("wv", [NO, 128, NO, 128], F32, kind="ExternalInput").ap()
    wo_d = nc.dram_tensor("wo", [WO_NC, 128, NO, 256], F32, kind="ExternalInput").ap()
    wt_d = nc.dram_tensor("wt", [FF_NC, 128, NO, 512], F32, kind="ExternalInput").ap()
    ar_d = nc.dram_tensor("ar", [128, NO, 136], F32, kind="ExternalInput").ap()
    bb_d = nc.dram_tensor("bb", [128, D], F32, kind="ExternalInput").ap()
    bq_d = nc.dram_tensor("bq", [128, NO], F32, kind="ExternalInput").ap()
    bk_d = nc.dram_tensor("bk", [128, NO], F32, kind="ExternalInput").ap()
    bv_d = nc.dram_tensor("bv", [128, NO], F32, kind="ExternalInput").ap()
    bo_d = nc.dram_tensor("bo", [1, D], F32, kind="ExternalInput").ap()
    idn_d = nc.dram_tensor("idn", [128, 128], F32, kind="ExternalInput").ap()
    e2_d = nc.dram_tensor("e2", [8, 128], F32, kind="ExternalInput").ap()
    ones_d = nc.dram_tensor("ones1", [1, 128], F32, kind="ExternalInput").ap()
    mask_d = nc.dram_tensor("maskc", [8, 1], F32, kind="ExternalInput").ap()
    eps_d = nc.dram_tensor("epsc", [128, 1], F32, kind="ExternalInput").ap()
    y_d = nc.dram_tensor("y", [T, D], F32, kind="ExternalOutput").ap()

    x2sp = nc.dram_tensor("x2sp", [T, D], F32, kind="Internal").ap()
    cin = [nc.dram_tensor(f"cin{g}", [1, 512], F32, kind="Internal").ap()
           for g in range(NG)]
    cout = [nc.dram_tensor(f"cout{g}", [8, 512], F32, kind="Internal",
                           addr_space="Shared").ap()
            for g in range(NG)]

    dbg = {}
    if DEBUG:
        for nm, shp in [
            ("d_h1t", [128, NO * T // 2]), ("d_phi", [128, T]),
            ("d_ukv", [128, T]), ("d_ctx", [128, NO * T // 2]),
            ("d_a", [128, D]), ("d_x2", [128, D]),
            ("d_lg", [128, 8]), ("d_w", [128, 8]),
            ("d_sx", [128, T // 2]), ("d_f", [128, D]),
            ("d_pref", [128, NO]), ("d_tot", [128, NO]),
            ("d_h3t", [128, NO * T // 2]), ("d_ax", [128, T]),
            ("d_x27", [128, D]), ("d_h3b7", [128, D // 2]),
            ("d_a7", [128, D]), ("d_st7", [128, 56]), ("d_asum", [128, 64]),
            ("d_f7", [128, D]), ("d_st47", [128, 56]), ("d_fsum", [128, 32]),
            ("d_xr7", [128, D]),
            ("d_wf", [8, T // 2]), ("d_lg2", [8, T]),
        ]:
            dbg[nm] = nc.dram_tensor(nm, shp, F32, kind="ExternalOutput").ap()

    ctx_mgr = ExitStack()
    with ctx_mgr as ctx:
        def sb(name, shape, dt=F32):
            return ctx.enter_context(nc.sbuf_tensor(name, shape, dt))

        def psum(name, shape, dt=F32):
            return ctx.enter_context(nc.psum_tensor(name, shape, dt))

        A_own = sb("A_own", [128, 8192])
        B_own = sb("B_own", [128, 8192])
        C_own = sb("C_own", [128, 16384])
        D_own = sb("D_own", [128, 6144])
        WOX = sb("WOX", [128, 6144])
        E_own = sb("E_own", [128, 2048])
        STG = sb("STG", [128, 2048])
        FB = sb("FB", [128, 2112])
        stats = sb("stats", [128, 56])
        asum = sb("asum", [128, 64])
        fsum = sb("fsum", [128, 32])
        xs1 = sb("xs1", [128, 8])
        bqc = sb("bqc", [128, NO]); bkc = sb("bkc", [128, NO]); bvc = sb("bvc", [128, NO])
        bqLn = sb("bqLn", [128, NO]); bqS2 = sb("bqS2", [128, NO])
        idn = sb("idn_s", [128, 128])
        e2f = sb("e2f", [8, 64])
        onesf = sb("onesf", [1, 64])
        bof = sb("bof", [1, 1024])
        maskc = sb("maskc_s", [8, 1])
        epsc = sb("epsc_s", [128, 1])
        tot0 = sb("tot0", [128, NO]); tot1 = sb("tot1", [128, NO])
        totS = sb("totS", [128, NO])
        pref = sb("pref", [128, NO])
        ltm = sb("ltm", [128, 8])
        r8a = sb("r8a", [128, 8]); r8b = sb("r8b", [128, 8]); r8c = sb("r8c", [128, 8])
        rc = sb("rc", [128, 8])
        wtm = sb("wtm", [128, 8])

        gsb = FB.ap()[0:8, 0:512]
        lgsb = E_own.ap()[0:8, 0:1024]
        wfm = E_own.ap()[0:8, 1024:1536]
        h1T = A_own.ap().bitcast(BF16).rearrange("p (n t) -> p n t", n=NO)
        ctxT = B_own.ap().bitcast(BF16).rearrange("p (n t) -> p n t", n=NO)
        wtch = B_own.ap().bitcast(BF16).rearrange("p (c n f) -> p c n f", c=2, n=NO)
        Cap = C_own.ap()
        C3 = Cap.rearrange("p (j f) -> p j f", j=NT)
        qkvw = D_own.ap().bitcast(BF16).rearrange("p (s f) -> p s f", s=6)
        woch = WOX.ap().bitcast(BF16).rearrange("p (c n f) -> p c n f", c=3, n=NO)
        dump6 = WOX.ap().rearrange("p (c f) -> p c f", c=3)
        h1b = FB.ap()[:, 0:1024].bitcast(BF16)
        bbv = FB.ap()[:, 0:1024].bitcast(BF16)
        arb = FB.ap()[:, 1024:2112].bitcast(BF16).rearrange(
            "p (n f) -> p n f", n=NO)
        e2b = e2f.ap().bitcast(BF16)
        onesb = onesf.ap().bitcast(BF16)
        bob = bof.ap().bitcast(BF16)
        axs = STG.ap()[:, 0:1024]
        sxb = STG.ap()[:, 1024:1536].bitcast(BF16)
        wfmb = wfm.bitcast(BF16)

        pb0 = psum("pb0", [128, 1024])
        pb1 = psum("pb1", [128, 1024])
        pvv = psum("pvv", [128, 1024])
        psm = psum("psm", [128, 512])

        stX = S(nc, ctx, "stX")
        stTP = S(nc, ctx, "stTP")
        stW = S(nc, ctx, "stW")
        stCIN = S(nc, ctx, "stCIN")
        stGSB = S(nc, ctx, "stGSB")
        stSP = S(nc, ctx, "stSP")
        stXR = S(nc, ctx, "stXR")
        stOUT = S(nc, ctx, "stOUT")
        stDBG = S(nc, ctx, "stDBG")
        sCC = S(nc, ctx, "sCC")
        sPL = S(nc, ctx, "sPL")
        sTE = S(nc, ctx, "sTE")
        sDV = S(nc, ctx, "sDV")
        sSC = S(nc, ctx, "sSC")

        blk = ctx.enter_context(nc.Block())
        snap = {}
        sc_defer = []

        def SCD(fn):
            sc_defer.append(fn)

        def dbg_dump(sy, name, src_ap):
            if DEBUG and name in dbg:
                stDBG.inc(sy.dma_start(out=dbg[name], in_=src_ap), 16)

        def _(g):
            for dst, src in [
                (bqc.ap(), bq_d), (bkc.ap(), bk_d), (bvc.ap(), bv_d),
                (idn.ap(), idn_d), (maskc.ap(), mask_d), (epsc.ap(), eps_d),
            ]:
                stW.inc(g.dma_start(out=dst, in_=src), 16)
            stW.inc(g.dma_start(out=e2b, in_=e2_d), 16)
            stW.inc(g.dma_start(out=onesb, in_=ones_d), 16)
            stW.inc(g.dma_start(out=bob, in_=bo_d), 16)
            stW.inc(g.dma_start(out=arb, in_=ar_d), 16)
            snap["consts"] = stW.n
        blk.gpsimd(_)

        def _(v):
            stW.wait(v, snap["consts"])
            sDV.inc(v.tensor_single_scalar(bqLn.ap(), bqc.ap(), LNS, ALU.add))
            sDV.inc(v.tensor_single_scalar(bqS2.ap(), bqc.ap(), SCALE, ALU.mult))
            snap["dbias"] = sDV.n
        blk.vector(_)

        SC_S, SC_Q, SC_M, SC_M2, SC_VAR, SC_STD, SC_R = 0, 8, 16, 24, 32, 40, 48

        def stc(base, j):
            return stats.ap()[:, base + j:base + j + 1]

        def ln_stats_A(j, src_ap, dump_ap, tag, sum_on_scalar=False):
            if sum_on_scalar:
                def _(s):
                    s.activation(dump_ap, src_ap, AF.Copy,
                                 accum_out=stc(SC_S, j)).then_inc(sSC.h, 1)
                SCD(_)
                snap[(tag, "s", j)] = ("sc", sSC.prebump())
            else:
                def _(v):
                    sDV.inc(v.reduce_sum(stc(SC_S, j), src_ap, AX))
                    snap[(tag, "s", j)] = ("dv", sDV.n)
                blk.vector(_)

            def _(s):
                s.activation(dump_ap, src_ap, AF.Square,
                             accum_out=stc(SC_Q, j)).then_inc(sSC.h, 1)
            SCD(_)
            snap[(tag, "q", j)] = ("sc", sSC.prebump())

        def ln_stats_B(j, tag):
            kind, sval = snap[(tag, "s", j)]
            qk, qval = snap[(tag, "q", j)]

            def _(v):
                if qk == "sc":
                    sSC.wait(v, qval)
                else:
                    sDV.wait(v, qval)
                if kind == "dv":
                    sDV.wait(v, sval)
                sDV.inc(v.tensor_single_scalar(stc(SC_M, j), stc(SC_S, j),
                                               1.0 / D, ALU.mult))
                sDV.wait(v)
                sDV.inc(v.tensor_tensor(stc(SC_M2, j), stc(SC_M, j),
                                        stc(SC_M, j), ALU.mult))
                sDV.wait(v)
                sDV.inc(v.scalar_tensor_tensor(stc(SC_VAR, j), stc(SC_Q, j),
                                               1.0 / D, stc(SC_M2, j),
                                               ALU.mult, ALU.subtract))
                snap[(tag, "var", j)] = sDV.n
            blk.vector(_)

            vvar = snap[(tag, "var", j)]
            vln = sSC.prebump()
            snap[(tag, "r", j)] = sSC.prebump()

            def _(s, vvar=vvar, vln=vln):
                sDV.wait(s, vvar)
                s.activation(stc(SC_STD, j), stc(SC_VAR, j),
                             AF.Ln, bias=epsc.ap()).then_inc(sSC.h, 1)
                s.wait_ge(sSC.h, vln)
                s.activation(stc(SC_R, j), stc(SC_STD, j),
                             AF.Exp, scale=-0.5).then_inc(sSC.h, 1)
            SCD(_)

        def p1_xslot(j):
            return E_own.ap() if j % 2 == 0 else STG.ap()

        def emit_ln1_B(j):
            xslot = p1_xslot(j)
            ln_stats_B(j, "ln1")

            def _(v, j=j):
                sDV.inc(v.tensor_copy(xs1.ap()[:, j:j + 1], stc(SC_S, j)))
            blk.vector(_)

            def _(v, j=j, xslot=xslot):
                sSC.wait(v, snap[("ln1", "r", j)])
                if j >= 1:
                    stTP.wait(v, snap[("tp1", j - 1)])
                sDV.inc(v.tensor_scalar(h1b, xslot, stc(SC_M, j), stc(SC_R, j),
                                        ALU.subtract, ALU.mult))
                snap[("ln1h1b", j)] = sDV.n
            blk.vector(_)

            def _(sy, j=j):
                sDV.wait(sy, snap[("ln1h1b", j)])
                src = h1b.rearrange("p (n t) -> p n t", n=NO)
                dst = h1T[:, :, j * 128:(j + 1) * 128]
                stTP.inc(sy.dma_start(out=dst, in_=src, transpose=True), 16)
                snap[("tp1", j)] = stTP.n
            blk.sync(_)

        for j in range(NT):
            xslot = p1_xslot(j)

            vgate = snap[("ln1h1b", j - 2)] if j >= 2 else 0

            def _(s, j=j, xslot=xslot, vgate=vgate):
                if vgate:
                    sDV.wait(s, vgate)
                s.dma_start(out=xslot,
                            in_=x_d[j * 128:(j + 1) * 128, :]).then_inc(stX.h, 16)
            SCD(_)
            snap[("xload", j)] = stX.prebump(16)

            def _(v, j=j):
                stX.wait(v, snap[("xload", j)])
            blk.vector(_)

            vx = snap[("xload", j)]

            def _(s, vx=vx):
                stX.wait(s, vx)
            SCD(_)

            ln_stats_A(j, xslot, Cap[:, (j % 2) * 2048:(j % 2) * 2048 + 2048], "ln1")
            if j >= 1:
                emit_ln1_B(j - 1)
        emit_ln1_B(NT - 1)

        slot_last_te = [0] * 6

        def load_qkv(g, i):
            for wi, wsrc in enumerate((wq_d, wk_d, wv_d)):
                s6 = (3 * i + wi) % 6
                sTE.wait(g, slot_last_te[s6])
                stW.inc(g.dma_start(out=qkvw[:, s6, :],
                                    in_=wsrc[i].rearrange("p n f -> p (n f)")), 16)
            snap[("wq", i)] = stW.n

        def emit_cc(g, grp):
            stCIN.wait(g, snap[("cin", grp)])
            sCC.inc(g.collective_compute(
                "AllGather", ALU.bypass, replica_groups=[list(range(8))],
                ins=[cin[grp].opt()], outs=[cout[grp].opt()]))
            snap[("cc", grp)] = sCC.n

        def emit_prefix(grp2):
            def _(sy):
                sCC.wait(sy, snap[("cc", grp2)])
                if grp2 > 0:
                    sTE.wait(sy, snap[("prefmm", grp2 - 1)])
                stGSB.inc(sy.dma_start(out=gsb, in_=cout[grp2]), 16)
                snap[("gsb", grp2)] = stGSB.n
            blk.sync(_)

            def _(t):
                stGSB.wait(t, snap[("gsb", grp2)])
                if grp2 > 0:
                    sDV.wait(t, snap[("prefevac", grp2 - 1)])
                gv = gsb.rearrange("r (c g) -> r c g", g=4)
                inst = None
                for q in range(4):
                    inst = t.matmul(psm.ap()[:, q:q + 1], gv[:, :, q],
                                    maskc.ap(), start=True, stop=True)
                sTE.inc(inst)
                snap[("prefmm", grp2)] = sTE.n
            blk.tensor(_)

            def _(v):
                sTE.wait(v, snap[("prefmm", grp2)])
                sDV.inc(v.tensor_copy(pref.ap()[:, grp2 * 4:grp2 * 4 + 4],
                                      psm.ap()[:, 0:4]))
                snap[("prefevac", grp2)] = sDV.n
            blk.vector(_)

        def emit_scan_ctx(i2):
            grp2 = i2 // 4

            def _(v):
                sDV.wait(v, snap[("prefevac", grp2)])
                slot = i2 % 8
                ub = Cap[:, 8192 + slot * 1024: 8192 + (slot + 1) * 1024]
                sDV.inc(v.tensor_tensor_scan(ub, ub, ub,
                                             pref.ap()[:, i2:i2 + 1],
                                             ALU.add, ALU.bypass))
                sDV.wait(v)
                sDV.inc(v.tensor_tensor(ctxT[:, i2, :], ub,
                                        Cap[:, slot * 1024:(slot + 1) * 1024],
                                        ALU.mult))
                snap[("ctx", i2)] = sDV.n
            blk.vector(_)

        def _(g):
            load_qkv(g, 0)
            load_qkv(g, 1)
        blk.gpsimd(_)

        for i in range(NO):
            for h in range(2):
                tsl = slice(h * 512, (h + 1) * 512)

                def _(t, i=i, h=h, tsl=tsl):
                    stW.wait(t, snap[("wq", i)])
                    if i == 0 and h == 0:
                        sDV.wait(t, snap["dbias"])
                    if (i, h) > (0, 0):
                        pi, ph = (i, 0) if h == 1 else (i - 1, 1)
                        sSC.wait(t, snap[("kacts", pi, ph)])
                        sDV.wait(t, snap[("u", pi, ph)])
                    for wi, pst in ((0, pb0), (1, pb1), (2, pvv)):
                        s6 = (3 * i + wi) % 6
                        inst = None
                        if i == 0:
                            for d in range(NO):
                                for j4 in range(4):
                                    jj = h * 4 + j4
                                    if wi == 0 and d == 0:
                                        stTP.wait(t, snap[("tp1", jj)])
                                    inst = t.matmul(
                                        pst.ap()[:, h * 512 + j4 * 128:
                                                 h * 512 + (j4 + 1) * 128],
                                        qkvw[:, s6, d * 128:(d + 1) * 128],
                                        h1T[:, d, jj * 128:(jj + 1) * 128],
                                        start=(d == 0 and j4 == 0),
                                        stop=(d == NO - 1),
                                        skip_group_check=True)
                        else:
                            for d in range(NO):
                                inst = t.matmul(
                                    pst.ap()[:, tsl],
                                    qkvw[:, s6, d * 128:(d + 1) * 128],
                                    h1T[:, d, tsl],
                                    start=(d == 0), stop=(d == NO - 1))
                        sTE.inc(inst)
                        snap[("mm", wi, i, h)] = sTE.n
                    for wi in range(3):
                        slot_last_te[(3 * i + wi) % 6] = sTE.n
                blk.tensor(_)

                vu_prev = 0
                if (i, h) > (0, 0):
                    pi, ph = (i, 0) if h == 1 else (i - 1, 1)
                    vu_prev = snap[("u", pi, ph)]
                vmm0 = None

                def _(s, i=i, h=h, tsl=tsl, vu_prev=vu_prev):
                    if vu_prev:
                        sDV.wait(s, vu_prev)
                    if i == 0 and h == 0:
                        sDV.wait(s, snap[("ln1h1b", NT - 1)])
                    sTE.wait(s, snap[("mm", 0, i, h)])
                    s.activation(STG.ap()[:, 0:512], pb0.ap()[:, tsl],
                                 AF.Exp, bias=bqLn.ap()[:, i:i + 1]).then_inc(sSC.h, 1)
                    s.activation(STG.ap()[:, 512:1024], pb0.ap()[:, tsl],
                                 AF.Relu, bias=bqS2.ap()[:, i:i + 1],
                                 scale=SCALE).then_inc(sSC.h, 1)
                    sTE.wait(s, snap[("mm", 1, i, h)])
                    s.activation(STG.ap()[:, 1024:1536], pb1.ap()[:, tsl],
                                 AF.Exp, bias=bkc.ap()[:, i:i + 1]).then_inc(sSC.h, 1)
                    s.activation(STG.ap()[:, 1536:2048], pb1.ap()[:, tsl],
                                 AF.Relu, bias=bkc.ap()[:, i:i + 1]).then_inc(sSC.h, 1)
                SCD(_)
                sSC.prebump(2)
                snap[("qacts", i, h)] = sSC.n
                sSC.prebump(2)
                snap[("kacts", i, h)] = sSC.n

                def _(v, i=i, h=h, tsl=tsl):
                    slot = i % 8
                    sSC.wait(v, snap[("qacts", i, h)])
                    sDV.inc(v.scalar_tensor_tensor(
                        Cap[:, slot * 1024 + h * 512: slot * 1024 + (h + 1) * 512],
                        STG.ap()[:, 0:512], SCALE, STG.ap()[:, 512:1024],
                        ALU.min, ALU.add))
                    sSC.wait(v, snap[("kacts", i, h)])
                    sDV.inc(v.scalar_tensor_tensor(
                        STG.ap()[:, 1024:1536],
                        STG.ap()[:, 1024:1536], 1.0, STG.ap()[:, 1536:2048],
                        ALU.min, ALU.add))
                    sDV.wait(v)
                    sTE.wait(v, snap[("mm", 2, i, h)])
                    tcol = tot0 if h == 0 else tot1
                    sDV.inc(v.scalar_tensor_tensor(
                        Cap[:, 8192 + slot * 1024 + h * 512:
                            8192 + slot * 1024 + (h + 1) * 512],
                        pvv.ap()[:, tsl], bvc.ap()[:, i:i + 1],
                        STG.ap()[:, 1024:1536],
                        ALU.add, ALU.mult,
                        accum_out=tcol.ap()[:, i:i + 1]))
                    snap[("u", i, h)] = sDV.n
                blk.vector(_)

            grp = i // 4
            if i % 4 == 3:
                g4 = slice(grp * 4, grp * 4 + 4)

                def _(v, grp=grp, g4=g4):
                    sDV.wait(v)
                    sDV.inc(v.tensor_tensor(totS.ap()[:, g4], tot0.ap()[:, g4],
                                            tot1.ap()[:, g4], ALU.add))
                    snap[("tot", grp)] = sDV.n
                blk.vector(_)

                def _(sy, grp=grp, g4=g4):
                    sDV.wait(sy, snap[("tot", grp)])
                    dst = cin[grp].rearrange("a (c g) -> a c g", g=4)[0]
                    stCIN.inc(sy.dma_start(out=dst, in_=totS.ap()[:, g4]), 16)
                    snap[("cin", grp)] = stCIN.n
                blk.sync(_)

            def _(g, i=i):
                if i + 2 < NO:
                    load_qkv(g, i + 2)
                if i in (11, 12, 13):
                    w0 = i - 11
                    stW.inc(g.dma_start(out=woch[:, w0, :, :], in_=wo_d[w0]), 16)
                    snap[("wo", w0)] = stW.n
                if i % 4 == 3:
                    emit_cc(g, i // 4)
            blk.gpsimd(_)

            _tails = {6: [("p", 0), ("s", 0)], 7: [("s", 1), ("s", 2)],
                      8: [("s", 3)], 9: [("p", 1), ("s", 4)],
                      10: [("s", 5), ("s", 6)], 11: [("s", 7)],
                      13: [("p", 2), ("s", 8)], 14: [("s", 9), ("s", 10)],
                      15: [("s", 11)]}
            for kind, arg in _tails.get(i, []):
                if kind == "p":
                    emit_prefix(arg)
                else:
                    emit_scan_ctx(arg)

        emit_prefix(3)
        for i2 in range(12, 16):
            emit_scan_ctx(i2)

        if DEBUG:
            def _(sy):
                sDV.wait(sy, snap[("ctx", 15)])
                dbg_dump(sy, "d_tot", totS.ap())
                dbg_dump(sy, "d_pref", pref.ap())
                dbg_dump(sy, "d_phi", Cap[:, 7 * 1024:8 * 1024])
                dbg_dump(sy, "d_ukv", Cap[:, 8192 + 7 * 1024: 8192 + 8 * 1024])
                dbg_dump(sy, "d_h1t", A_own.ap())
                dbg_dump(sy, "d_ctx", B_own.ap())
            blk.sync(_)

        def emit_ln23_A(j):
            xslot = E_own.ap() if j % 2 == 0 else STG.ap()
            arow = C3[:, j, :]
            dmp2 = D_own.ap()[:, (j % 2) * 2048:(j % 2) * 2048 + 2048]

            if j >= 2:
                vgate2 = ("dv", snap[("h3b", j - 2)])
            elif j == 1:
                vgate2 = ("dv", snap[("u", NO - 1, 1)])
            else:
                vgate2 = None

            def _(s, j=j, xslot=xslot, vgate2=vgate2):
                if vgate2 is not None:
                    sDV.wait(s, vgate2[1])
                s.dma_start(out=xslot,
                            in_=x_d[j * 128:(j + 1) * 128, :]).then_inc(stX.h, 16)
            SCD(_)
            snap[("xload2", j)] = stX.prebump(16)

            def _(v, j=j):
                sSC.wait(v, snap[("aevac", WO_NC - 1, j)])
                sDV.inc(v.reduce_sum(stc(SC_S, j),
                                     asum.ap()[:, j * 8:(j + 1) * 8], AX))
                snap[("ln2_%d" % j, "s", j)] = ("dv", sDV.n)
            blk.vector(_)

            def _(s, j=j):
                sSC.wait(s, snap[("aevac", WO_NC - 1, j)])
            SCD(_)

            tag2 = "ln2_%d" % j

            def _(s, j=j, tag2=tag2):
                s.activation(dmp2, arow, AF.Square,
                             accum_out=stc(SC_Q, j)).then_inc(sSC.h, 1)
            SCD(_)
            snap[(tag2, "q", j)] = ("sc", sSC.prebump())

        def emit_ln23_B(j):
            xslot = E_own.ap() if j % 2 == 0 else STG.ap()
            arow = C3[:, j, :]
            dmp2 = D_own.ap()[:, (j % 2) * 2048:(j % 2) * 2048 + 2048]

            ln_stats_B(j, "ln2_%d" % j)

            if DEBUG and j == NT - 1:
                def _(sy, j=j, arow=arow):
                    sSC.wait(sy, snap[("ln2_%d" % j, "r", j)])
                    dbg_dump(sy, "d_a7", arow)
                    dbg_dump(sy, "d_st7", stats.ap())
                    dbg_dump(sy, "d_asum", asum.ap())
                blk.sync(_)

            def _(v, j=j, xslot=xslot, arow=arow):
                sSC.wait(v, snap[("ln2_%d" % j, "r", j)])
                if DEBUG and j == NT - 1:
                    stDBG.wait(v)
                sDV.inc(v.tensor_scalar(arow, arow, stc(SC_M, j), stc(SC_R, j),
                                        ALU.subtract, ALU.mult))
                snap[("h2", j)] = sDV.n
            blk.vector(_)

            def _(g, j=j, xslot=xslot, arow=arow):
                sDV.wait(g, snap[("h2", j)])
                stX.wait(g, snap[("xload2", j)])
                sPL.inc(g.tensor_tensor(xslot, arow, xslot, ALU.add))
                snap[("x2", j)] = sPL.n
            blk.gpsimd(_)

            vx2 = snap[("x2", j)]

            def _(s, j=j, xslot=xslot, vx2=vx2):
                sPL.wait(s, vx2)
                s.dma_start(out=x2sp[j * 128:(j + 1) * 128, :],
                            in_=xslot).then_inc(stSP.h, 16)
            SCD(_)
            snap[("sp", j)] = stSP.prebump(16)

            if DEBUG and (j == 0 or j == NT - 1):
                def _(sy, j=j, xslot=xslot):
                    sPL.wait(sy, snap[("x2", j)])
                    if j == 0:
                        dbg_dump(sy, "d_x2", xslot)
                    else:
                        dbg_dump(sy, "d_x27", xslot)
                blk.sync(_)

            tag3 = "ln3_%d" % j

            def _(v, j=j, xslot=xslot, vx2=vx2):
                sDV.inc(v.tensor_copy(stc(SC_S, j), xs1.ap()[:, j:j + 1]))
                snap[(tag3, "s", j)] = ("dv", sDV.n)
                sPL.wait(v, vx2)
                sDV.inc(v.scalar_tensor_tensor(
                    dmp2, xslot, 1.0, xslot, ALU.bypass, ALU.mult,
                    accum_out=stc(SC_Q, j)))
                snap[(tag3, "q", j)] = ("dv", sDV.n)
            blk.vector(_)

            ln_stats_B(j, "ln3_%d" % j)

            def _(v, j=j, xslot=xslot):
                sSC.wait(v, snap[("ln3_%d" % j, "r", j)])
                if j >= 1:
                    stTP.wait(v, snap[("tp3", j - 1)])
                else:
                    stTP.wait(v, snap[("tp1", NT - 1)])
                sDV.inc(v.tensor_scalar(h1b, xslot, stc(SC_M, j), stc(SC_R, j),
                                        ALU.subtract, ALU.mult))
                snap[("h3b", j)] = sDV.n
            blk.vector(_)

            def _(sy, j=j):
                sDV.wait(sy, snap[("h3b", j)])
                src = h1b.rearrange("p (n t) -> p n t", n=NO)
                dst = h1T[:, :, j * 128:(j + 1) * 128]
                stTP.inc(sy.dma_start(out=dst, in_=src, transpose=True), 16)
                snap[("tp3", j)] = stTP.n
            blk.sync(_)


        bank_prev = [None, None]

        def emit_wo_group(c, j):
            bank = (c + j) % 2
            aps = pb0.ap()[:, 0:256] if bank == 0 else pb0.ap()[:, 512:768]
            osl = slice(c * 256, (c + 1) * 256)

            def _(t, c=c, j=j, aps=aps, osl=osl, bprev=bank_prev[bank]):
                stW.wait(t, snap[("wo", c)])
                if bprev is not None:
                    sSC.wait(t, bprev)
                inst = t.matmul(aps, onesb, bob[0:1, osl],
                                start=True, stop=False)
                for ct in range(NO):
                    if c == 0 and j == 0:
                        sDV.wait(t, snap[("ctx", ct)])
                    inst = t.matmul(
                        aps, ctxT[:, ct, j * 128:(j + 1) * 128],
                        woch[:, c % 3, ct, :],
                        start=False, stop=(ct == NO - 1))
                sTE.inc(inst)
                snap[("womm", c, j)] = sTE.n
            blk.tensor(_)

            def _(s, c=c, j=j, aps=aps, osl=osl):
                sTE.wait(s, snap[("womm", c, j)])
                s.activation(C3[:, j, osl], aps, AF.Copy,
                             accum_out=asum.ap()[:, j * 8 + c:j * 8 + c + 1]
                             ).then_inc(sSC.h, 1)
            SCD(_)
            snap[("aevac", c, j)] = sSC.prebump()
            bank_prev[bank] = snap[("aevac", c, j)]

        for c in range(WO_NC - 2):
            for j in range(NT):
                emit_wo_group(c, j)
            if c + 3 < WO_NC:
                def _(g, c=c):
                    sTE.wait(g, snap[("womm", c, NT - 1)])
                    stW.inc(g.dma_start(out=woch[:, (c + 3) % 3, :, :],
                                        in_=wo_d[c + 3]), 16)
                    snap[("wo", c + 3)] = stW.n
                blk.gpsimd(_)

        for j in range(NT):
            emit_wo_group(WO_NC - 2, j)
            emit_wo_group(WO_NC - 1, j)
            emit_ln23_A(j)
            if j >= 1:
                emit_ln23_B(j - 1)

        emit_ln23_B(NT - 1)

        def _(g):
            stTP.wait(g, snap[("tp3", NT - 1)])
            stW.inc(g.dma_start(out=bbv, in_=bb_d), 16)
            snap["bb"] = stW.n
            sTE.wait(g, snap[("womm", WO_NC - 1, NT - 1)])
            stW.inc(g.dma_start(out=wtch[:, 0, :, :], in_=wt_d[0]), 16)
            snap[("wt", 0)] = stW.n
            stW.inc(g.dma_start(out=wtch[:, 1, :, :], in_=wt_d[1]), 16)
            snap[("wt", 1)] = stW.n
        blk.gpsimd(_)

        if DEBUG:
            def _(sy):
                stTP.wait(sy, snap[("tp3", NT - 1)])
                dbg_dump(sy, "d_h3t", A_own.ap())
            blk.sync(_)

        for hh in range(2):
            pslice = pb1.ap()[:, hh * 512:(hh + 1) * 512]

            def _(t, hh=hh, pslice=pslice):
                stTP.wait(t, snap[("tp3", 3 if hh == 0 else NT - 1)])
                if hh == 0:
                    sSC.wait(t, snap[("kacts", NO - 1, 1)])
                inst = None
                for d in range(NO):
                    inst = t.matmul(pslice, arb[:, d, 0:128],
                                    h1T[:, d, hh * 512:(hh + 1) * 512],
                                    start=(d == 0), stop=(d == NO - 1))
                sTE.inc(inst)
                snap[("axmm", hh)] = sTE.n
            blk.tensor(_)

            def _(v, hh=hh, pslice=pslice):
                sTE.wait(v, snap[("axmm", hh)])
                if hh == 0:
                    stSP.wait(v, snap[("sp", NT - 1)])
                sDV.inc(v.tensor_copy(axs[:, hh * 512:(hh + 1) * 512], pslice))
                snap[("axev", hh)] = sDV.n
            blk.vector(_)

            def _(t, hh=hh, pslice=pslice):
                sDV.wait(t, snap[("axev", hh)])
                inst = None
                for d in range(NO):
                    inst = t.matmul(pslice[0:8, :], arb[:, d, 128:136],
                                    h1T[:, d, hh * 512:(hh + 1) * 512],
                                    start=(d == 0), stop=(d == NO - 1))
                sTE.inc(inst)
                snap[("lgmm", hh)] = sTE.n
            blk.tensor(_)

            def _(v, hh=hh, pslice=pslice):
                sTE.wait(v, snap[("lgmm", hh)])
                sDV.inc(v.tensor_copy(lgsb[0:8, hh * 512:(hh + 1) * 512],
                                      pslice[0:8, :]))
                snap[("lgev", hh)] = sDV.n
            blk.vector(_)

        ffn_bank_prev = [("dv", snap[("u", NO - 1, 1)]),
                         ("dv", snap[("u", NO - 1, 1)])]

        def emit_ffn_group(c, j):
            bank = j % 2
            fps = pvv.ap()[:, 0:512] if bank == 0 else pvv.ap()[:, 512:1024]
            osl = slice(c * 512, (c + 1) * 512)

            def _(t, c=c, j=j, fps=fps, osl=osl, bprev=ffn_bank_prev[bank]):
                stW.wait(t, snap[("wt", c)])
                if c == 0:
                    stW.wait(t, snap["bb"])
                    sDV.wait(t, snap[("sx", j)])
                kind, val = bprev
                if kind == "dv":
                    sDV.wait(t, val)
                else:
                    sSC.wait(t, val)
                for d in range(NO):
                    t.matmul(fps, h1T[:, d, j * 128:(j + 1) * 128],
                             wtch[:, c % 2, d, :],
                             start=(d == 0), stop=False)
                inst = t.matmul(fps, sxb[:, j * 128:(j + 1) * 128],
                                bbv[:, osl], start=False, stop=True)
                sTE.inc(inst)
                snap[("fmm", c, j)] = sTE.n
            blk.tensor(_)

            def _(s, c=c, j=j, fps=fps, osl=osl):
                sTE.wait(s, snap[("fmm", c, j)])
                s.activation(C3[:, j, osl], fps, AF.Copy,
                             accum_out=fsum.ap()[:, j * 4 + c:j * 4 + c + 1]
                             ).then_inc(sSC.h, 1)
            SCD(_)
            snap[("fevac", c, j)] = sSC.prebump()
            ffn_bank_prev[bank] = ("sc", snap[("fevac", c, j)])

        for j in range(NT):
            hh = j // 4

            def _(t, j=j, hh=hh):
                sDV.wait(t, snap[("lgev", hh)])
                if j > 0:
                    sDV.wait(t, snap[("sx", j - 1)])
                sTE.inc(t.transpose(psm.ap()[:, 0:8],
                                    lgsb[0:8, j * 128:(j + 1) * 128],
                                    idn.ap()[0:8, 0:8]))
                snap[("lgT", j)] = sTE.n
            blk.tensor(_)

            def _(v, j=j):
                sTE.wait(v, snap[("lgT", j)])
                sDV.inc(v.tensor_copy(ltm.ap(), psm.ap()[:, 0:8]))
                sDV.wait(v)
                sDV.inc(v.reduce_max(rc.ap()[:, 0:1], ltm.ap(), AX))
                sDV.wait(v)
                sDV.inc(v.tensor_single_scalar(r8a.ap(), ltm.ap(),
                                               rc.ap()[:, 0:1], ALU.is_ge))
                sDV.wait(v)
                sDV.inc(v.scalar_tensor_tensor(r8b.ap(), r8a.ap(), -BIG,
                                               ltm.ap(), ALU.mult, ALU.add))
                sDV.wait(v)
                sDV.inc(v.reduce_max(rc.ap()[:, 1:2], r8b.ap(), AX))
                sDV.wait(v)
                sDV.inc(v.tensor_single_scalar(rc.ap()[:, 2:3], rc.ap()[:, 0:1],
                                               -1.0, ALU.mult))
                sDV.wait(v)
                sDV.inc(v.tensor_tensor(rc.ap()[:, 3:4], rc.ap()[:, 1:2],
                                        rc.ap()[:, 0:1], ALU.subtract))
                snap[("rmath", j)] = sDV.n
            blk.vector(_)

            def _(s, j=j):
                sDV.wait(s, snap[("rmath", j)])
                s.activation(r8c.ap(), ltm.ap(), AF.Exp,
                             bias=rc.ap()[:, 2:3]).then_inc(sSC.h, 1)
                s.activation(rc.ap()[:, 4:5], rc.ap()[:, 3:4],
                             AF.Exp).then_inc(sSC.h, 1)
            SCD(_)
            snap[("rexp", j)] = sSC.prebump(2)

            def _(v, j=j):
                sSC.wait(v, snap[("rexp", j)])
                sDV.inc(v.tensor_single_scalar(r8a.ap(), ltm.ap(),
                                               rc.ap()[:, 1:2], ALU.is_ge))
                sDV.inc(v.tensor_single_scalar(rc.ap()[:, 5:6], rc.ap()[:, 4:5],
                                               1.0, ALU.add))
                sDV.wait(v)
                sDV.inc(v.reciprocal(rc.ap()[:, 6:7], rc.ap()[:, 5:6]))
                sDV.wait(v)
                sDV.inc(v.tensor_tensor(r8b.ap(), r8c.ap(), r8a.ap(), ALU.mult))
                sDV.wait(v)
                sDV.inc(v.tensor_single_scalar(wtm.ap(), r8b.ap(),
                                               rc.ap()[:, 6:7], ALU.mult))
                snap[("w", j)] = sDV.n
            blk.vector(_)

            if j >= 1:
                emit_ffn_group(0, j - 1)

            def _(t, j=j):
                sDV.wait(t, snap[("w", j)])
                sTE.inc(t.transpose(psm.ap()[0:8, 128:256], wtm.ap(), idn.ap()))
                snap[("wT", j)] = sTE.n
            blk.tensor(_)

            def _(v, j=j):
                sTE.wait(v, snap[("wT", j)])
                sDV.inc(v.tensor_copy(wfmb[0:8, j * 128:(j + 1) * 128],
                                      psm.ap()[0:8, 128:256]))
                snap[("wev", j)] = sDV.n
            blk.vector(_)

            def _(t, j=j):
                sDV.wait(t, snap[("wev", j)])
                sTE.inc(t.matmul(psm.ap()[:, 256:384], e2b,
                                 wfmb[0:8, j * 128:(j + 1) * 128],
                                 start=True, stop=True))
                snap[("wexpmm", j)] = sTE.n
            blk.tensor(_)

            def _(v, j=j, hh=hh):
                sTE.wait(v, snap[("wexpmm", j)])
                sDV.inc(v.tensor_tensor(sxb[:, j * 128:(j + 1) * 128],
                                        axs[:, j * 128:(j + 1) * 128],
                                        psm.ap()[:, 256:384], ALU.mult))
                snap[("sx", j)] = sDV.n
            blk.vector(_)

        emit_ffn_group(0, NT - 1)

        def _(g):
            sTE.wait(g, snap[("fmm", 0, NT - 1)])
            stW.inc(g.dma_start(out=wtch[:, 0, :, :], in_=wt_d[2]), 16)
            snap[("wt", 2)] = stW.n
        blk.gpsimd(_)

        if DEBUG:
            def _(sy):
                sDV.wait(sy, snap[("lgev", 1)])
                dbg_dump(sy, "d_lg2", lgsb[0:8, :])
                sDV.wait(sy, snap[("sx", NT - 1)])
                dbg_dump(sy, "d_sx", sxb.bitcast(F32))
                dbg_dump(sy, "d_ax", axs)
                dbg_dump(sy, "d_wf", wfmb.bitcast(F32))
            blk.sync(_)

        def emit_p6_A(j):
            frow = C3[:, j, :]
            dmp = dump6[:, 0, :]

            def _(v, j=j):
                sSC.wait(v, snap[("fevac", FF_NC - 1, j)])
            blk.vector(_)

            def _(s, j=j):
                sSC.wait(s, snap[("fevac", FF_NC - 1, j)])
                if j == 0:
                    sTE.wait(s, snap[("womm", WO_NC - 1, NT - 1)])
            SCD(_)

            tag4 = "ln4_%d" % j

            def _(v, j=j):
                sSC.wait(v, snap[("fevac", FF_NC - 1, j)])
                sDV.inc(v.reduce_sum(stc(SC_S, j),
                                     fsum.ap()[:, j * 4:(j + 1) * 4], AX))
                snap[(tag4, "s", j)] = ("dv", sDV.n)
            blk.vector(_)

            def _(s, j=j, frow=frow, dmp=dmp):
                s.activation(dmp, frow, AF.Square,
                             accum_out=stc(SC_Q, j)).then_inc(sSC.h, 1)
            SCD(_)
            snap[(tag4, "q", j)] = ("sc", sSC.prebump())

            vgate6 = snap[("out", j - 2)] if j >= 2 else snap[("h3b", NT - 1)]
            vsp = snap[("sp", j)]

            def _(s, j=j, vgate6=vgate6, vsp=vsp):
                sDV.wait(s, vgate6)
                stSP.wait(s, vsp)
                s.dma_start(
                    out=D_own.ap()[:, (j % 2) * 2048:(j % 2) * 2048 + 2048],
                    in_=x2sp[j * 128:(j + 1) * 128, :]).then_inc(stXR.h, 16)
            SCD(_)
            snap[("x2r", j)] = stXR.prebump(16)

        def emit_p6_B(j):
            frow = C3[:, j, :]
            outstg = dump6[:, 1 + (j % 2), :]
            x2slot = D_own.ap()[:, (j % 2) * 2048:(j % 2) * 2048 + 2048]

            ln_stats_B(j, "ln4_%d" % j)

            if DEBUG and j == NT - 1:
                def _(sy, j=j, frow=frow):
                    sSC.wait(sy, snap[("ln4_%d" % j, "r", j)])
                    dbg_dump(sy, "d_f7", frow)
                    dbg_dump(sy, "d_st47", stats.ap())
                    dbg_dump(sy, "d_fsum", fsum.ap())
                blk.sync(_)

            def _(v, j=j, frow=frow, outstg=outstg, x2slot=x2slot):
                sSC.wait(v, snap[("ln4_%d" % j, "r", j)])
                if DEBUG and j == NT - 1:
                    stDBG.wait(v)
                sDV.inc(v.tensor_scalar(frow, frow, stc(SC_M, j), stc(SC_R, j),
                                        ALU.subtract, ALU.mult))
                snap[("h4", j)] = sDV.n
                stXR.wait(v, snap[("x2r", j)])
                if j < 2:
                    sTE.wait(v, snap[("womm", WO_NC - 1, NT - 1)])
                else:
                    stOUT.wait(v, snap[("outd", j - 2)])
                sDV.wait(v)
                sDV.inc(v.tensor_tensor(outstg, frow, x2slot, ALU.add))
                snap[("out", j)] = sDV.n
            blk.vector(_)

            def _(sy, j=j, outstg=outstg, x2slot=x2slot):
                sDV.wait(sy, snap[("out", j)])
                if DEBUG and j == NT - 1:
                    dbg_dump(sy, "d_xr7", x2slot)
                stOUT.inc(sy.dma_start(out=y_d[j * 128:(j + 1) * 128, :],
                                       in_=outstg), 16)
                snap[("outd", j)] = stOUT.n
            blk.sync(_)


        for c in range(1, FF_NC):
            for j in range(NT):
                emit_ffn_group(c, j)
                if c == FF_NC - 1:
                    emit_p6_A(j)
                    if j >= 1:
                        emit_p6_B(j - 1)
            if c + 2 < FF_NC:
                def _(g, c=c):
                    sTE.wait(g, snap[("fmm", c, NT - 1)])
                    stW.inc(g.dma_start(out=wtch[:, c % 2, :, :],
                                        in_=wt_d[c + 2]), 16)
                    snap[("wt", c + 2)] = stW.n
                blk.gpsimd(_)

        emit_p6_B(NT - 1)

        def _(s):
            for fn in sc_defer:
                fn(s)
        blk.scalar(_)

        def _(sy):
            stOUT.wait(sy)
            if DEBUG:
                stDBG.wait(sy)
        blk.sync(_)

    return nc


def _host_layout(inputs):
    f = np.ascontiguousarray

    def wqkv(W):
        return f(W.T.reshape(NO, 128, NO, 128).transpose(2, 1, 0, 3))

    def wchunk(W, nch, co):
        return f(W.T.reshape(NO, 128, nch, co).transpose(2, 1, 0, 3))

    A = np.asarray(inputs["L0_ffn_lora_A"], np.float32)
    Bl = np.asarray(inputs["L0_ffn_lora_B"], np.float32)
    rw = np.asarray(inputs["L0_ffn_router_w"], np.float32)
    ar_full = np.concatenate(
        [A.transpose(2, 0, 1).reshape(D, 128), rw.T], axis=1)
    out = {
        "wq": wqkv(np.asarray(inputs["L0_attn_Wq"], np.float32)),
        "wk": wqkv(np.asarray(inputs["L0_attn_Wk"], np.float32)),
        "wv": wqkv(np.asarray(inputs["L0_attn_Wv"], np.float32)),
        "wo": wchunk(np.asarray(inputs["L0_attn_Wo"], np.float32), WO_NC, 256),
        "wt": wchunk(np.asarray(inputs["L0_ffn_weight"], np.float32), FF_NC, 512),
        "ar": f(ar_full.reshape(NO, 128, 136).transpose(1, 0, 2)),
        "bb": f(Bl.transpose(0, 2, 1).reshape(128, D)),
        "bq": f(np.asarray(inputs["L0_attn_bq"], np.float32).reshape(NO, 128).T),
        "bk": f(np.asarray(inputs["L0_attn_bk"], np.float32).reshape(NO, 128).T),
        "bv": f(np.asarray(inputs["L0_attn_bv"], np.float32).reshape(NO, 128).T),
        "bo": f(np.asarray(inputs["L0_attn_bo"], np.float32).reshape(1, D)),
        "idn": np.eye(128, dtype=np.float32),
        "epsc": np.full((128, 1), EPS, np.float32),
        "ones1": np.ones((1, 128), np.float32),
    }
    e2 = np.zeros((8, 128), np.float32)
    for e in range(8):
        e2[e, e * 16:(e + 1) * 16] = 2.0
    out["e2"] = e2
    return out


def kernel(**inputs):
    global _dbg_out
    x = np.asarray(inputs["L0_attn_x"], np.float32)
    Bsz, L, _ = x.shape
    xf = np.ascontiguousarray(x.reshape(Bsz * L, D))
    shared = _host_layout(inputs)

    nc = build_program()

    in_maps = []
    for c in range(8):
        m = dict(shared)
        m["x"] = np.ascontiguousarray(xf[c * T:(c + 1) * T])
        mask = np.zeros((8, 1), np.float32)
        s, p = c // 4, c % 4
        for r in range(8):
            if r // 4 == s and r % 4 < p:
                mask[r, 0] = 1.0
        m["maskc"] = mask
        in_maps.append(m)

    res = run_bass_kernel_spmd(nc, in_maps, core_ids=list(range(8)))
    if DEBUG:
        _dbg_out = res.results
    y = np.concatenate([res.results[c]["y"] for c in range(8)], axis=0)
    return y.reshape(Bsz, L, D)


# version 09
# speedup vs baseline: 1.0194x; 1.0194x over previous
import math
import numpy as np
from contextlib import ExitStack

import concourse.bass as bass
import concourse.mybir as mybir
from concourse.bass_utils import run_bass_kernel_spmd

F32 = mybir.dt.float32
BF16 = mybir.dt.bfloat16
ALU = mybir.AluOpType
AF = mybir.ActivationFunctionType
AX = mybir.AxisListType.X

T = 1024
D = 2048
NT = 8
NO = 16
NG = 4
WO_NC = 8
FF_NC = 4
EPS = 1e-5
SCALE = (D // 16) ** -0.5
LNS = math.log(SCALE)
BIG = 1.0e30

DEBUG = False
_dbg_out = {}


class S:

    def __init__(self, nc, ctx, name):
        self.h = ctx.enter_context(nc.semaphore(name))
        self.n = 0

    def inc(self, inst, k=1):
        inst.then_inc(self.h, k)
        self.n += k
        return self.n

    def wait(self, eng, val=None):
        v = self.n if val is None else val
        if v > 0:
            eng.wait_ge(self.h, v)

    def prebump(self, k=1):
        self.n += k
        return self.n


def build_program():
    nc = bass.Bass("TRN2", target_bir_lowering=False, debug=False)

    x_d = nc.dram_tensor("x", [T, D], F32, kind="ExternalInput").ap()
    wq_d = nc.dram_tensor("wq", [NO, 128, NO, 128], F32, kind="ExternalInput").ap()
    wk_d = nc.dram_tensor("wk", [NO, 128, NO, 128], F32, kind="ExternalInput").ap()
    wv_d = nc.dram_tensor("wv", [NO, 128, NO, 128], F32, kind="ExternalInput").ap()
    wo_d = nc.dram_tensor("wo", [WO_NC, 128, NO, 256], F32, kind="ExternalInput").ap()
    wt_d = nc.dram_tensor("wt", [FF_NC, 128, NO, 512], F32, kind="ExternalInput").ap()
    ar_d = nc.dram_tensor("ar", [128, NO, 136], F32, kind="ExternalInput").ap()
    bb_d = nc.dram_tensor("bb", [128, D], F32, kind="ExternalInput").ap()
    bq_d = nc.dram_tensor("bq", [128, NO], F32, kind="ExternalInput").ap()
    bk_d = nc.dram_tensor("bk", [128, NO], F32, kind="ExternalInput").ap()
    bv_d = nc.dram_tensor("bv", [128, NO], F32, kind="ExternalInput").ap()
    bo_d = nc.dram_tensor("bo", [1, D], F32, kind="ExternalInput").ap()
    idn_d = nc.dram_tensor("idn", [128, 128], F32, kind="ExternalInput").ap()
    e2_d = nc.dram_tensor("e2", [8, 128], F32, kind="ExternalInput").ap()
    ones_d = nc.dram_tensor("ones1", [1, 128], F32, kind="ExternalInput").ap()
    mask_d = nc.dram_tensor("maskc", [8, 1], F32, kind="ExternalInput").ap()
    eps_d = nc.dram_tensor("epsc", [128, 1], F32, kind="ExternalInput").ap()
    y_d = nc.dram_tensor("y", [T, D], F32, kind="ExternalOutput").ap()

    x2sp = nc.dram_tensor("x2sp", [T, D], F32, kind="Internal").ap()
    cin = [nc.dram_tensor(f"cin{g}", [1, 512], F32, kind="Internal").ap()
           for g in range(NG)]
    cout = [nc.dram_tensor(f"cout{g}", [8, 512], F32, kind="Internal",
                           addr_space="Shared").ap()
            for g in range(NG)]

    dbg = {}
    if DEBUG:
        for nm, shp in [
            ("d_h1t", [128, NO * T // 2]), ("d_phi", [128, T]),
            ("d_ukv", [128, T]), ("d_ctx", [128, NO * T // 2]),
            ("d_a", [128, D]), ("d_x2", [128, D]),
            ("d_lg", [128, 8]), ("d_w", [128, 8]),
            ("d_sx", [128, T // 2]), ("d_f", [128, D]),
            ("d_pref", [128, NO]), ("d_tot", [128, NO]),
            ("d_h3t", [128, NO * T // 2]), ("d_ax", [128, T]),
            ("d_x27", [128, D]), ("d_h3b7", [128, D // 2]),
            ("d_a7", [128, D]), ("d_st7", [128, 56]), ("d_asum", [128, 64]),
            ("d_f7", [128, D]), ("d_st47", [128, 56]), ("d_fsum", [128, 32]),
            ("d_xr7", [128, D]),
            ("d_wf", [8, T // 2]), ("d_lg2", [8, T]),
        ]:
            dbg[nm] = nc.dram_tensor(nm, shp, F32, kind="ExternalOutput").ap()

    ctx_mgr = ExitStack()
    with ctx_mgr as ctx:
        def sb(name, shape, dt=F32):
            return ctx.enter_context(nc.sbuf_tensor(name, shape, dt))

        def psum(name, shape, dt=F32):
            return ctx.enter_context(nc.psum_tensor(name, shape, dt))

        A_own = sb("A_own", [128, 8192])
        B_own = sb("B_own", [128, 8192])
        C_own = sb("C_own", [128, 16384])
        D_own = sb("D_own", [128, 6144])
        WOX = sb("WOX", [128, 6144])
        E_own = sb("E_own", [128, 2048])
        STG = sb("STG", [128, 2048])
        FB = sb("FB", [128, 2112])
        stats = sb("stats", [128, 56])
        asum = sb("asum", [128, 64])
        fsum = sb("fsum", [128, 32])
        xs1 = sb("xs1", [128, 8])
        bqc = sb("bqc", [128, NO]); bkc = sb("bkc", [128, NO]); bvc = sb("bvc", [128, NO])
        bqLn = sb("bqLn", [128, NO]); bqS2 = sb("bqS2", [128, NO])
        idn = sb("idn_s", [128, 128])
        e2f = sb("e2f", [8, 64])
        onesf = sb("onesf", [1, 64])
        bof = sb("bof", [1, 1024])
        maskc = sb("maskc_s", [8, 1])
        epsc = sb("epsc_s", [128, 1])
        tot0 = sb("tot0", [128, NO]); tot1 = sb("tot1", [128, NO])
        totS = sb("totS", [128, NO])
        pref = sb("pref", [128, NO])
        ltm = sb("ltm", [128, 8])
        r8a = sb("r8a", [128, 8]); r8b = sb("r8b", [128, 8]); r8c = sb("r8c", [128, 8])
        rc = sb("rc", [128, 8])
        wtm = sb("wtm", [128, 8])

        gsb = FB.ap()[0:8, 0:512]
        lgsb = E_own.ap()[0:8, 0:1024]
        wfm = E_own.ap()[0:8, 1024:1536]
        h1T = A_own.ap().bitcast(BF16).rearrange("p (n t) -> p n t", n=NO)
        ctxT = B_own.ap().bitcast(BF16).rearrange("p (n t) -> p n t", n=NO)
        wtch = B_own.ap().bitcast(BF16).rearrange("p (c n f) -> p c n f", c=2, n=NO)
        Cap = C_own.ap()
        C3 = Cap.rearrange("p (j f) -> p j f", j=NT)
        qkvw = D_own.ap().bitcast(BF16).rearrange("p (s f) -> p s f", s=6)
        woch = WOX.ap().bitcast(BF16).rearrange("p (c n f) -> p c n f", c=3, n=NO)
        dump6 = WOX.ap().rearrange("p (c f) -> p c f", c=3)
        h1b = FB.ap()[:, 0:1024].bitcast(BF16)
        bbv = FB.ap()[:, 0:1024].bitcast(BF16)
        arb = FB.ap()[:, 1024:2112].bitcast(BF16).rearrange(
            "p (n f) -> p n f", n=NO)
        e2b = e2f.ap().bitcast(BF16)
        onesb = onesf.ap().bitcast(BF16)
        bob = bof.ap().bitcast(BF16)
        axs = STG.ap()[:, 0:1024]
        sxb = STG.ap()[:, 1024:1536].bitcast(BF16)
        wfmb = wfm.bitcast(BF16)

        pb0 = psum("pb0", [128, 1024])
        pb1 = psum("pb1", [128, 1024])
        pvv = psum("pvv", [128, 1024])
        psm = psum("psm", [128, 512])

        stX = S(nc, ctx, "stX")
        stTP = S(nc, ctx, "stTP")
        stW = S(nc, ctx, "stW")
        stCIN = S(nc, ctx, "stCIN")
        stGSB = S(nc, ctx, "stGSB")
        stSP = S(nc, ctx, "stSP")
        stXR = S(nc, ctx, "stXR")
        stOUT = S(nc, ctx, "stOUT")
        stDBG = S(nc, ctx, "stDBG")
        sCC = S(nc, ctx, "sCC")
        sPL = S(nc, ctx, "sPL")
        sTE = S(nc, ctx, "sTE")
        sDV = S(nc, ctx, "sDV")
        sSC = S(nc, ctx, "sSC")

        blk = ctx.enter_context(nc.Block())
        snap = {}
        sc_defer = []

        def SCD(fn):
            sc_defer.append(fn)

        def dbg_dump(sy, name, src_ap):
            if DEBUG and name in dbg:
                stDBG.inc(sy.dma_start(out=dbg[name], in_=src_ap), 16)

        def _(g):
            for dst, src in [
                (bqc.ap(), bq_d), (bkc.ap(), bk_d), (bvc.ap(), bv_d),
                (idn.ap(), idn_d), (maskc.ap(), mask_d), (epsc.ap(), eps_d),
            ]:
                stW.inc(g.dma_start(out=dst, in_=src), 16)
            stW.inc(g.dma_start(out=e2b, in_=e2_d), 16)
            stW.inc(g.dma_start(out=onesb, in_=ones_d), 16)
            stW.inc(g.dma_start(out=bob, in_=bo_d), 16)
            stW.inc(g.dma_start(out=arb, in_=ar_d), 16)
            snap["consts"] = stW.n
        blk.gpsimd(_)

        def _(v):
            stW.wait(v, snap["consts"])
            sDV.inc(v.tensor_single_scalar(bqLn.ap(), bqc.ap(), LNS, ALU.add))
            sDV.inc(v.tensor_single_scalar(bqS2.ap(), bqc.ap(), SCALE, ALU.mult))
            snap["dbias"] = sDV.n
        blk.vector(_)

        SC_S, SC_Q, SC_M, SC_M2, SC_VAR, SC_STD, SC_R = 0, 8, 16, 24, 32, 40, 48

        def stc(base, j):
            return stats.ap()[:, base + j:base + j + 1]

        def ln_stats_A(j, src_ap, dump_ap, tag, sum_on_scalar=False):
            if sum_on_scalar:
                def _(s):
                    s.activation(dump_ap, src_ap, AF.Copy,
                                 accum_out=stc(SC_S, j)).then_inc(sSC.h, 1)
                SCD(_)
                snap[(tag, "s", j)] = ("sc", sSC.prebump())
            else:
                def _(v):
                    sDV.inc(v.reduce_sum(stc(SC_S, j), src_ap, AX))
                    snap[(tag, "s", j)] = ("dv", sDV.n)
                blk.vector(_)

            def _(s):
                s.activation(dump_ap, src_ap, AF.Square,
                             accum_out=stc(SC_Q, j)).then_inc(sSC.h, 1)
            SCD(_)
            snap[(tag, "q", j)] = ("sc", sSC.prebump())

        def ln_stats_B(j, tag):
            kind, sval = snap[(tag, "s", j)]
            qk, qval = snap[(tag, "q", j)]

            def _(v):
                if qk == "sc":
                    sSC.wait(v, qval)
                else:
                    sDV.wait(v, qval)
                if kind == "dv":
                    sDV.wait(v, sval)
                sDV.inc(v.tensor_single_scalar(stc(SC_M, j), stc(SC_S, j),
                                               1.0 / D, ALU.mult))
                sDV.wait(v)
                sDV.inc(v.tensor_tensor(stc(SC_M2, j), stc(SC_M, j),
                                        stc(SC_M, j), ALU.mult))
                sDV.wait(v)
                sDV.inc(v.scalar_tensor_tensor(stc(SC_VAR, j), stc(SC_Q, j),
                                               1.0 / D, stc(SC_M2, j),
                                               ALU.mult, ALU.subtract))
                snap[(tag, "var", j)] = sDV.n
            blk.vector(_)

            vvar = snap[(tag, "var", j)]
            vln = sSC.prebump()
            snap[(tag, "r", j)] = sSC.prebump()

            def _(s, vvar=vvar, vln=vln):
                sDV.wait(s, vvar)
                s.activation(stc(SC_STD, j), stc(SC_VAR, j),
                             AF.Ln, bias=epsc.ap()).then_inc(sSC.h, 1)
                s.wait_ge(sSC.h, vln)
                s.activation(stc(SC_R, j), stc(SC_STD, j),
                             AF.Exp, scale=-0.5).then_inc(sSC.h, 1)
            SCD(_)

        def p1_xslot(j):
            return E_own.ap() if j % 2 == 0 else STG.ap()

        def emit_ln1_B(j):
            xslot = p1_xslot(j)
            ln_stats_B(j, "ln1")

            def _(v, j=j):
                sDV.inc(v.tensor_copy(xs1.ap()[:, j:j + 1], stc(SC_S, j)))
            blk.vector(_)

            def _(v, j=j, xslot=xslot):
                sSC.wait(v, snap[("ln1", "r", j)])
                if j >= 1:
                    stTP.wait(v, snap[("tp1", j - 1)])
                sDV.inc(v.tensor_scalar(h1b, xslot, stc(SC_M, j), stc(SC_R, j),
                                        ALU.subtract, ALU.mult))
                snap[("ln1h1b", j)] = sDV.n
            blk.vector(_)

            def _(sy, j=j):
                sDV.wait(sy, snap[("ln1h1b", j)])
                src = h1b.rearrange("p (n t) -> p n t", n=NO)
                dst = h1T[:, :, j * 128:(j + 1) * 128]
                stTP.inc(sy.dma_start(out=dst, in_=src, transpose=True), 16)
                snap[("tp1", j)] = stTP.n
            blk.sync(_)

        for j in range(NT):
            xslot = p1_xslot(j)

            vgate = snap[("ln1h1b", j - 2)] if j >= 2 else 0

            def _(s, j=j, xslot=xslot, vgate=vgate):
                if vgate:
                    sDV.wait(s, vgate)
                s.dma_start(out=xslot,
                            in_=x_d[j * 128:(j + 1) * 128, :]).then_inc(stX.h, 16)
            SCD(_)
            snap[("xload", j)] = stX.prebump(16)

            def _(v, j=j):
                stX.wait(v, snap[("xload", j)])
            blk.vector(_)

            vx = snap[("xload", j)]

            def _(s, vx=vx):
                stX.wait(s, vx)
            SCD(_)

            ln_stats_A(j, xslot, Cap[:, (j % 2) * 2048:(j % 2) * 2048 + 2048], "ln1")
            if j >= 1:
                emit_ln1_B(j - 1)
        emit_ln1_B(NT - 1)

        slot_last_te = [0] * 6

        def load_qkv(g, i):
            for wi, wsrc in enumerate((wq_d, wk_d, wv_d)):
                s6 = (3 * i + wi) % 6
                sTE.wait(g, slot_last_te[s6])
                stW.inc(g.dma_start(out=qkvw[:, s6, :],
                                    in_=wsrc[i].rearrange("p n f -> p (n f)")), 16)
            snap[("wq", i)] = stW.n

        def emit_cc(g, grp):
            stCIN.wait(g, snap[("cin", grp)])
            sCC.inc(g.collective_compute(
                "AllGather", ALU.bypass, replica_groups=[list(range(8))],
                ins=[cin[grp].opt()], outs=[cout[grp].opt()]))
            snap[("cc", grp)] = sCC.n

        def emit_prefix(grp2):
            def _(sy):
                sCC.wait(sy, snap[("cc", grp2)])
                if grp2 > 0:
                    sTE.wait(sy, snap[("prefmm", grp2 - 1)])
                stGSB.inc(sy.dma_start(out=gsb, in_=cout[grp2]), 16)
                snap[("gsb", grp2)] = stGSB.n
            blk.sync(_)

            def _(t):
                stGSB.wait(t, snap[("gsb", grp2)])
                if grp2 > 0:
                    sDV.wait(t, snap[("prefevac", grp2 - 1)])
                gv = gsb.rearrange("r (c g) -> r c g", g=4)
                inst = None
                for q in range(4):
                    inst = t.matmul(psm.ap()[:, q:q + 1], gv[:, :, q],
                                    maskc.ap(), start=True, stop=True)
                sTE.inc(inst)
                snap[("prefmm", grp2)] = sTE.n
            blk.tensor(_)

            def _(v):
                sTE.wait(v, snap[("prefmm", grp2)])
                sDV.inc(v.tensor_copy(pref.ap()[:, grp2 * 4:grp2 * 4 + 4],
                                      psm.ap()[:, 0:4]))
                snap[("prefevac", grp2)] = sDV.n
            blk.vector(_)

        def emit_scan_ctx(i2):
            grp2 = i2 // 4

            def _(v):
                sDV.wait(v, snap[("prefevac", grp2)])
                slot = i2 % 8
                ub = Cap[:, 8192 + slot * 1024: 8192 + (slot + 1) * 1024]
                sDV.inc(v.tensor_tensor_scan(ub, ub, ub,
                                             pref.ap()[:, i2:i2 + 1],
                                             ALU.add, ALU.bypass))
                sDV.wait(v)
                sDV.inc(v.tensor_tensor(ctxT[:, i2, :], ub,
                                        Cap[:, slot * 1024:(slot + 1) * 1024],
                                        ALU.mult))
                snap[("ctx", i2)] = sDV.n
            blk.vector(_)

        def _(g):
            stX.wait(g, snap[("xload", 2)])
            load_qkv(g, 0)
            stX.wait(g, snap[("xload", 4)])
            load_qkv(g, 1)
        blk.gpsimd(_)

        for i in range(NO):
            for h in range(2):
                tsl = slice(h * 512, (h + 1) * 512)

                def _(t, i=i, h=h, tsl=tsl):
                    stW.wait(t, snap[("wq", i)])
                    if i == 0 and h == 0:
                        sDV.wait(t, snap["dbias"])
                    if (i, h) > (0, 0):
                        pi, ph = (i, 0) if h == 1 else (i - 1, 1)
                        sSC.wait(t, snap[("kacts", pi, ph)])
                        sDV.wait(t, snap[("u", pi, ph)])
                    for wi, pst in ((0, pb0), (1, pb1), (2, pvv)):
                        s6 = (3 * i + wi) % 6
                        inst = None
                        if i == 0:
                            for d in range(NO):
                                for j4 in range(4):
                                    jj = h * 4 + j4
                                    if wi == 0 and d == 0:
                                        stTP.wait(t, snap[("tp1", jj)])
                                    inst = t.matmul(
                                        pst.ap()[:, h * 512 + j4 * 128:
                                                 h * 512 + (j4 + 1) * 128],
                                        qkvw[:, s6, d * 128:(d + 1) * 128],
                                        h1T[:, d, jj * 128:(jj + 1) * 128],
                                        start=(d == 0 and j4 == 0),
                                        stop=(d == NO - 1),
                                        skip_group_check=True)
                        else:
                            for d in range(NO):
                                inst = t.matmul(
                                    pst.ap()[:, tsl],
                                    qkvw[:, s6, d * 128:(d + 1) * 128],
                                    h1T[:, d, tsl],
                                    start=(d == 0), stop=(d == NO - 1))
                        sTE.inc(inst)
                        snap[("mm", wi, i, h)] = sTE.n
                    for wi in range(3):
                        slot_last_te[(3 * i + wi) % 6] = sTE.n
                blk.tensor(_)

                vu_prev = 0
                if (i, h) > (0, 0):
                    pi, ph = (i, 0) if h == 1 else (i - 1, 1)
                    vu_prev = snap[("u", pi, ph)]
                vmm0 = None

                def _(s, i=i, h=h, tsl=tsl, vu_prev=vu_prev):
                    if vu_prev:
                        sDV.wait(s, vu_prev)
                    if i == 0 and h == 0:
                        sDV.wait(s, snap[("ln1h1b", NT - 1)])
                    sTE.wait(s, snap[("mm", 0, i, h)])
                    s.activation(STG.ap()[:, 0:512], pb0.ap()[:, tsl],
                                 AF.Exp, bias=bqLn.ap()[:, i:i + 1]).then_inc(sSC.h, 1)
                    s.activation(STG.ap()[:, 512:1024], pb0.ap()[:, tsl],
                                 AF.Relu, bias=bqS2.ap()[:, i:i + 1],
                                 scale=SCALE).then_inc(sSC.h, 1)
                    sTE.wait(s, snap[("mm", 1, i, h)])
                    s.activation(STG.ap()[:, 1024:1536], pb1.ap()[:, tsl],
                                 AF.Exp, bias=bkc.ap()[:, i:i + 1]).then_inc(sSC.h, 1)
                    s.activation(STG.ap()[:, 1536:2048], pb1.ap()[:, tsl],
                                 AF.Relu, bias=bkc.ap()[:, i:i + 1]).then_inc(sSC.h, 1)
                SCD(_)
                sSC.prebump(2)
                snap[("qacts", i, h)] = sSC.n
                sSC.prebump(2)
                snap[("kacts", i, h)] = sSC.n

                def _(v, i=i, h=h, tsl=tsl):
                    slot = i % 8
                    sSC.wait(v, snap[("qacts", i, h)])
                    sDV.inc(v.scalar_tensor_tensor(
                        Cap[:, slot * 1024 + h * 512: slot * 1024 + (h + 1) * 512],
                        STG.ap()[:, 0:512], SCALE, STG.ap()[:, 512:1024],
                        ALU.min, ALU.add))
                    sSC.wait(v, snap[("kacts", i, h)])
                    sDV.inc(v.scalar_tensor_tensor(
                        STG.ap()[:, 1024:1536],
                        STG.ap()[:, 1024:1536], 1.0, STG.ap()[:, 1536:2048],
                        ALU.min, ALU.add))
                    sDV.wait(v)
                    sTE.wait(v, snap[("mm", 2, i, h)])
                    tcol = tot0 if h == 0 else tot1
                    sDV.inc(v.scalar_tensor_tensor(
                        Cap[:, 8192 + slot * 1024 + h * 512:
                            8192 + slot * 1024 + (h + 1) * 512],
                        pvv.ap()[:, tsl], bvc.ap()[:, i:i + 1],
                        STG.ap()[:, 1024:1536],
                        ALU.add, ALU.mult,
                        accum_out=tcol.ap()[:, i:i + 1]))
                    snap[("u", i, h)] = sDV.n
                blk.vector(_)

            grp = i // 4
            if i % 4 == 3:
                g4 = slice(grp * 4, grp * 4 + 4)

                def _(v, grp=grp, g4=g4):
                    sDV.wait(v)
                    sDV.inc(v.tensor_tensor(totS.ap()[:, g4], tot0.ap()[:, g4],
                                            tot1.ap()[:, g4], ALU.add))
                    snap[("tot", grp)] = sDV.n
                blk.vector(_)

                def _(sy, grp=grp, g4=g4):
                    sDV.wait(sy, snap[("tot", grp)])
                    dst = cin[grp].rearrange("a (c g) -> a c g", g=4)[0]
                    stCIN.inc(sy.dma_start(out=dst, in_=totS.ap()[:, g4]), 16)
                    snap[("cin", grp)] = stCIN.n
                blk.sync(_)

            def _(g, i=i):
                if i + 2 < NO:
                    load_qkv(g, i + 2)
                if i in (11, 12, 13):
                    w0 = i - 11
                    stW.inc(g.dma_start(out=woch[:, w0, :, :], in_=wo_d[w0]), 16)
                    snap[("wo", w0)] = stW.n
                if i % 4 == 3:
                    emit_cc(g, i // 4)
            blk.gpsimd(_)

            _tails = {6: [("p", 0), ("s", 0)], 7: [("s", 1), ("s", 2)],
                      8: [("s", 3)], 9: [("p", 1), ("s", 4)],
                      10: [("s", 5), ("s", 6)], 11: [("s", 7)],
                      13: [("p", 2), ("s", 8)], 14: [("s", 9), ("s", 10)],
                      15: [("s", 11)]}
            for kind, arg in _tails.get(i, []):
                if kind == "p":
                    emit_prefix(arg)
                else:
                    emit_scan_ctx(arg)

        emit_prefix(3)
        for i2 in range(12, 16):
            emit_scan_ctx(i2)

        if DEBUG:
            def _(sy):
                sDV.wait(sy, snap[("ctx", 15)])
                dbg_dump(sy, "d_tot", totS.ap())
                dbg_dump(sy, "d_pref", pref.ap())
                dbg_dump(sy, "d_phi", Cap[:, 7 * 1024:8 * 1024])
                dbg_dump(sy, "d_ukv", Cap[:, 8192 + 7 * 1024: 8192 + 8 * 1024])
                dbg_dump(sy, "d_h1t", A_own.ap())
                dbg_dump(sy, "d_ctx", B_own.ap())
            blk.sync(_)

        def emit_ln23_A(j):
            xslot = E_own.ap() if j % 2 == 0 else STG.ap()
            arow = C3[:, j, :]
            dmp2 = D_own.ap()[:, (j % 2) * 2048:(j % 2) * 2048 + 2048]

            if j >= 2:
                vgate2 = ("dv", snap[("h3b", j - 2)])
            elif j == 1:
                vgate2 = ("dv", snap[("u", NO - 1, 1)])
            else:
                vgate2 = None

            def _(s, j=j, xslot=xslot, vgate2=vgate2):
                if vgate2 is not None:
                    sDV.wait(s, vgate2[1])
                s.dma_start(out=xslot,
                            in_=x_d[j * 128:(j + 1) * 128, :]).then_inc(stX.h, 16)
            SCD(_)
            snap[("xload2", j)] = stX.prebump(16)

            def _(v, j=j):
                sSC.wait(v, snap[("aevac", WO_NC - 1, j)])
                sDV.inc(v.reduce_sum(stc(SC_S, j),
                                     asum.ap()[:, j * 8:(j + 1) * 8], AX))
                snap[("ln2_%d" % j, "s", j)] = ("dv", sDV.n)
            blk.vector(_)

            def _(s, j=j):
                sSC.wait(s, snap[("aevac", WO_NC - 1, j)])
            SCD(_)

            tag2 = "ln2_%d" % j

            def _(s, j=j, tag2=tag2):
                s.activation(dmp2, arow, AF.Square,
                             accum_out=stc(SC_Q, j)).then_inc(sSC.h, 1)
            SCD(_)
            snap[(tag2, "q", j)] = ("sc", sSC.prebump())

        def emit_ln23_B(j):
            xslot = E_own.ap() if j % 2 == 0 else STG.ap()
            arow = C3[:, j, :]
            dmp2 = D_own.ap()[:, (j % 2) * 2048:(j % 2) * 2048 + 2048]

            ln_stats_B(j, "ln2_%d" % j)

            if DEBUG and j == NT - 1:
                def _(sy, j=j, arow=arow):
                    sSC.wait(sy, snap[("ln2_%d" % j, "r", j)])
                    dbg_dump(sy, "d_a7", arow)
                    dbg_dump(sy, "d_st7", stats.ap())
                    dbg_dump(sy, "d_asum", asum.ap())
                blk.sync(_)

            def _(v, j=j, xslot=xslot, arow=arow):
                sSC.wait(v, snap[("ln2_%d" % j, "r", j)])
                if DEBUG and j == NT - 1:
                    stDBG.wait(v)
                sDV.inc(v.tensor_scalar(arow, arow, stc(SC_M, j), stc(SC_R, j),
                                        ALU.subtract, ALU.mult))
                stX.wait(v, snap[("xload2", j)])
                sDV.wait(v)
                sDV.inc(v.tensor_tensor(xslot, arow, xslot, ALU.add))
                snap[("x2", j)] = sDV.n
            blk.vector(_)

            vx2 = snap[("x2", j)]

            def _(s, j=j, xslot=xslot, vx2=vx2):
                sDV.wait(s, vx2)
                s.dma_start(out=x2sp[j * 128:(j + 1) * 128, :],
                            in_=xslot).then_inc(stSP.h, 16)
            SCD(_)
            snap[("sp", j)] = stSP.prebump(16)

            if DEBUG and (j == 0 or j == NT - 1):
                def _(sy, j=j, xslot=xslot):
                    sDV.wait(sy, snap[("x2", j)])
                    if j == 0:
                        dbg_dump(sy, "d_x2", xslot)
                    else:
                        dbg_dump(sy, "d_x27", xslot)
                blk.sync(_)

            tag3 = "ln3_%d" % j

            def _(v, j=j, xslot=xslot, vx2=vx2):
                sDV.inc(v.tensor_copy(stc(SC_S, j), xs1.ap()[:, j:j + 1]))
                snap[(tag3, "s", j)] = ("dv", sDV.n)
            blk.vector(_)

            def _(s, j=j, xslot=xslot, vx2=vx2):
                sDV.wait(s, vx2)
                s.activation(dmp2, xslot, AF.Square,
                             accum_out=stc(SC_Q, j)).then_inc(sSC.h, 1)
            SCD(_)
            snap[(tag3, "q", j)] = ("sc", sSC.prebump())

            ln_stats_B(j, "ln3_%d" % j)

            def _(v, j=j, xslot=xslot):
                sSC.wait(v, snap[("ln3_%d" % j, "r", j)])
                if j >= 1:
                    stTP.wait(v, snap[("tp3", j - 1)])
                else:
                    stTP.wait(v, snap[("tp1", NT - 1)])
                sDV.inc(v.tensor_scalar(h1b, xslot, stc(SC_M, j), stc(SC_R, j),
                                        ALU.subtract, ALU.mult))
                snap[("h3b", j)] = sDV.n
            blk.vector(_)

            def _(sy, j=j):
                sDV.wait(sy, snap[("h3b", j)])
                src = h1b.rearrange("p (n t) -> p n t", n=NO)
                dst = h1T[:, :, j * 128:(j + 1) * 128]
                stTP.inc(sy.dma_start(out=dst, in_=src, transpose=True), 16)
                snap[("tp3", j)] = stTP.n
            blk.sync(_)


        bank_prev = [None, None]

        def emit_wo_group(c, j):
            bank = (c + j) % 2
            aps = pb0.ap()[:, 0:256] if bank == 0 else pb0.ap()[:, 512:768]
            osl = slice(c * 256, (c + 1) * 256)

            def _(t, c=c, j=j, aps=aps, osl=osl, bprev=bank_prev[bank]):
                stW.wait(t, snap[("wo", c)])
                if bprev is not None:
                    sSC.wait(t, bprev)
                inst = t.matmul(aps, onesb, bob[0:1, osl],
                                start=True, stop=False)
                for ct in range(NO):
                    if c == 0 and j == 0:
                        sDV.wait(t, snap[("ctx", ct)])
                    inst = t.matmul(
                        aps, ctxT[:, ct, j * 128:(j + 1) * 128],
                        woch[:, c % 3, ct, :],
                        start=False, stop=(ct == NO - 1))
                sTE.inc(inst)
                snap[("womm", c, j)] = sTE.n
            blk.tensor(_)

            def _(s, c=c, j=j, aps=aps, osl=osl):
                sTE.wait(s, snap[("womm", c, j)])
                s.activation(C3[:, j, osl], aps, AF.Copy,
                             accum_out=asum.ap()[:, j * 8 + c:j * 8 + c + 1]
                             ).then_inc(sSC.h, 1)
            SCD(_)
            snap[("aevac", c, j)] = sSC.prebump()
            bank_prev[bank] = snap[("aevac", c, j)]

        for c in range(WO_NC - 2):
            for j in range(NT):
                emit_wo_group(c, j)
            if c + 3 < WO_NC:
                def _(g, c=c):
                    sTE.wait(g, snap[("womm", c, NT - 1)])
                    stW.inc(g.dma_start(out=woch[:, (c + 3) % 3, :, :],
                                        in_=wo_d[c + 3]), 16)
                    snap[("wo", c + 3)] = stW.n
                blk.gpsimd(_)

        for j in range(NT):
            emit_wo_group(WO_NC - 2, j)
            emit_wo_group(WO_NC - 1, j)
            emit_ln23_A(j)
            if j >= 1:
                emit_ln23_B(j - 1)

        emit_ln23_B(NT - 1)

        def _(g):
            stTP.wait(g, snap[("tp3", NT - 1)])
            stW.inc(g.dma_start(out=bbv, in_=bb_d), 16)
            snap["bb"] = stW.n
            sTE.wait(g, snap[("womm", WO_NC - 1, NT - 1)])
            stW.inc(g.dma_start(out=wtch[:, 0, :, :], in_=wt_d[0]), 16)
            snap[("wt", 0)] = stW.n
            stW.inc(g.dma_start(out=wtch[:, 1, :, :], in_=wt_d[1]), 16)
            snap[("wt", 1)] = stW.n
        blk.gpsimd(_)

        if DEBUG:
            def _(sy):
                stTP.wait(sy, snap[("tp3", NT - 1)])
                dbg_dump(sy, "d_h3t", A_own.ap())
            blk.sync(_)

        for hh in range(2):
            pslice = pb1.ap()[:, hh * 512:(hh + 1) * 512]

            def _(t, hh=hh, pslice=pslice):
                stTP.wait(t, snap[("tp3", 3 if hh == 0 else NT - 1)])
                if hh == 0:
                    sSC.wait(t, snap[("kacts", NO - 1, 1)])
                inst = None
                for d in range(NO):
                    inst = t.matmul(pslice, arb[:, d, 0:128],
                                    h1T[:, d, hh * 512:(hh + 1) * 512],
                                    start=(d == 0), stop=(d == NO - 1))
                sTE.inc(inst)
                snap[("axmm", hh)] = sTE.n
            blk.tensor(_)

            def _(v, hh=hh, pslice=pslice):
                sTE.wait(v, snap[("axmm", hh)])
                if hh == 0:
                    stSP.wait(v, snap[("sp", NT - 1)])
                sDV.inc(v.tensor_copy(axs[:, hh * 512:(hh + 1) * 512], pslice))
                snap[("axev", hh)] = sDV.n
            blk.vector(_)

            def _(t, hh=hh, pslice=pslice):
                sDV.wait(t, snap[("axev", hh)])
                inst = None
                for d in range(NO):
                    inst = t.matmul(pslice[0:8, :], arb[:, d, 128:136],
                                    h1T[:, d, hh * 512:(hh + 1) * 512],
                                    start=(d == 0), stop=(d == NO - 1))
                sTE.inc(inst)
                snap[("lgmm", hh)] = sTE.n
            blk.tensor(_)

            def _(v, hh=hh, pslice=pslice):
                sTE.wait(v, snap[("lgmm", hh)])
                sDV.inc(v.tensor_copy(lgsb[0:8, hh * 512:(hh + 1) * 512],
                                      pslice[0:8, :]))
                snap[("lgev", hh)] = sDV.n
            blk.vector(_)

        ffn_bank_prev = [("dv", snap[("u", NO - 1, 1)]),
                         ("dv", snap[("u", NO - 1, 1)])]

        def emit_ffn_group(c, j):
            bank = j % 2
            fps = pvv.ap()[:, 0:512] if bank == 0 else pvv.ap()[:, 512:1024]
            osl = slice(c * 512, (c + 1) * 512)

            def _(t, c=c, j=j, fps=fps, osl=osl, bprev=ffn_bank_prev[bank]):
                stW.wait(t, snap[("wt", c)])
                if c == 0:
                    stW.wait(t, snap["bb"])
                    sDV.wait(t, snap[("sx", j)])
                kind, val = bprev
                if kind == "dv":
                    sDV.wait(t, val)
                else:
                    sSC.wait(t, val)
                for d in range(NO):
                    t.matmul(fps, h1T[:, d, j * 128:(j + 1) * 128],
                             wtch[:, c % 2, d, :],
                             start=(d == 0), stop=False)
                inst = t.matmul(fps, sxb[:, j * 128:(j + 1) * 128],
                                bbv[:, osl], start=False, stop=True)
                sTE.inc(inst)
                snap[("fmm", c, j)] = sTE.n
            blk.tensor(_)

            def _(s, c=c, j=j, fps=fps, osl=osl):
                sTE.wait(s, snap[("fmm", c, j)])
                s.activation(C3[:, j, osl], fps, AF.Copy,
                             accum_out=fsum.ap()[:, j * 4 + c:j * 4 + c + 1]
                             ).then_inc(sSC.h, 1)
            SCD(_)
            snap[("fevac", c, j)] = sSC.prebump()
            ffn_bank_prev[bank] = ("sc", snap[("fevac", c, j)])

        for j in range(NT):
            hh = j // 4

            def _(t, j=j, hh=hh):
                sDV.wait(t, snap[("lgev", hh)])
                if j > 0:
                    sDV.wait(t, snap[("sx", j - 1)])
                sTE.inc(t.transpose(psm.ap()[:, 0:8],
                                    lgsb[0:8, j * 128:(j + 1) * 128],
                                    idn.ap()[0:8, 0:8]))
                snap[("lgT", j)] = sTE.n
            blk.tensor(_)

            def _(v, j=j):
                sTE.wait(v, snap[("lgT", j)])
                sDV.inc(v.tensor_copy(ltm.ap(), psm.ap()[:, 0:8]))
                sDV.wait(v)
                sDV.inc(v.reduce_max(rc.ap()[:, 0:1], ltm.ap(), AX))
                sDV.wait(v)
                sDV.inc(v.tensor_single_scalar(r8a.ap(), ltm.ap(),
                                               rc.ap()[:, 0:1], ALU.is_ge))
                sDV.wait(v)
                sDV.inc(v.scalar_tensor_tensor(r8b.ap(), r8a.ap(), -BIG,
                                               ltm.ap(), ALU.mult, ALU.add))
                sDV.wait(v)
                sDV.inc(v.reduce_max(rc.ap()[:, 1:2], r8b.ap(), AX))
                sDV.wait(v)
                sDV.inc(v.tensor_single_scalar(rc.ap()[:, 2:3], rc.ap()[:, 0:1],
                                               -1.0, ALU.mult))
                sDV.wait(v)
                sDV.inc(v.tensor_tensor(rc.ap()[:, 3:4], rc.ap()[:, 1:2],
                                        rc.ap()[:, 0:1], ALU.subtract))
                snap[("rmath", j)] = sDV.n
            blk.vector(_)

            def _(s, j=j):
                sDV.wait(s, snap[("rmath", j)])
                s.activation(r8c.ap(), ltm.ap(), AF.Exp,
                             bias=rc.ap()[:, 2:3]).then_inc(sSC.h, 1)
                s.activation(rc.ap()[:, 4:5], rc.ap()[:, 3:4],
                             AF.Exp).then_inc(sSC.h, 1)
            SCD(_)
            snap[("rexp", j)] = sSC.prebump(2)

            def _(v, j=j):
                sSC.wait(v, snap[("rexp", j)])
                sDV.inc(v.tensor_single_scalar(r8a.ap(), ltm.ap(),
                                               rc.ap()[:, 1:2], ALU.is_ge))
                sDV.inc(v.tensor_single_scalar(rc.ap()[:, 5:6], rc.ap()[:, 4:5],
                                               1.0, ALU.add))
                sDV.wait(v)
                sDV.inc(v.reciprocal(rc.ap()[:, 6:7], rc.ap()[:, 5:6]))
                sDV.wait(v)
                sDV.inc(v.tensor_tensor(r8b.ap(), r8c.ap(), r8a.ap(), ALU.mult))
                sDV.wait(v)
                sDV.inc(v.tensor_single_scalar(wtm.ap(), r8b.ap(),
                                               rc.ap()[:, 6:7], ALU.mult))
                snap[("w", j)] = sDV.n
            blk.vector(_)

            if j >= 1:
                emit_ffn_group(0, j - 1)

            def _(t, j=j):
                sDV.wait(t, snap[("w", j)])
                sTE.inc(t.transpose(psm.ap()[0:8, 128:256], wtm.ap(), idn.ap()))
                snap[("wT", j)] = sTE.n
            blk.tensor(_)

            def _(v, j=j):
                sTE.wait(v, snap[("wT", j)])
                sDV.inc(v.tensor_copy(wfmb[0:8, j * 128:(j + 1) * 128],
                                      psm.ap()[0:8, 128:256]))
                snap[("wev", j)] = sDV.n
            blk.vector(_)

            def _(t, j=j):
                sDV.wait(t, snap[("wev", j)])
                sTE.inc(t.matmul(psm.ap()[:, 256:384], e2b,
                                 wfmb[0:8, j * 128:(j + 1) * 128],
                                 start=True, stop=True))
                snap[("wexpmm", j)] = sTE.n
            blk.tensor(_)

            def _(v, j=j, hh=hh):
                sTE.wait(v, snap[("wexpmm", j)])
                sDV.inc(v.tensor_tensor(sxb[:, j * 128:(j + 1) * 128],
                                        axs[:, j * 128:(j + 1) * 128],
                                        psm.ap()[:, 256:384], ALU.mult))
                snap[("sx", j)] = sDV.n
            blk.vector(_)

        emit_ffn_group(0, NT - 1)

        def _(g):
            sTE.wait(g, snap[("fmm", 0, NT - 1)])
            stW.inc(g.dma_start(out=wtch[:, 0, :, :], in_=wt_d[2]), 16)
            snap[("wt", 2)] = stW.n
        blk.gpsimd(_)

        if DEBUG:
            def _(sy):
                sDV.wait(sy, snap[("lgev", 1)])
                dbg_dump(sy, "d_lg2", lgsb[0:8, :])
                sDV.wait(sy, snap[("sx", NT - 1)])
                dbg_dump(sy, "d_sx", sxb.bitcast(F32))
                dbg_dump(sy, "d_ax", axs)
                dbg_dump(sy, "d_wf", wfmb.bitcast(F32))
            blk.sync(_)

        def emit_p6_A(j):
            frow = C3[:, j, :]
            dmp = dump6[:, 0, :]

            def _(v, j=j):
                sSC.wait(v, snap[("fevac", FF_NC - 1, j)])
            blk.vector(_)

            def _(s, j=j):
                sSC.wait(s, snap[("fevac", FF_NC - 1, j)])
                if j == 0:
                    sTE.wait(s, snap[("womm", WO_NC - 1, NT - 1)])
            SCD(_)

            tag4 = "ln4_%d" % j

            def _(v, j=j):
                sSC.wait(v, snap[("fevac", FF_NC - 1, j)])
                sDV.inc(v.reduce_sum(stc(SC_S, j),
                                     fsum.ap()[:, j * 4:(j + 1) * 4], AX))
                snap[(tag4, "s", j)] = ("dv", sDV.n)
            blk.vector(_)

            def _(s, j=j, frow=frow, dmp=dmp):
                s.activation(dmp, frow, AF.Square,
                             accum_out=stc(SC_Q, j)).then_inc(sSC.h, 1)
            SCD(_)
            snap[(tag4, "q", j)] = ("sc", sSC.prebump())

            vgate6 = snap[("out", j - 2)] if j >= 2 else snap[("h3b", NT - 1)]
            vsp = snap[("sp", j)]

            def _(s, j=j, vgate6=vgate6, vsp=vsp):
                sDV.wait(s, vgate6)
                stSP.wait(s, vsp)
                s.dma_start(
                    out=D_own.ap()[:, (j % 2) * 2048:(j % 2) * 2048 + 2048],
                    in_=x2sp[j * 128:(j + 1) * 128, :]).then_inc(stXR.h, 16)
            SCD(_)
            snap[("x2r", j)] = stXR.prebump(16)

        def emit_p6_B(j):
            frow = C3[:, j, :]
            outstg = dump6[:, 1 + (j % 2), :]
            x2slot = D_own.ap()[:, (j % 2) * 2048:(j % 2) * 2048 + 2048]

            ln_stats_B(j, "ln4_%d" % j)

            if DEBUG and j == NT - 1:
                def _(sy, j=j, frow=frow):
                    sSC.wait(sy, snap[("ln4_%d" % j, "r", j)])
                    dbg_dump(sy, "d_f7", frow)
                    dbg_dump(sy, "d_st47", stats.ap())
                    dbg_dump(sy, "d_fsum", fsum.ap())
                blk.sync(_)

            def _(v, j=j, frow=frow, outstg=outstg, x2slot=x2slot):
                sSC.wait(v, snap[("ln4_%d" % j, "r", j)])
                if DEBUG and j == NT - 1:
                    stDBG.wait(v)
                sDV.inc(v.tensor_scalar(frow, frow, stc(SC_M, j), stc(SC_R, j),
                                        ALU.subtract, ALU.mult))
                snap[("h4", j)] = sDV.n
                stXR.wait(v, snap[("x2r", j)])
                if j < 2:
                    sTE.wait(v, snap[("womm", WO_NC - 1, NT - 1)])
                else:
                    stOUT.wait(v, snap[("outd", j - 2)])
                sDV.wait(v)
                sDV.inc(v.tensor_tensor(outstg, frow, x2slot, ALU.add))
                snap[("out", j)] = sDV.n
            blk.vector(_)

            def _(sy, j=j, outstg=outstg, x2slot=x2slot):
                sDV.wait(sy, snap[("out", j)])
                if DEBUG and j == NT - 1:
                    dbg_dump(sy, "d_xr7", x2slot)
                stOUT.inc(sy.dma_start(out=y_d[j * 128:(j + 1) * 128, :],
                                       in_=outstg), 16)
                snap[("outd", j)] = stOUT.n
            blk.sync(_)


        for c in range(1, FF_NC):
            for j in range(NT):
                emit_ffn_group(c, j)
                if c == FF_NC - 1:
                    emit_p6_A(j)
                    if j >= 1:
                        emit_p6_B(j - 1)
            if c + 2 < FF_NC:
                def _(g, c=c):
                    sTE.wait(g, snap[("fmm", c, NT - 1)])
                    stW.inc(g.dma_start(out=wtch[:, c % 2, :, :],
                                        in_=wt_d[c + 2]), 16)
                    snap[("wt", c + 2)] = stW.n
                blk.gpsimd(_)

        emit_p6_B(NT - 1)

        def _(s):
            for fn in sc_defer:
                fn(s)
        blk.scalar(_)

        def _(sy):
            stOUT.wait(sy)
            if DEBUG:
                stDBG.wait(sy)
        blk.sync(_)

    return nc


def _host_layout(inputs):
    f = np.ascontiguousarray

    def wqkv(W):
        return f(W.T.reshape(NO, 128, NO, 128).transpose(2, 1, 0, 3))

    def wchunk(W, nch, co):
        return f(W.T.reshape(NO, 128, nch, co).transpose(2, 1, 0, 3))

    A = np.asarray(inputs["L0_ffn_lora_A"], np.float32)
    Bl = np.asarray(inputs["L0_ffn_lora_B"], np.float32)
    rw = np.asarray(inputs["L0_ffn_router_w"], np.float32)
    ar_full = np.concatenate(
        [A.transpose(2, 0, 1).reshape(D, 128), rw.T], axis=1)
    out = {
        "wq": wqkv(np.asarray(inputs["L0_attn_Wq"], np.float32)),
        "wk": wqkv(np.asarray(inputs["L0_attn_Wk"], np.float32)),
        "wv": wqkv(np.asarray(inputs["L0_attn_Wv"], np.float32)),
        "wo": wchunk(np.asarray(inputs["L0_attn_Wo"], np.float32), WO_NC, 256),
        "wt": wchunk(np.asarray(inputs["L0_ffn_weight"], np.float32), FF_NC, 512),
        "ar": f(ar_full.reshape(NO, 128, 136).transpose(1, 0, 2)),
        "bb": f(Bl.transpose(0, 2, 1).reshape(128, D)),
        "bq": f(np.asarray(inputs["L0_attn_bq"], np.float32).reshape(NO, 128).T),
        "bk": f(np.asarray(inputs["L0_attn_bk"], np.float32).reshape(NO, 128).T),
        "bv": f(np.asarray(inputs["L0_attn_bv"], np.float32).reshape(NO, 128).T),
        "bo": f(np.asarray(inputs["L0_attn_bo"], np.float32).reshape(1, D)),
        "idn": np.eye(128, dtype=np.float32),
        "epsc": np.full((128, 1), EPS, np.float32),
        "ones1": np.ones((1, 128), np.float32),
    }
    e2 = np.zeros((8, 128), np.float32)
    for e in range(8):
        e2[e, e * 16:(e + 1) * 16] = 2.0
    out["e2"] = e2
    return out


def kernel(**inputs):
    global _dbg_out
    x = np.asarray(inputs["L0_attn_x"], np.float32)
    Bsz, L, _ = x.shape
    xf = np.ascontiguousarray(x.reshape(Bsz * L, D))
    shared = _host_layout(inputs)

    nc = build_program()

    in_maps = []
    for c in range(8):
        m = dict(shared)
        m["x"] = np.ascontiguousarray(xf[c * T:(c + 1) * T])
        mask = np.zeros((8, 1), np.float32)
        s, p = c // 4, c % 4
        for r in range(8):
            if r // 4 == s and r % 4 < p:
                mask[r, 0] = 1.0
        m["maskc"] = mask
        in_maps.append(m)

    res = run_bass_kernel_spmd(nc, in_maps, core_ids=list(range(8)))
    if DEBUG:
        _dbg_out = res.results
    y = np.concatenate([res.results[c]["y"] for c in range(8)], axis=0)
    return y.reshape(Bsz, L, D)


# version 10
# speedup vs baseline: 1.0592x; 1.0390x over previous
import math
import numpy as np
from contextlib import ExitStack

import concourse.bass as bass
import concourse.mybir as mybir
from concourse.bass_utils import run_bass_kernel_spmd

F32 = mybir.dt.float32
BF16 = mybir.dt.bfloat16
ALU = mybir.AluOpType
AF = mybir.ActivationFunctionType
AX = mybir.AxisListType.X

T = 1024
D = 2048
NT = 8
NO = 16
NG = 4
WO_NC = 8
FF_NC = 4
EPS = 1e-5
SCALE = (D // 16) ** -0.5
LNS = math.log(SCALE)
BIG = 1.0e30

DEBUG = False
_dbg_out = {}


class S:

    def __init__(self, nc, ctx, name):
        self.h = ctx.enter_context(nc.semaphore(name))
        self.n = 0

    def inc(self, inst, k=1):
        inst.then_inc(self.h, k)
        self.n += k
        return self.n

    def wait(self, eng, val=None):
        v = self.n if val is None else val
        if v > 0:
            eng.wait_ge(self.h, v)

    def prebump(self, k=1):
        self.n += k
        return self.n


def build_program():
    nc = bass.Bass("TRN2", target_bir_lowering=False, debug=False)

    x_d = nc.dram_tensor("x", [T, D], F32, kind="ExternalInput").ap()
    wq_d = nc.dram_tensor("wq", [NO, 128, NO, 128], F32, kind="ExternalInput").ap()
    wk_d = nc.dram_tensor("wk", [NO, 128, NO, 128], F32, kind="ExternalInput").ap()
    wv_d = nc.dram_tensor("wv", [NO, 128, NO, 128], F32, kind="ExternalInput").ap()
    wo_d = nc.dram_tensor("wo", [WO_NC, 128, NO, 256], F32, kind="ExternalInput").ap()
    wt_d = nc.dram_tensor("wt", [FF_NC, 128, NO, 512], F32, kind="ExternalInput").ap()
    ar_d = nc.dram_tensor("ar", [128, NO, 136], F32, kind="ExternalInput").ap()
    bb_d = nc.dram_tensor("bb", [128, D], F32, kind="ExternalInput").ap()
    bq_d = nc.dram_tensor("bq", [128, NO], F32, kind="ExternalInput").ap()
    bk_d = nc.dram_tensor("bk", [128, NO], F32, kind="ExternalInput").ap()
    bv_d = nc.dram_tensor("bv", [128, NO], F32, kind="ExternalInput").ap()
    bo_d = nc.dram_tensor("bo", [1, D], F32, kind="ExternalInput").ap()
    idn_d = nc.dram_tensor("idn", [128, 128], F32, kind="ExternalInput").ap()
    e2_d = nc.dram_tensor("e2", [8, 128], F32, kind="ExternalInput").ap()
    ones_d = nc.dram_tensor("ones1", [1, 128], F32, kind="ExternalInput").ap()
    mask_d = nc.dram_tensor("maskc", [8, 1], F32, kind="ExternalInput").ap()
    eps_d = nc.dram_tensor("epsc", [128, 1], F32, kind="ExternalInput").ap()
    y_d = nc.dram_tensor("y", [T, D], F32, kind="ExternalOutput").ap()

    x2sp = nc.dram_tensor("x2sp", [T, D], F32, kind="Internal").ap()
    cin = [nc.dram_tensor(f"cin{g}", [1, 512], F32, kind="Internal").ap()
           for g in range(NG)]
    cout = [nc.dram_tensor(f"cout{g}", [8, 512], F32, kind="Internal",
                           addr_space="Shared").ap()
            for g in range(NG)]

    dbg = {}
    if DEBUG:
        for nm, shp in [
            ("d_h1t", [128, NO * T // 2]), ("d_phi", [128, T]),
            ("d_ukv", [128, T]), ("d_ctx", [128, NO * T // 2]),
            ("d_a", [128, D]), ("d_x2", [128, D]),
            ("d_lg", [128, 8]), ("d_w", [128, 8]),
            ("d_sx", [128, T // 2]), ("d_f", [128, D]),
            ("d_pref", [128, NO]), ("d_tot", [128, NO]),
            ("d_h3t", [128, NO * T // 2]), ("d_ax", [128, T]),
            ("d_x27", [128, D]), ("d_h3b7", [128, D // 2]),
            ("d_a7", [128, D]), ("d_st7", [128, 56]), ("d_asum", [128, 64]),
            ("d_f7", [128, D]), ("d_st47", [128, 56]), ("d_fsum", [128, 32]),
            ("d_xr7", [128, D]),
            ("d_wf", [8, T // 2]), ("d_lg2", [8, T]),
        ]:
            dbg[nm] = nc.dram_tensor(nm, shp, F32, kind="ExternalOutput").ap()

    ctx_mgr = ExitStack()
    with ctx_mgr as ctx:
        def sb(name, shape, dt=F32):
            return ctx.enter_context(nc.sbuf_tensor(name, shape, dt))

        def psum(name, shape, dt=F32):
            return ctx.enter_context(nc.psum_tensor(name, shape, dt))

        A_own = sb("A_own", [128, 8192])
        B_own = sb("B_own", [128, 8192])
        C_own = sb("C_own", [128, 16384])
        D_own = sb("D_own", [128, 6144])
        WOX = sb("WOX", [128, 6144])
        E_own = sb("E_own", [128, 2048])
        STG = sb("STG", [128, 2048])
        FB = sb("FB", [128, 2112])
        stats = sb("stats", [128, 56])
        asum = sb("asum", [128, 64])
        fsum = sb("fsum", [128, 32])
        xs1 = sb("xs1", [128, 8])
        bqc = sb("bqc", [128, NO]); bkc = sb("bkc", [128, NO]); bvc = sb("bvc", [128, NO])
        bqLn = sb("bqLn", [128, NO]); bqS2 = sb("bqS2", [128, NO])
        idn = sb("idn_s", [128, 128])
        e2f = sb("e2f", [8, 64])
        onesf = sb("onesf", [1, 64])
        bof = sb("bof", [1, 1024])
        maskc = sb("maskc_s", [8, 1])
        epsc = sb("epsc_s", [128, 1])
        tot0 = sb("tot0", [128, NO]); tot1 = sb("tot1", [128, NO])
        totS = sb("totS", [128, NO])
        pref = sb("pref", [128, NO])
        ltm = sb("ltm", [128, 8])
        r8a = sb("r8a", [128, 8]); r8b = sb("r8b", [128, 8]); r8c = sb("r8c", [128, 8])
        rc = sb("rc", [128, 8])
        wtm = sb("wtm", [128, 8])

        gsb = FB.ap()[0:8, 0:512]
        lgsb = E_own.ap()[0:8, 0:1024]
        wfm = E_own.ap()[0:8, 1024:1536]
        h1T = A_own.ap().bitcast(BF16).rearrange("p (n t) -> p n t", n=NO)
        ctxT = B_own.ap().bitcast(BF16).rearrange("p (n t) -> p n t", n=NO)
        wtch = B_own.ap().bitcast(BF16).rearrange("p (c n f) -> p c n f", c=2, n=NO)
        Cap = C_own.ap()
        C3 = Cap.rearrange("p (j f) -> p j f", j=NT)
        qkvw = D_own.ap().bitcast(BF16).rearrange("p (s f) -> p s f", s=6)
        woch = WOX.ap().bitcast(BF16).rearrange("p (c n f) -> p c n f", c=3, n=NO)
        dump6 = WOX.ap().rearrange("p (c f) -> p c f", c=3)
        h1b = FB.ap()[:, 0:1024].bitcast(BF16)
        bbv = FB.ap()[:, 0:1024].bitcast(BF16)
        arb = FB.ap()[:, 1024:2112].bitcast(BF16).rearrange(
            "p (n f) -> p n f", n=NO)
        e2b = e2f.ap().bitcast(BF16)
        onesb = onesf.ap().bitcast(BF16)
        bob = bof.ap().bitcast(BF16)
        axs = STG.ap()[:, 0:1024]
        sxb = STG.ap()[:, 1024:1536].bitcast(BF16)
        wfmb = wfm.bitcast(BF16)

        pb0 = psum("pb0", [128, 1024])
        pb1 = psum("pb1", [128, 1024])
        pvv = psum("pvv", [128, 1024])
        psm = psum("psm", [128, 512])

        stX = S(nc, ctx, "stX")
        stTP = S(nc, ctx, "stTP")
        stW = S(nc, ctx, "stW")
        stCIN = S(nc, ctx, "stCIN")
        stGSB = S(nc, ctx, "stGSB")
        stSP = S(nc, ctx, "stSP")
        stXR = S(nc, ctx, "stXR")
        stOUT = S(nc, ctx, "stOUT")
        stDBG = S(nc, ctx, "stDBG")
        sCC = S(nc, ctx, "sCC")
        sPL = S(nc, ctx, "sPL")
        sTE = S(nc, ctx, "sTE")
        sDV = S(nc, ctx, "sDV")
        sSC = S(nc, ctx, "sSC")

        blk = ctx.enter_context(nc.Block())
        snap = {}
        sc_defer = []

        def SCD(fn):
            sc_defer.append(fn)

        def dbg_dump(sy, name, src_ap):
            if DEBUG and name in dbg:
                stDBG.inc(sy.dma_start(out=dbg[name], in_=src_ap), 16)

        def _(g):
            for dst, src in [
                (bqc.ap(), bq_d), (bkc.ap(), bk_d), (bvc.ap(), bv_d),
                (idn.ap(), idn_d), (maskc.ap(), mask_d), (epsc.ap(), eps_d),
            ]:
                stW.inc(g.dma_start(out=dst, in_=src), 16)
            stW.inc(g.dma_start(out=e2b, in_=e2_d), 16)
            stW.inc(g.dma_start(out=onesb, in_=ones_d), 16)
            stW.inc(g.dma_start(out=bob, in_=bo_d), 16)
            stW.inc(g.dma_start(out=arb, in_=ar_d), 16)
            snap["consts"] = stW.n
        blk.gpsimd(_)

        def _(v):
            stW.wait(v, snap["consts"])
            sDV.inc(v.tensor_single_scalar(bqLn.ap(), bqc.ap(), LNS, ALU.add))
            sDV.inc(v.tensor_single_scalar(bqS2.ap(), bqc.ap(), SCALE, ALU.mult))
            snap["dbias"] = sDV.n
        blk.vector(_)

        SC_S, SC_Q, SC_M, SC_M2, SC_VAR, SC_STD, SC_R = 0, 8, 16, 24, 32, 40, 48

        def stc(base, j):
            return stats.ap()[:, base + j:base + j + 1]

        def ln_stats_A(j, src_ap, dump_ap, tag, sum_on_scalar=False):
            if sum_on_scalar:
                def _(s):
                    s.activation(dump_ap, src_ap, AF.Copy,
                                 accum_out=stc(SC_S, j)).then_inc(sSC.h, 1)
                SCD(_)
                snap[(tag, "s", j)] = ("sc", sSC.prebump())
            else:
                def _(v):
                    sDV.inc(v.reduce_sum(stc(SC_S, j), src_ap, AX))
                    snap[(tag, "s", j)] = ("dv", sDV.n)
                blk.vector(_)

            def _(s):
                s.activation(dump_ap, src_ap, AF.Square,
                             accum_out=stc(SC_Q, j)).then_inc(sSC.h, 1)
            SCD(_)
            snap[(tag, "q", j)] = ("sc", sSC.prebump())

        def ln_stats_B(j, tag):
            kind, sval = snap[(tag, "s", j)]
            qk, qval = snap[(tag, "q", j)]

            def _(v):
                if qk == "sc":
                    sSC.wait(v, qval)
                else:
                    sDV.wait(v, qval)
                if kind == "dv":
                    sDV.wait(v, sval)
                sDV.inc(v.tensor_single_scalar(stc(SC_M, j), stc(SC_S, j),
                                               1.0 / D, ALU.mult))
                sDV.wait(v)
                sDV.inc(v.tensor_tensor(stc(SC_M2, j), stc(SC_M, j),
                                        stc(SC_M, j), ALU.mult))
                sDV.wait(v)
                sDV.inc(v.scalar_tensor_tensor(stc(SC_VAR, j), stc(SC_Q, j),
                                               1.0 / D, stc(SC_M2, j),
                                               ALU.mult, ALU.subtract))
                snap[(tag, "var", j)] = sDV.n
            blk.vector(_)

            vvar = snap[(tag, "var", j)]
            vln = sSC.prebump()
            snap[(tag, "r", j)] = sSC.prebump()

            def _(s, vvar=vvar, vln=vln):
                sDV.wait(s, vvar)
                s.activation(stc(SC_STD, j), stc(SC_VAR, j),
                             AF.Ln, bias=epsc.ap()).then_inc(sSC.h, 1)
                s.wait_ge(sSC.h, vln)
                s.activation(stc(SC_R, j), stc(SC_STD, j),
                             AF.Exp, scale=-0.5).then_inc(sSC.h, 1)
            SCD(_)

        def p1_xslot(j):
            return E_own.ap() if j % 2 == 0 else STG.ap()

        def emit_ln1_B(j):
            xslot = p1_xslot(j)
            ln_stats_B(j, "ln1")

            def _(v, j=j):
                sDV.inc(v.tensor_copy(xs1.ap()[:, j:j + 1], stc(SC_S, j)))
            blk.vector(_)

            def _(v, j=j, xslot=xslot):
                sSC.wait(v, snap[("ln1", "r", j)])
                if j >= 1:
                    stTP.wait(v, snap[("tp1", j - 1)])
                sDV.inc(v.tensor_scalar(h1b, xslot, stc(SC_M, j), stc(SC_R, j),
                                        ALU.subtract, ALU.mult))
                snap[("ln1h1b", j)] = sDV.n
            blk.vector(_)

            def _(sy, j=j):
                sDV.wait(sy, snap[("ln1h1b", j)])
                src = h1b.rearrange("p (n t) -> p n t", n=NO)
                dst = h1T[:, :, j * 128:(j + 1) * 128]
                stTP.inc(sy.dma_start(out=dst, in_=src, transpose=True), 16)
                snap[("tp1", j)] = stTP.n
            blk.sync(_)

        for j in range(NT):
            xslot = p1_xslot(j)

            vgate = snap[("ln1h1b", j - 2)] if j >= 2 else 0

            def _(s, j=j, xslot=xslot, vgate=vgate):
                if vgate:
                    sDV.wait(s, vgate)
                s.dma_start(out=xslot,
                            in_=x_d[j * 128:(j + 1) * 128, :]).then_inc(stX.h, 16)
            SCD(_)
            snap[("xload", j)] = stX.prebump(16)

            def _(v, j=j):
                stX.wait(v, snap[("xload", j)])
            blk.vector(_)

            vx = snap[("xload", j)]

            def _(s, vx=vx):
                stX.wait(s, vx)
            SCD(_)

            ln_stats_A(j, xslot, Cap[:, (j % 2) * 2048:(j % 2) * 2048 + 2048], "ln1")
            if j >= 1:
                emit_ln1_B(j - 1)
        emit_ln1_B(NT - 1)

        slot_last_te = [0] * 6

        def load_qkv(g, i):
            for wi, wsrc in enumerate((wq_d, wk_d, wv_d)):
                s6 = (3 * i + wi) % 6
                sTE.wait(g, slot_last_te[s6])
                stW.inc(g.dma_start(out=qkvw[:, s6, :],
                                    in_=wsrc[i].rearrange("p n f -> p (n f)")), 16)
            snap[("wq", i)] = stW.n

        def emit_cc(g, grp):
            stCIN.wait(g, snap[("cin", grp)])
            sCC.inc(g.collective_compute(
                "AllGather", ALU.bypass, replica_groups=[list(range(8))],
                ins=[cin[grp].opt()], outs=[cout[grp].opt()]))
            snap[("cc", grp)] = sCC.n

        def emit_prefix(grp2, prevg):
            def _(sy):
                sCC.wait(sy, snap[("cc", grp2)])
                if prevg is not None:
                    sTE.wait(sy, snap[("prefmm", prevg)])
                stGSB.inc(sy.dma_start(out=gsb, in_=cout[grp2]), 16)
                snap[("gsb", grp2)] = stGSB.n
            blk.sync(_)

            def _(t):
                stGSB.wait(t, snap[("gsb", grp2)])
                if prevg is not None:
                    sDV.wait(t, snap[("prefevac", prevg)])
                gv = gsb.rearrange("r (c g) -> r c g", g=4)
                inst = None
                for q in range(4):
                    inst = t.matmul(psm.ap()[:, q:q + 1], gv[:, :, q],
                                    maskc.ap(), start=True, stop=True)
                sTE.inc(inst)
                snap[("prefmm", grp2)] = sTE.n
            blk.tensor(_)

            def _(v):
                sTE.wait(v, snap[("prefmm", grp2)])
                sDV.inc(v.tensor_copy(pref.ap()[:, grp2 * 4:grp2 * 4 + 4],
                                      psm.ap()[:, 0:4]))
                snap[("prefevac", grp2)] = sDV.n
            blk.vector(_)

        def emit_scan_ctx(i2):
            grp2 = i2 // 4

            def _(v):
                sDV.wait(v, snap[("prefevac", grp2)])
                slot = i2 % 8
                ub = Cap[:, 8192 + slot * 1024: 8192 + (slot + 1) * 1024]
                sDV.inc(v.tensor_tensor_scan(ub, ub, ub,
                                             pref.ap()[:, i2:i2 + 1],
                                             ALU.add, ALU.bypass))
                sDV.wait(v)
                sDV.inc(v.tensor_tensor(ctxT[:, i2, :], ub,
                                        Cap[:, slot * 1024:(slot + 1) * 1024],
                                        ALU.mult))
                snap[("ctx", i2)] = sDV.n
            blk.vector(_)

        ORDER = [12, 13, 14, 15] + list(range(12))

        def _(g):
            stX.wait(g, snap[("xload", 2)])
            load_qkv(g, ORDER[0])
            stX.wait(g, snap[("xload", 4)])
            load_qkv(g, ORDER[1])
        blk.gpsimd(_)

        for p in range(NO):
            i = ORDER[p]
            iprev = ORDER[p - 1] if p >= 1 else None
            for h in range(2):
                tsl = slice(h * 512, (h + 1) * 512)

                def _(t, i=i, h=h, tsl=tsl, p=p, iprev=iprev):
                    stW.wait(t, snap[("wq", i)])
                    if p == 0 and h == 0:
                        sDV.wait(t, snap["dbias"])
                    if iprev is not None:
                        sSC.wait(t, snap[("kacts", iprev, h)])
                        sDV.wait(t, snap[("u", iprev, h)])
                    for wi, pst in ((0, pb0), (1, pb1), (2, pvv)):
                        s6 = (3 * i + wi) % 6
                        inst = None
                        if p == 0:
                            for d in range(NO):
                                for j4 in range(4):
                                    jj = h * 4 + j4
                                    if wi == 0 and d == 0:
                                        stTP.wait(t, snap[("tp1", jj)])
                                    inst = t.matmul(
                                        pst.ap()[:, h * 512 + j4 * 128:
                                                 h * 512 + (j4 + 1) * 128],
                                        qkvw[:, s6, d * 128:(d + 1) * 128],
                                        h1T[:, d, jj * 128:(jj + 1) * 128],
                                        start=(d == 0 and j4 == 0),
                                        stop=(d == NO - 1),
                                        skip_group_check=True)
                        else:
                            for d in range(NO):
                                inst = t.matmul(
                                    pst.ap()[:, tsl],
                                    qkvw[:, s6, d * 128:(d + 1) * 128],
                                    h1T[:, d, tsl],
                                    start=(d == 0), stop=(d == NO - 1))
                        sTE.inc(inst)
                        snap[("mm", wi, i, h)] = sTE.n
                    for wi in range(3):
                        slot_last_te[(3 * i + wi) % 6] = sTE.n
                blk.tensor(_)

                vu_prev = snap[("u", iprev, h)] if iprev is not None else 0

                def _(s, i=i, h=h, tsl=tsl, vu_prev=vu_prev, p=p):
                    if vu_prev:
                        sDV.wait(s, vu_prev)
                    if p == 0 and h == 0:
                        sDV.wait(s, snap[("ln1h1b", NT - 1)])
                    sTE.wait(s, snap[("mm", 0, i, h)])
                    s.activation(STG.ap()[:, 0:512], pb0.ap()[:, tsl],
                                 AF.Exp, bias=bqLn.ap()[:, i:i + 1]).then_inc(sSC.h, 1)
                    s.activation(STG.ap()[:, 512:1024], pb0.ap()[:, tsl],
                                 AF.Relu, bias=bqS2.ap()[:, i:i + 1],
                                 scale=SCALE).then_inc(sSC.h, 1)
                    sTE.wait(s, snap[("mm", 1, i, h)])
                    s.activation(STG.ap()[:, 1024:1536], pb1.ap()[:, tsl],
                                 AF.Exp, bias=bkc.ap()[:, i:i + 1]).then_inc(sSC.h, 1)
                    s.activation(STG.ap()[:, 1536:2048], pb1.ap()[:, tsl],
                                 AF.Relu, bias=bkc.ap()[:, i:i + 1]).then_inc(sSC.h, 1)
                SCD(_)
                sSC.prebump(2)
                snap[("qacts", i, h)] = sSC.n
                sSC.prebump(2)
                snap[("kacts", i, h)] = sSC.n

                def _(v, i=i, h=h, tsl=tsl):
                    slot = i % 8
                    sSC.wait(v, snap[("qacts", i, h)])
                    sDV.inc(v.scalar_tensor_tensor(
                        Cap[:, slot * 1024 + h * 512: slot * 1024 + (h + 1) * 512],
                        STG.ap()[:, 0:512], SCALE, STG.ap()[:, 512:1024],
                        ALU.min, ALU.add))
                    sSC.wait(v, snap[("kacts", i, h)])
                    sDV.inc(v.scalar_tensor_tensor(
                        STG.ap()[:, 1024:1536],
                        STG.ap()[:, 1024:1536], 1.0, STG.ap()[:, 1536:2048],
                        ALU.min, ALU.add))
                    sDV.wait(v)
                    sTE.wait(v, snap[("mm", 2, i, h)])
                    tcol = tot0 if h == 0 else tot1
                    sDV.inc(v.scalar_tensor_tensor(
                        Cap[:, 8192 + slot * 1024 + h * 512:
                            8192 + slot * 1024 + (h + 1) * 512],
                        pvv.ap()[:, tsl], bvc.ap()[:, i:i + 1],
                        STG.ap()[:, 1024:1536],
                        ALU.add, ALU.mult,
                        accum_out=tcol.ap()[:, i:i + 1]))
                    snap[("u", i, h)] = sDV.n
                blk.vector(_)

            grp = i // 4
            if i % 4 == 3:
                g4 = slice(grp * 4, grp * 4 + 4)

                def _(v, grp=grp, g4=g4):
                    sDV.wait(v)
                    sDV.inc(v.tensor_tensor(totS.ap()[:, g4], tot0.ap()[:, g4],
                                            tot1.ap()[:, g4], ALU.add))
                    snap[("tot", grp)] = sDV.n
                blk.vector(_)

                def _(sy, grp=grp, g4=g4):
                    sDV.wait(sy, snap[("tot", grp)])
                    dst = cin[grp].rearrange("a (c g) -> a c g", g=4)[0]
                    stCIN.inc(sy.dma_start(out=dst, in_=totS.ap()[:, g4]), 16)
                    snap[("cin", grp)] = stCIN.n
                blk.sync(_)

            def _(g, i=i, p=p):
                if p + 2 < NO:
                    load_qkv(g, ORDER[p + 2])
                if p in (11, 12, 13):
                    w0 = p - 11
                    stW.inc(g.dma_start(out=woch[:, w0, :, :], in_=wo_d[w0]), 16)
                    snap[("wo", w0)] = stW.n
                if i % 4 == 3:
                    emit_cc(g, i // 4)
            blk.gpsimd(_)

            _tails = {5: [("p", 3, None), ("s", 12)],
                      6: [("s", 13), ("s", 14)], 7: [("s", 15)],
                      9: [("p", 0, 3), ("s", 0)],
                      10: [("s", 1), ("s", 2)], 11: [("s", 3)],
                      13: [("p", 1, 0), ("s", 4)],
                      14: [("s", 5), ("s", 6)], 15: [("s", 7)]}
            for task in _tails.get(p, []):
                if task[0] == "p":
                    emit_prefix(task[1], task[2])
                else:
                    emit_scan_ctx(task[1])

        emit_prefix(2, 1)
        for i2 in range(8, 12):
            emit_scan_ctx(i2)

        if DEBUG:
            def _(sy):
                sDV.wait(sy, snap[("ctx", 11)])
                dbg_dump(sy, "d_tot", totS.ap())
                dbg_dump(sy, "d_pref", pref.ap())
                dbg_dump(sy, "d_phi", Cap[:, 7 * 1024:8 * 1024])
                dbg_dump(sy, "d_ukv", Cap[:, 8192 + 7 * 1024: 8192 + 8 * 1024])
                dbg_dump(sy, "d_h1t", A_own.ap())
                dbg_dump(sy, "d_ctx", B_own.ap())
            blk.sync(_)

        def emit_ln23_A(j):
            xslot = E_own.ap() if j % 2 == 0 else STG.ap()
            arow = C3[:, j, :]
            dmp2 = D_own.ap()[:, (j % 2) * 2048:(j % 2) * 2048 + 2048]

            if j >= 2:
                vgate2 = ("dv", snap[("h3b", j - 2)])
            elif j == 1:
                vgate2 = ("dv", snap[("u", NO - 1, 1)])
            else:
                vgate2 = None

            def _(s, j=j, xslot=xslot, vgate2=vgate2):
                if vgate2 is not None:
                    sDV.wait(s, vgate2[1])
                s.dma_start(out=xslot,
                            in_=x_d[j * 128:(j + 1) * 128, :]).then_inc(stX.h, 16)
            SCD(_)
            snap[("xload2", j)] = stX.prebump(16)

            def _(v, j=j):
                sSC.wait(v, snap[("aevac", WO_NC - 1, j)])
                sDV.inc(v.reduce_sum(stc(SC_S, j),
                                     asum.ap()[:, j * 8:(j + 1) * 8], AX))
                snap[("ln2_%d" % j, "s", j)] = ("dv", sDV.n)
            blk.vector(_)

            def _(s, j=j):
                sSC.wait(s, snap[("aevac", WO_NC - 1, j)])
            SCD(_)

            tag2 = "ln2_%d" % j

            def _(s, j=j, tag2=tag2):
                s.activation(dmp2, arow, AF.Square,
                             accum_out=stc(SC_Q, j)).then_inc(sSC.h, 1)
            SCD(_)
            snap[(tag2, "q", j)] = ("sc", sSC.prebump())

        def emit_ln23_B(j):
            xslot = E_own.ap() if j % 2 == 0 else STG.ap()
            arow = C3[:, j, :]
            dmp2 = D_own.ap()[:, (j % 2) * 2048:(j % 2) * 2048 + 2048]

            ln_stats_B(j, "ln2_%d" % j)

            if DEBUG and j == NT - 1:
                def _(sy, j=j, arow=arow):
                    sSC.wait(sy, snap[("ln2_%d" % j, "r", j)])
                    dbg_dump(sy, "d_a7", arow)
                    dbg_dump(sy, "d_st7", stats.ap())
                    dbg_dump(sy, "d_asum", asum.ap())
                blk.sync(_)

            def _(v, j=j, xslot=xslot, arow=arow):
                sSC.wait(v, snap[("ln2_%d" % j, "r", j)])
                if DEBUG and j == NT - 1:
                    stDBG.wait(v)
                sDV.inc(v.tensor_scalar(arow, arow, stc(SC_M, j), stc(SC_R, j),
                                        ALU.subtract, ALU.mult))
                stX.wait(v, snap[("xload2", j)])
                sDV.wait(v)
                sDV.inc(v.tensor_tensor(xslot, arow, xslot, ALU.add))
                snap[("x2", j)] = sDV.n
            blk.vector(_)

            vx2 = snap[("x2", j)]

            def _(s, j=j, xslot=xslot, vx2=vx2):
                sDV.wait(s, vx2)
                s.dma_start(out=x2sp[j * 128:(j + 1) * 128, :],
                            in_=xslot).then_inc(stSP.h, 16)
            SCD(_)
            snap[("sp", j)] = stSP.prebump(16)

            if DEBUG and (j == 0 or j == NT - 1):
                def _(sy, j=j, xslot=xslot):
                    sDV.wait(sy, snap[("x2", j)])
                    if j == 0:
                        dbg_dump(sy, "d_x2", xslot)
                    else:
                        dbg_dump(sy, "d_x27", xslot)
                blk.sync(_)

            tag3 = "ln3_%d" % j

            def _(v, j=j, xslot=xslot, vx2=vx2):
                sDV.inc(v.tensor_copy(stc(SC_S, j), xs1.ap()[:, j:j + 1]))
                snap[(tag3, "s", j)] = ("dv", sDV.n)
            blk.vector(_)

            def _(s, j=j, xslot=xslot, vx2=vx2):
                sDV.wait(s, vx2)
                s.activation(dmp2, xslot, AF.Square,
                             accum_out=stc(SC_Q, j)).then_inc(sSC.h, 1)
            SCD(_)
            snap[(tag3, "q", j)] = ("sc", sSC.prebump())

            ln_stats_B(j, "ln3_%d" % j)

            def _(v, j=j, xslot=xslot):
                sSC.wait(v, snap[("ln3_%d" % j, "r", j)])
                if j >= 1:
                    stTP.wait(v, snap[("tp3", j - 1)])
                else:
                    stTP.wait(v, snap[("tp1", NT - 1)])
                sDV.inc(v.tensor_scalar(h1b, xslot, stc(SC_M, j), stc(SC_R, j),
                                        ALU.subtract, ALU.mult))
                snap[("h3b", j)] = sDV.n
            blk.vector(_)

            def _(sy, j=j):
                sDV.wait(sy, snap[("h3b", j)])
                src = h1b.rearrange("p (n t) -> p n t", n=NO)
                dst = h1T[:, :, j * 128:(j + 1) * 128]
                stTP.inc(sy.dma_start(out=dst, in_=src, transpose=True), 16)
                snap[("tp3", j)] = stTP.n
            blk.sync(_)


        bank_prev = [None, None]

        def emit_wo_group(c, j):
            bank = (c + j) % 2
            aps = pb0.ap()[:, 0:256] if bank == 0 else pb0.ap()[:, 512:768]
            osl = slice(c * 256, (c + 1) * 256)

            def _(t, c=c, j=j, aps=aps, osl=osl, bprev=bank_prev[bank]):
                stW.wait(t, snap[("wo", c)])
                if bprev is not None:
                    sSC.wait(t, bprev)
                inst = t.matmul(aps, onesb, bob[0:1, osl],
                                start=True, stop=False)
                cts = list(range(8)) + list(range(12, 16)) + list(range(8, 12))
                for k_, ct in enumerate(cts):
                    if c == 0 and j == 0:
                        sDV.wait(t, snap[("ctx", ct)])
                    inst = t.matmul(
                        aps, ctxT[:, ct, j * 128:(j + 1) * 128],
                        woch[:, c % 3, ct, :],
                        start=False, stop=(k_ == NO - 1))
                sTE.inc(inst)
                snap[("womm", c, j)] = sTE.n
            blk.tensor(_)

            def _(s, c=c, j=j, aps=aps, osl=osl):
                sTE.wait(s, snap[("womm", c, j)])
                s.activation(C3[:, j, osl], aps, AF.Copy,
                             accum_out=asum.ap()[:, j * 8 + c:j * 8 + c + 1]
                             ).then_inc(sSC.h, 1)
            SCD(_)
            snap[("aevac", c, j)] = sSC.prebump()
            bank_prev[bank] = snap[("aevac", c, j)]

        for c in range(WO_NC - 2):
            for j in range(NT):
                emit_wo_group(c, j)
            if c + 3 < WO_NC:
                def _(g, c=c):
                    sTE.wait(g, snap[("womm", c, NT - 1)])
                    stW.inc(g.dma_start(out=woch[:, (c + 3) % 3, :, :],
                                        in_=wo_d[c + 3]), 16)
                    snap[("wo", c + 3)] = stW.n
                blk.gpsimd(_)

        for j in range(NT):
            emit_wo_group(WO_NC - 2, j)
            emit_wo_group(WO_NC - 1, j)
            emit_ln23_A(j)
            if j >= 1:
                emit_ln23_B(j - 1)

        emit_ln23_B(NT - 1)

        def _(g):
            stTP.wait(g, snap[("tp3", NT - 1)])
            stW.inc(g.dma_start(out=bbv, in_=bb_d), 16)
            snap["bb"] = stW.n
            sTE.wait(g, snap[("womm", WO_NC - 1, NT - 1)])
            stW.inc(g.dma_start(out=wtch[:, 0, :, :], in_=wt_d[0]), 16)
            snap[("wt", 0)] = stW.n
            stW.inc(g.dma_start(out=wtch[:, 1, :, :], in_=wt_d[1]), 16)
            snap[("wt", 1)] = stW.n
        blk.gpsimd(_)

        if DEBUG:
            def _(sy):
                stTP.wait(sy, snap[("tp3", NT - 1)])
                dbg_dump(sy, "d_h3t", A_own.ap())
            blk.sync(_)

        for hh in range(2):
            pslice = pb1.ap()[:, hh * 512:(hh + 1) * 512]

            def _(t, hh=hh, pslice=pslice):
                stTP.wait(t, snap[("tp3", 3 if hh == 0 else NT - 1)])
                if hh == 0:
                    sSC.wait(t, snap[("kacts", NO - 1, 1)])
                inst = None
                for d in range(NO):
                    inst = t.matmul(pslice, arb[:, d, 0:128],
                                    h1T[:, d, hh * 512:(hh + 1) * 512],
                                    start=(d == 0), stop=(d == NO - 1))
                sTE.inc(inst)
                snap[("axmm", hh)] = sTE.n
            blk.tensor(_)

            def _(v, hh=hh, pslice=pslice):
                sTE.wait(v, snap[("axmm", hh)])
                if hh == 0:
                    stSP.wait(v, snap[("sp", NT - 1)])
                sDV.inc(v.tensor_copy(axs[:, hh * 512:(hh + 1) * 512], pslice))
                snap[("axev", hh)] = sDV.n
            blk.vector(_)

            def _(t, hh=hh, pslice=pslice):
                sDV.wait(t, snap[("axev", hh)])
                inst = None
                for d in range(NO):
                    inst = t.matmul(pslice[0:8, :], arb[:, d, 128:136],
                                    h1T[:, d, hh * 512:(hh + 1) * 512],
                                    start=(d == 0), stop=(d == NO - 1))
                sTE.inc(inst)
                snap[("lgmm", hh)] = sTE.n
            blk.tensor(_)

            def _(v, hh=hh, pslice=pslice):
                sTE.wait(v, snap[("lgmm", hh)])
                sDV.inc(v.tensor_copy(lgsb[0:8, hh * 512:(hh + 1) * 512],
                                      pslice[0:8, :]))
                snap[("lgev", hh)] = sDV.n
            blk.vector(_)

        ffn_bank_prev = [("dv", snap[("u", NO - 1, 1)]),
                         ("dv", snap[("u", NO - 1, 1)])]

        def emit_ffn_group(c, j):
            bank = j % 2
            fps = pvv.ap()[:, 0:512] if bank == 0 else pvv.ap()[:, 512:1024]
            osl = slice(c * 512, (c + 1) * 512)

            def _(t, c=c, j=j, fps=fps, osl=osl, bprev=ffn_bank_prev[bank]):
                stW.wait(t, snap[("wt", c)])
                if c == 0:
                    stW.wait(t, snap["bb"])
                    sDV.wait(t, snap[("sx", j)])
                kind, val = bprev
                if kind == "dv":
                    sDV.wait(t, val)
                else:
                    sSC.wait(t, val)
                for d in range(NO):
                    t.matmul(fps, h1T[:, d, j * 128:(j + 1) * 128],
                             wtch[:, c % 2, d, :],
                             start=(d == 0), stop=False)
                inst = t.matmul(fps, sxb[:, j * 128:(j + 1) * 128],
                                bbv[:, osl], start=False, stop=True)
                sTE.inc(inst)
                snap[("fmm", c, j)] = sTE.n
            blk.tensor(_)

            def _(s, c=c, j=j, fps=fps, osl=osl):
                sTE.wait(s, snap[("fmm", c, j)])
                s.activation(C3[:, j, osl], fps, AF.Copy,
                             accum_out=fsum.ap()[:, j * 4 + c:j * 4 + c + 1]
                             ).then_inc(sSC.h, 1)
            SCD(_)
            snap[("fevac", c, j)] = sSC.prebump()
            ffn_bank_prev[bank] = ("sc", snap[("fevac", c, j)])

        for j in range(NT):
            hh = j // 4

            def _(t, j=j, hh=hh):
                sDV.wait(t, snap[("lgev", hh)])
                if j > 0:
                    sDV.wait(t, snap[("sx", j - 1)])
                sTE.inc(t.transpose(psm.ap()[:, 0:8],
                                    lgsb[0:8, j * 128:(j + 1) * 128],
                                    idn.ap()[0:8, 0:8]))
                snap[("lgT", j)] = sTE.n
            blk.tensor(_)

            def _(v, j=j):
                sTE.wait(v, snap[("lgT", j)])
                sDV.inc(v.tensor_copy(ltm.ap(), psm.ap()[:, 0:8]))
                sDV.wait(v)
                sDV.inc(v.reduce_max(rc.ap()[:, 0:1], ltm.ap(), AX))
                sDV.wait(v)
                sDV.inc(v.tensor_single_scalar(r8a.ap(), ltm.ap(),
                                               rc.ap()[:, 0:1], ALU.is_ge))
                sDV.wait(v)
                sDV.inc(v.scalar_tensor_tensor(r8b.ap(), r8a.ap(), -BIG,
                                               ltm.ap(), ALU.mult, ALU.add))
                sDV.wait(v)
                sDV.inc(v.reduce_max(rc.ap()[:, 1:2], r8b.ap(), AX))
                sDV.wait(v)
                sDV.inc(v.tensor_single_scalar(rc.ap()[:, 2:3], rc.ap()[:, 0:1],
                                               -1.0, ALU.mult))
                sDV.wait(v)
                sDV.inc(v.tensor_tensor(rc.ap()[:, 3:4], rc.ap()[:, 1:2],
                                        rc.ap()[:, 0:1], ALU.subtract))
                snap[("rmath", j)] = sDV.n
            blk.vector(_)

            def _(s, j=j):
                sDV.wait(s, snap[("rmath", j)])
                s.activation(r8c.ap(), ltm.ap(), AF.Exp,
                             bias=rc.ap()[:, 2:3]).then_inc(sSC.h, 1)
                s.activation(rc.ap()[:, 4:5], rc.ap()[:, 3:4],
                             AF.Exp).then_inc(sSC.h, 1)
            SCD(_)
            snap[("rexp", j)] = sSC.prebump(2)

            def _(v, j=j):
                sSC.wait(v, snap[("rexp", j)])
                sDV.inc(v.tensor_single_scalar(r8a.ap(), ltm.ap(),
                                               rc.ap()[:, 1:2], ALU.is_ge))
                sDV.inc(v.tensor_single_scalar(rc.ap()[:, 5:6], rc.ap()[:, 4:5],
                                               1.0, ALU.add))
                sDV.wait(v)
                sDV.inc(v.reciprocal(rc.ap()[:, 6:7], rc.ap()[:, 5:6]))
                sDV.wait(v)
                sDV.inc(v.tensor_tensor(r8b.ap(), r8c.ap(), r8a.ap(), ALU.mult))
                sDV.wait(v)
                sDV.inc(v.tensor_single_scalar(wtm.ap(), r8b.ap(),
                                               rc.ap()[:, 6:7], ALU.mult))
                snap[("w", j)] = sDV.n
            blk.vector(_)

            if j >= 1:
                emit_ffn_group(0, j - 1)

            def _(t, j=j):
                sDV.wait(t, snap[("w", j)])
                sTE.inc(t.transpose(psm.ap()[0:8, 128:256], wtm.ap(), idn.ap()))
                snap[("wT", j)] = sTE.n
            blk.tensor(_)

            def _(v, j=j):
                sTE.wait(v, snap[("wT", j)])
                sDV.inc(v.tensor_copy(wfmb[0:8, j * 128:(j + 1) * 128],
                                      psm.ap()[0:8, 128:256]))
                snap[("wev", j)] = sDV.n
            blk.vector(_)

            def _(t, j=j):
                sDV.wait(t, snap[("wev", j)])
                sTE.inc(t.matmul(psm.ap()[:, 256:384], e2b,
                                 wfmb[0:8, j * 128:(j + 1) * 128],
                                 start=True, stop=True))
                snap[("wexpmm", j)] = sTE.n
            blk.tensor(_)

            def _(v, j=j, hh=hh):
                sTE.wait(v, snap[("wexpmm", j)])
                sDV.inc(v.tensor_tensor(sxb[:, j * 128:(j + 1) * 128],
                                        axs[:, j * 128:(j + 1) * 128],
                                        psm.ap()[:, 256:384], ALU.mult))
                snap[("sx", j)] = sDV.n
            blk.vector(_)

        emit_ffn_group(0, NT - 1)

        def _(g):
            sTE.wait(g, snap[("fmm", 0, NT - 1)])
            stW.inc(g.dma_start(out=wtch[:, 0, :, :], in_=wt_d[2]), 16)
            snap[("wt", 2)] = stW.n
        blk.gpsimd(_)

        if DEBUG:
            def _(sy):
                sDV.wait(sy, snap[("lgev", 1)])
                dbg_dump(sy, "d_lg2", lgsb[0:8, :])
                sDV.wait(sy, snap[("sx", NT - 1)])
                dbg_dump(sy, "d_sx", sxb.bitcast(F32))
                dbg_dump(sy, "d_ax", axs)
                dbg_dump(sy, "d_wf", wfmb.bitcast(F32))
            blk.sync(_)

        def emit_p6_A(j):
            frow = C3[:, j, :]
            dmp = dump6[:, 0, :]

            def _(v, j=j):
                sSC.wait(v, snap[("fevac", FF_NC - 1, j)])
            blk.vector(_)

            def _(s, j=j):
                sSC.wait(s, snap[("fevac", FF_NC - 1, j)])
                if j == 0:
                    sTE.wait(s, snap[("womm", WO_NC - 1, NT - 1)])
            SCD(_)

            tag4 = "ln4_%d" % j

            def _(v, j=j):
                sSC.wait(v, snap[("fevac", FF_NC - 1, j)])
                sDV.inc(v.reduce_sum(stc(SC_S, j),
                                     fsum.ap()[:, j * 4:(j + 1) * 4], AX))
                snap[(tag4, "s", j)] = ("dv", sDV.n)
            blk.vector(_)

            def _(s, j=j, frow=frow, dmp=dmp):
                s.activation(dmp, frow, AF.Square,
                             accum_out=stc(SC_Q, j)).then_inc(sSC.h, 1)
            SCD(_)
            snap[(tag4, "q", j)] = ("sc", sSC.prebump())

            vgate6 = snap[("out", j - 2)] if j >= 2 else snap[("h3b", NT - 1)]
            vsp = snap[("sp", j)]

            def _(s, j=j, vgate6=vgate6, vsp=vsp):
                sDV.wait(s, vgate6)
                stSP.wait(s, vsp)
                s.dma_start(
                    out=D_own.ap()[:, (j % 2) * 2048:(j % 2) * 2048 + 2048],
                    in_=x2sp[j * 128:(j + 1) * 128, :]).then_inc(stXR.h, 16)
            SCD(_)
            snap[("x2r", j)] = stXR.prebump(16)

        def emit_p6_B(j):
            frow = C3[:, j, :]
            outstg = dump6[:, 1 + (j % 2), :]
            x2slot = D_own.ap()[:, (j % 2) * 2048:(j % 2) * 2048 + 2048]

            ln_stats_B(j, "ln4_%d" % j)

            if DEBUG and j == NT - 1:
                def _(sy, j=j, frow=frow):
                    sSC.wait(sy, snap[("ln4_%d" % j, "r", j)])
                    dbg_dump(sy, "d_f7", frow)
                    dbg_dump(sy, "d_st47", stats.ap())
                    dbg_dump(sy, "d_fsum", fsum.ap())
                blk.sync(_)

            def _(v, j=j, frow=frow, outstg=outstg, x2slot=x2slot):
                sSC.wait(v, snap[("ln4_%d" % j, "r", j)])
                if DEBUG and j == NT - 1:
                    stDBG.wait(v)
                sDV.inc(v.tensor_scalar(frow, frow, stc(SC_M, j), stc(SC_R, j),
                                        ALU.subtract, ALU.mult))
                snap[("h4", j)] = sDV.n
                stXR.wait(v, snap[("x2r", j)])
                if j < 2:
                    sTE.wait(v, snap[("womm", WO_NC - 1, NT - 1)])
                else:
                    stOUT.wait(v, snap[("outd", j - 2)])
                sDV.wait(v)
                sDV.inc(v.tensor_tensor(outstg, frow, x2slot, ALU.add))
                snap[("out", j)] = sDV.n
            blk.vector(_)

            def _(sy, j=j, outstg=outstg, x2slot=x2slot):
                sDV.wait(sy, snap[("out", j)])
                if DEBUG and j == NT - 1:
                    dbg_dump(sy, "d_xr7", x2slot)
                stOUT.inc(sy.dma_start(out=y_d[j * 128:(j + 1) * 128, :],
                                       in_=outstg), 16)
                snap[("outd", j)] = stOUT.n
            blk.sync(_)


        for c in range(1, FF_NC):
            for j in range(NT):
                emit_ffn_group(c, j)
                if c == FF_NC - 1:
                    emit_p6_A(j)
                    if j >= 1:
                        emit_p6_B(j - 1)
            if c + 2 < FF_NC:
                def _(g, c=c):
                    sTE.wait(g, snap[("fmm", c, NT - 1)])
                    stW.inc(g.dma_start(out=wtch[:, c % 2, :, :],
                                        in_=wt_d[c + 2]), 16)
                    snap[("wt", c + 2)] = stW.n
                blk.gpsimd(_)

        emit_p6_B(NT - 1)

        def _(s):
            for fn in sc_defer:
                fn(s)
        blk.scalar(_)

        def _(sy):
            stOUT.wait(sy)
            if DEBUG:
                stDBG.wait(sy)
        blk.sync(_)

    return nc


def _host_layout(inputs):
    f = np.ascontiguousarray

    def wqkv(W):
        return f(W.T.reshape(NO, 128, NO, 128).transpose(2, 1, 0, 3))

    def wchunk(W, nch, co):
        return f(W.T.reshape(NO, 128, nch, co).transpose(2, 1, 0, 3))

    A = np.asarray(inputs["L0_ffn_lora_A"], np.float32)
    Bl = np.asarray(inputs["L0_ffn_lora_B"], np.float32)
    rw = np.asarray(inputs["L0_ffn_router_w"], np.float32)
    ar_full = np.concatenate(
        [A.transpose(2, 0, 1).reshape(D, 128), rw.T], axis=1)
    out = {
        "wq": wqkv(np.asarray(inputs["L0_attn_Wq"], np.float32)),
        "wk": wqkv(np.asarray(inputs["L0_attn_Wk"], np.float32)),
        "wv": wqkv(np.asarray(inputs["L0_attn_Wv"], np.float32)),
        "wo": wchunk(np.asarray(inputs["L0_attn_Wo"], np.float32), WO_NC, 256),
        "wt": wchunk(np.asarray(inputs["L0_ffn_weight"], np.float32), FF_NC, 512),
        "ar": f(ar_full.reshape(NO, 128, 136).transpose(1, 0, 2)),
        "bb": f(Bl.transpose(0, 2, 1).reshape(128, D)),
        "bq": f(np.asarray(inputs["L0_attn_bq"], np.float32).reshape(NO, 128).T),
        "bk": f(np.asarray(inputs["L0_attn_bk"], np.float32).reshape(NO, 128).T),
        "bv": f(np.asarray(inputs["L0_attn_bv"], np.float32).reshape(NO, 128).T),
        "bo": f(np.asarray(inputs["L0_attn_bo"], np.float32).reshape(1, D)),
        "idn": np.eye(128, dtype=np.float32),
        "epsc": np.full((128, 1), EPS, np.float32),
        "ones1": np.ones((1, 128), np.float32),
    }
    e2 = np.zeros((8, 128), np.float32)
    for e in range(8):
        e2[e, e * 16:(e + 1) * 16] = 2.0
    out["e2"] = e2
    return out


def kernel(**inputs):
    global _dbg_out
    x = np.asarray(inputs["L0_attn_x"], np.float32)
    Bsz, L, _ = x.shape
    xf = np.ascontiguousarray(x.reshape(Bsz * L, D))
    shared = _host_layout(inputs)

    nc = build_program()

    in_maps = []
    for c in range(8):
        m = dict(shared)
        m["x"] = np.ascontiguousarray(xf[c * T:(c + 1) * T])
        mask = np.zeros((8, 1), np.float32)
        s, p = c // 4, c % 4
        for r in range(8):
            if r // 4 == s and r % 4 < p:
                mask[r, 0] = 1.0
        m["maskc"] = mask
        in_maps.append(m)

    res = run_bass_kernel_spmd(nc, in_maps, core_ids=list(range(8)))
    if DEBUG:
        _dbg_out = res.results
    y = np.concatenate([res.results[c]["y"] for c in range(8)], axis=0)
    return y.reshape(Bsz, L, D)


# version 11 (final)
# speedup vs baseline: 1.0673x; 1.0076x over previous
import math
import numpy as np
from contextlib import ExitStack

import concourse.bass as bass
import concourse.mybir as mybir
from concourse.bass_utils import run_bass_kernel_spmd

F32 = mybir.dt.float32
BF16 = mybir.dt.bfloat16
ALU = mybir.AluOpType
AF = mybir.ActivationFunctionType
AX = mybir.AxisListType.X

T = 1024
D = 2048
NT = 8
NO = 16
NG = 4
WO_NC = 4
FF_NC = 4
EPS = 1e-5
SCALE = (D // 16) ** -0.5
LNS = math.log(SCALE)
BIG = 1.0e30

DEBUG = False
_dbg_out = {}


class S:

    def __init__(self, nc, ctx, name):
        self.h = ctx.enter_context(nc.semaphore(name))
        self.n = 0

    def inc(self, inst, k=1):
        inst.then_inc(self.h, k)
        self.n += k
        return self.n

    def wait(self, eng, val=None):
        v = self.n if val is None else val
        if v > 0:
            eng.wait_ge(self.h, v)

    def prebump(self, k=1):
        self.n += k
        return self.n


def build_program():
    nc = bass.Bass("TRN2", target_bir_lowering=False, debug=False)

    x_d = nc.dram_tensor("x", [T, D], F32, kind="ExternalInput").ap()
    wq_d = nc.dram_tensor("wq", [NO, 128, NO, 128], F32, kind="ExternalInput").ap()
    wk_d = nc.dram_tensor("wk", [NO, 128, NO, 128], F32, kind="ExternalInput").ap()
    wv_d = nc.dram_tensor("wv", [NO, 128, NO, 128], F32, kind="ExternalInput").ap()
    wo_d = nc.dram_tensor("wo", [WO_NC, 128, NO, 512], F32, kind="ExternalInput").ap()
    wt_d = nc.dram_tensor("wt", [FF_NC, 128, NO, 512], F32, kind="ExternalInput").ap()
    ar_d = nc.dram_tensor("ar", [128, NO, 136], F32, kind="ExternalInput").ap()
    bb_d = nc.dram_tensor("bb", [128, D], F32, kind="ExternalInput").ap()
    bq_d = nc.dram_tensor("bq", [128, NO], F32, kind="ExternalInput").ap()
    bk_d = nc.dram_tensor("bk", [128, NO], F32, kind="ExternalInput").ap()
    bv_d = nc.dram_tensor("bv", [128, NO], F32, kind="ExternalInput").ap()
    bo_d = nc.dram_tensor("bo", [1, D], F32, kind="ExternalInput").ap()
    idn_d = nc.dram_tensor("idn", [128, 128], F32, kind="ExternalInput").ap()
    e2_d = nc.dram_tensor("e2", [8, 128], F32, kind="ExternalInput").ap()
    ones_d = nc.dram_tensor("ones1", [1, 128], F32, kind="ExternalInput").ap()
    mask_d = nc.dram_tensor("maskc", [8, 1], F32, kind="ExternalInput").ap()
    eps_d = nc.dram_tensor("epsc", [128, 1], F32, kind="ExternalInput").ap()
    y_d = nc.dram_tensor("y", [T, D], F32, kind="ExternalOutput").ap()

    x2sp = nc.dram_tensor("x2sp", [T, D], F32, kind="Internal").ap()
    cin = [nc.dram_tensor(f"cin{g}", [1, 512], F32, kind="Internal").ap()
           for g in range(NG)]
    cout = [nc.dram_tensor(f"cout{g}", [8, 512], F32, kind="Internal",
                           addr_space="Shared").ap()
            for g in range(NG)]

    dbg = {}
    if DEBUG:
        for nm, shp in [
            ("d_h1t", [128, NO * T // 2]), ("d_phi", [128, T]),
            ("d_ukv", [128, T]), ("d_ctx", [128, NO * T // 2]),
            ("d_a", [128, D]), ("d_x2", [128, D]),
            ("d_lg", [128, 8]), ("d_w", [128, 8]),
            ("d_sx", [128, T // 2]), ("d_f", [128, D]),
            ("d_pref", [128, NO]), ("d_tot", [128, NO]),
            ("d_h3t", [128, NO * T // 2]), ("d_ax", [128, T]),
            ("d_x27", [128, D]), ("d_h3b7", [128, D // 2]),
            ("d_a7", [128, D]), ("d_st7", [128, 56]), ("d_asum", [128, 64]),
            ("d_f7", [128, D]), ("d_st47", [128, 56]), ("d_fsum", [128, 32]),
            ("d_xr7", [128, D]), ("d_wo3", [128, D]),
            ("d_wf", [8, T // 2]), ("d_lg2", [8, T]),
        ]:
            dbg[nm] = nc.dram_tensor(nm, shp, F32, kind="ExternalOutput").ap()

    ctx_mgr = ExitStack()
    with ctx_mgr as ctx:
        def sb(name, shape, dt=F32):
            return ctx.enter_context(nc.sbuf_tensor(name, shape, dt))

        def psum(name, shape, dt=F32):
            return ctx.enter_context(nc.psum_tensor(name, shape, dt))

        A_own = sb("A_own", [128, 8192])
        B_own = sb("B_own", [128, 8192])
        C_own = sb("C_own", [128, 16384])
        D_own = sb("D_own", [128, 6144])
        WOX = sb("WOX", [128, 6144])
        E_own = sb("E_own", [128, 2048])
        STG = sb("STG", [128, 2048])
        FB = sb("FB", [128, 2112])
        stats = sb("stats", [128, 56])
        asum = sb("asum", [128, 64])
        fsum = sb("fsum", [128, 32])
        xs1 = sb("xs1", [128, 8])
        bqc = sb("bqc", [128, NO]); bkc = sb("bkc", [128, NO]); bvc = sb("bvc", [128, NO])
        bqLn = sb("bqLn", [128, NO]); bqS2 = sb("bqS2", [128, NO])
        idn = sb("idn_s", [128, 128])
        e2f = sb("e2f", [8, 64])
        onesf = sb("onesf", [1, 64])
        bof = sb("bof", [1, 1024])
        maskc = sb("maskc_s", [8, 1])
        epsc = sb("epsc_s", [128, 1])
        tot0 = sb("tot0", [128, NO]); tot1 = sb("tot1", [128, NO])
        totS = sb("totS", [128, NO])
        pref = sb("pref", [128, NO])
        ltm = sb("ltm", [128, 8])
        r8a = sb("r8a", [128, 8]); r8b = sb("r8b", [128, 8]); r8c = sb("r8c", [128, 8])
        rc = sb("rc", [128, 8])
        wtm = sb("wtm", [128, 8])

        gsb = FB.ap()[0:8, 0:512]
        lgsb = E_own.ap()[0:8, 0:1024]
        wfm = E_own.ap()[0:8, 1024:1536]
        h1T = A_own.ap().bitcast(BF16).rearrange("p (n t) -> p n t", n=NO)
        ctxT = B_own.ap().bitcast(BF16).rearrange("p (n t) -> p n t", n=NO)
        wtch = B_own.ap().bitcast(BF16).rearrange("p (c n f) -> p c n f", c=2, n=NO)
        Cap = C_own.ap()
        C3 = Cap.rearrange("p (j f) -> p j f", j=NT)
        qkvw = D_own.ap().bitcast(BF16).rearrange("p (s f) -> p s f", s=6)
        woch = [WOX.ap()[:, 0:4096].bitcast(BF16).rearrange("p (n f) -> p n f", n=NO),
                D_own.ap()[:, 0:4096].bitcast(BF16).rearrange("p (n f) -> p n f", n=NO)]
        dump6 = WOX.ap().rearrange("p (c f) -> p c f", c=3)
        dmp23 = WOX.ap()[:, 4096:6144]
        h1b = FB.ap()[:, 0:1024].bitcast(BF16)
        bbv = FB.ap()[:, 0:1024].bitcast(BF16)
        arb = FB.ap()[:, 1024:2112].bitcast(BF16).rearrange(
            "p (n f) -> p n f", n=NO)
        e2b = e2f.ap().bitcast(BF16)
        onesb = onesf.ap().bitcast(BF16)
        bob = bof.ap().bitcast(BF16)
        axs = STG.ap()[:, 0:1024]
        sxb = STG.ap()[:, 1024:1536].bitcast(BF16)
        wfmb = wfm.bitcast(BF16)

        pb0 = psum("pb0", [128, 1024])
        pb1 = psum("pb1", [128, 1024])
        pvv = psum("pvv", [128, 1024])
        psm = psum("psm", [128, 512])

        stX = S(nc, ctx, "stX")
        stTP = S(nc, ctx, "stTP")
        stW = S(nc, ctx, "stW")
        stCIN = S(nc, ctx, "stCIN")
        stGSB = S(nc, ctx, "stGSB")
        stSP = S(nc, ctx, "stSP")
        stXR = S(nc, ctx, "stXR")
        stOUT = S(nc, ctx, "stOUT")
        stDBG = S(nc, ctx, "stDBG")
        sCC = S(nc, ctx, "sCC")
        sPL = S(nc, ctx, "sPL")
        sTE = S(nc, ctx, "sTE")
        sDV = S(nc, ctx, "sDV")
        sSC = S(nc, ctx, "sSC")

        blk = ctx.enter_context(nc.Block())
        snap = {}
        sc_defer = []

        def SCD(fn):
            sc_defer.append(fn)

        def dbg_dump(sy, name, src_ap):
            if DEBUG and name in dbg:
                stDBG.inc(sy.dma_start(out=dbg[name], in_=src_ap), 16)

        def _(g):
            for dst, src in [
                (bqc.ap(), bq_d), (bkc.ap(), bk_d), (bvc.ap(), bv_d),
                (idn.ap(), idn_d), (maskc.ap(), mask_d), (epsc.ap(), eps_d),
            ]:
                stW.inc(g.dma_start(out=dst, in_=src), 16)
            stW.inc(g.dma_start(out=e2b, in_=e2_d), 16)
            stW.inc(g.dma_start(out=onesb, in_=ones_d), 16)
            stW.inc(g.dma_start(out=bob, in_=bo_d), 16)
            stW.inc(g.dma_start(out=arb, in_=ar_d), 16)
            snap["consts"] = stW.n
        blk.gpsimd(_)

        def _(v):
            stW.wait(v, snap["consts"])
            sDV.inc(v.tensor_single_scalar(bqLn.ap(), bqc.ap(), LNS, ALU.add))
            sDV.inc(v.tensor_single_scalar(bqS2.ap(), bqc.ap(), SCALE, ALU.mult))
            snap["dbias"] = sDV.n
        blk.vector(_)

        SC_S, SC_Q, SC_M, SC_M2, SC_VAR, SC_STD, SC_R = 0, 8, 16, 24, 32, 40, 48

        def stc(base, j):
            return stats.ap()[:, base + j:base + j + 1]

        def ln_stats_A(j, src_ap, dump_ap, tag, sum_on_scalar=False):
            if sum_on_scalar:
                def _(s):
                    s.activation(dump_ap, src_ap, AF.Copy,
                                 accum_out=stc(SC_S, j)).then_inc(sSC.h, 1)
                SCD(_)
                snap[(tag, "s", j)] = ("sc", sSC.prebump())
            else:
                def _(v):
                    sDV.inc(v.reduce_sum(stc(SC_S, j), src_ap, AX))
                    snap[(tag, "s", j)] = ("dv", sDV.n)
                blk.vector(_)

            def _(s):
                s.activation(dump_ap, src_ap, AF.Square,
                             accum_out=stc(SC_Q, j)).then_inc(sSC.h, 1)
            SCD(_)
            snap[(tag, "q", j)] = ("sc", sSC.prebump())

        def ln_stats_B(j, tag):
            kind, sval = snap[(tag, "s", j)]
            qk, qval = snap[(tag, "q", j)]

            def _(v):
                if qk == "sc":
                    sSC.wait(v, qval)
                else:
                    sDV.wait(v, qval)
                if kind == "dv":
                    sDV.wait(v, sval)
                sDV.inc(v.tensor_single_scalar(stc(SC_M, j), stc(SC_S, j),
                                               1.0 / D, ALU.mult))
                sDV.wait(v)
                sDV.inc(v.tensor_tensor(stc(SC_M2, j), stc(SC_M, j),
                                        stc(SC_M, j), ALU.mult))
                sDV.wait(v)
                sDV.inc(v.scalar_tensor_tensor(stc(SC_VAR, j), stc(SC_Q, j),
                                               1.0 / D, stc(SC_M2, j),
                                               ALU.mult, ALU.subtract))
                snap[(tag, "var", j)] = sDV.n
            blk.vector(_)

            vvar = snap[(tag, "var", j)]
            vln = sSC.prebump()
            snap[(tag, "r", j)] = sSC.prebump()

            def _(s, vvar=vvar, vln=vln):
                sDV.wait(s, vvar)
                s.activation(stc(SC_STD, j), stc(SC_VAR, j),
                             AF.Ln, bias=epsc.ap()).then_inc(sSC.h, 1)
                s.wait_ge(sSC.h, vln)
                s.activation(stc(SC_R, j), stc(SC_STD, j),
                             AF.Exp, scale=-0.5).then_inc(sSC.h, 1)
            SCD(_)

        def p1_xslot(j):
            return E_own.ap() if j % 2 == 0 else STG.ap()

        def emit_ln1_B(j):
            xslot = p1_xslot(j)
            ln_stats_B(j, "ln1")

            def _(v, j=j):
                sDV.inc(v.tensor_copy(xs1.ap()[:, j:j + 1], stc(SC_S, j)))
            blk.vector(_)

            def _(v, j=j, xslot=xslot):
                sSC.wait(v, snap[("ln1", "r", j)])
                if j >= 1:
                    stTP.wait(v, snap[("tp1", j - 1)])
                sDV.inc(v.tensor_scalar(h1b, xslot, stc(SC_M, j), stc(SC_R, j),
                                        ALU.subtract, ALU.mult))
                snap[("ln1h1b", j)] = sDV.n
            blk.vector(_)

            def _(sy, j=j):
                sDV.wait(sy, snap[("ln1h1b", j)])
                src = h1b.rearrange("p (n t) -> p n t", n=NO)
                dst = h1T[:, :, j * 128:(j + 1) * 128]
                stTP.inc(sy.dma_start(out=dst, in_=src, transpose=True), 16)
                snap[("tp1", j)] = stTP.n
            blk.sync(_)

        for j in range(NT):
            xslot = p1_xslot(j)

            vgate = snap[("ln1h1b", j - 2)] if j >= 2 else 0

            def _(s, j=j, xslot=xslot, vgate=vgate):
                if vgate:
                    sDV.wait(s, vgate)
                s.dma_start(out=xslot,
                            in_=x_d[j * 128:(j + 1) * 128, :]).then_inc(stX.h, 16)
            SCD(_)
            snap[("xload", j)] = stX.prebump(16)

            def _(v, j=j):
                stX.wait(v, snap[("xload", j)])
            blk.vector(_)

            vx = snap[("xload", j)]

            def _(s, vx=vx):
                stX.wait(s, vx)
            SCD(_)

            ln_stats_A(j, xslot, Cap[:, (j % 2) * 2048:(j % 2) * 2048 + 2048], "ln1")
            if j >= 1:
                emit_ln1_B(j - 1)
        emit_ln1_B(NT - 1)

        slot_last_te = [0] * 6

        def load_qkv(g, i):
            for wi, wsrc in enumerate((wq_d, wk_d, wv_d)):
                s6 = (3 * i + wi) % 6
                sTE.wait(g, slot_last_te[s6])
                stW.inc(g.dma_start(out=qkvw[:, s6, :],
                                    in_=wsrc[i].rearrange("p n f -> p (n f)")), 16)
            snap[("wq", i)] = stW.n

        def emit_cc(g, grp):
            stCIN.wait(g, snap[("cin", grp)])
            sCC.inc(g.collective_compute(
                "AllGather", ALU.bypass, replica_groups=[list(range(8))],
                ins=[cin[grp].opt()], outs=[cout[grp].opt()]))
            snap[("cc", grp)] = sCC.n

        def emit_prefix(grp2, prevg):
            def _(sy):
                sCC.wait(sy, snap[("cc", grp2)])
                if prevg is not None:
                    sTE.wait(sy, snap[("prefmm", prevg)])
                stGSB.inc(sy.dma_start(out=gsb, in_=cout[grp2]), 16)
                snap[("gsb", grp2)] = stGSB.n
            blk.sync(_)

            def _(t):
                stGSB.wait(t, snap[("gsb", grp2)])
                if prevg is not None:
                    sDV.wait(t, snap[("prefevac", prevg)])
                gv = gsb.rearrange("r (c g) -> r c g", g=4)
                inst = None
                for q in range(4):
                    inst = t.matmul(psm.ap()[:, q:q + 1], gv[:, :, q],
                                    maskc.ap(), start=True, stop=True)
                sTE.inc(inst)
                snap[("prefmm", grp2)] = sTE.n
            blk.tensor(_)

            def _(v):
                sTE.wait(v, snap[("prefmm", grp2)])
                sDV.inc(v.tensor_copy(pref.ap()[:, grp2 * 4:grp2 * 4 + 4],
                                      psm.ap()[:, 0:4]))
                snap[("prefevac", grp2)] = sDV.n
            blk.vector(_)

        def emit_scan_ctx(i2):
            grp2 = i2 // 4

            def _(v):
                sDV.wait(v, snap[("prefevac", grp2)])
                slot = i2 % 8
                ub = Cap[:, 8192 + slot * 1024: 8192 + (slot + 1) * 1024]
                sDV.inc(v.tensor_tensor_scan(ub, ub, ub,
                                             pref.ap()[:, i2:i2 + 1],
                                             ALU.add, ALU.bypass))
                sDV.wait(v)
                sDV.inc(v.tensor_tensor(ctxT[:, i2, :], ub,
                                        Cap[:, slot * 1024:(slot + 1) * 1024],
                                        ALU.mult))
                snap[("ctx", i2)] = sDV.n
            blk.vector(_)

        ORDER = [12, 13, 14, 15] + list(range(12))

        def _(g):
            stX.wait(g, snap[("xload", 2)])
            load_qkv(g, ORDER[0])
            stX.wait(g, snap[("xload", 4)])
            load_qkv(g, ORDER[1])
        blk.gpsimd(_)

        for p in range(NO):
            i = ORDER[p]
            iprev = ORDER[p - 1] if p >= 1 else None
            for h in range(2):
                tsl = slice(h * 512, (h + 1) * 512)

                def _(t, i=i, h=h, tsl=tsl, p=p, iprev=iprev):
                    stW.wait(t, snap[("wq", i)])
                    if p == 0 and h == 0:
                        sDV.wait(t, snap["dbias"])
                    if iprev is not None:
                        sSC.wait(t, snap[("kacts", iprev, h)])
                        sDV.wait(t, snap[("u", iprev, h)])
                    for wi, pst in ((0, pb0), (1, pb1), (2, pvv)):
                        s6 = (3 * i + wi) % 6
                        inst = None
                        if p == 0:
                            for d in range(NO):
                                for j4 in range(4):
                                    jj = h * 4 + j4
                                    if wi == 0 and d == 0:
                                        stTP.wait(t, snap[("tp1", jj)])
                                    inst = t.matmul(
                                        pst.ap()[:, h * 512 + j4 * 128:
                                                 h * 512 + (j4 + 1) * 128],
                                        qkvw[:, s6, d * 128:(d + 1) * 128],
                                        h1T[:, d, jj * 128:(jj + 1) * 128],
                                        start=(d == 0 and j4 == 0),
                                        stop=(d == NO - 1),
                                        skip_group_check=True)
                        else:
                            for d in range(NO):
                                inst = t.matmul(
                                    pst.ap()[:, tsl],
                                    qkvw[:, s6, d * 128:(d + 1) * 128],
                                    h1T[:, d, tsl],
                                    start=(d == 0), stop=(d == NO - 1))
                        sTE.inc(inst)
                        snap[("mm", wi, i, h)] = sTE.n
                    for wi in range(3):
                        slot_last_te[(3 * i + wi) % 6] = sTE.n
                blk.tensor(_)

                vu_prev = snap[("u", iprev, h)] if iprev is not None else 0

                def _(s, i=i, h=h, tsl=tsl, vu_prev=vu_prev, p=p):
                    if vu_prev:
                        sDV.wait(s, vu_prev)
                    if p == 0 and h == 0:
                        sDV.wait(s, snap[("ln1h1b", NT - 1)])
                    sTE.wait(s, snap[("mm", 0, i, h)])
                    s.activation(STG.ap()[:, 0:512], pb0.ap()[:, tsl],
                                 AF.Exp, bias=bqLn.ap()[:, i:i + 1]).then_inc(sSC.h, 1)
                    s.activation(STG.ap()[:, 512:1024], pb0.ap()[:, tsl],
                                 AF.Relu, bias=bqS2.ap()[:, i:i + 1],
                                 scale=SCALE).then_inc(sSC.h, 1)
                    sTE.wait(s, snap[("mm", 1, i, h)])
                    s.activation(STG.ap()[:, 1024:1536], pb1.ap()[:, tsl],
                                 AF.Exp, bias=bkc.ap()[:, i:i + 1]).then_inc(sSC.h, 1)
                    s.activation(STG.ap()[:, 1536:2048], pb1.ap()[:, tsl],
                                 AF.Relu, bias=bkc.ap()[:, i:i + 1]).then_inc(sSC.h, 1)
                SCD(_)
                sSC.prebump(2)
                snap[("qacts", i, h)] = sSC.n
                sSC.prebump(2)
                snap[("kacts", i, h)] = sSC.n

                def _(v, i=i, h=h, tsl=tsl):
                    slot = i % 8
                    sSC.wait(v, snap[("qacts", i, h)])
                    sDV.inc(v.scalar_tensor_tensor(
                        Cap[:, slot * 1024 + h * 512: slot * 1024 + (h + 1) * 512],
                        STG.ap()[:, 0:512], SCALE, STG.ap()[:, 512:1024],
                        ALU.min, ALU.add))
                    sSC.wait(v, snap[("kacts", i, h)])
                    sDV.inc(v.scalar_tensor_tensor(
                        STG.ap()[:, 1024:1536],
                        STG.ap()[:, 1024:1536], 1.0, STG.ap()[:, 1536:2048],
                        ALU.min, ALU.add))
                    sDV.wait(v)
                    sTE.wait(v, snap[("mm", 2, i, h)])
                    tcol = tot0 if h == 0 else tot1
                    sDV.inc(v.scalar_tensor_tensor(
                        Cap[:, 8192 + slot * 1024 + h * 512:
                            8192 + slot * 1024 + (h + 1) * 512],
                        pvv.ap()[:, tsl], bvc.ap()[:, i:i + 1],
                        STG.ap()[:, 1024:1536],
                        ALU.add, ALU.mult,
                        accum_out=tcol.ap()[:, i:i + 1]))
                    snap[("u", i, h)] = sDV.n
                blk.vector(_)

            grp = i // 4
            if i % 4 == 3:
                g4 = slice(grp * 4, grp * 4 + 4)

                def _(v, grp=grp, g4=g4):
                    sDV.wait(v)
                    sDV.inc(v.tensor_tensor(totS.ap()[:, g4], tot0.ap()[:, g4],
                                            tot1.ap()[:, g4], ALU.add))
                    snap[("tot", grp)] = sDV.n
                blk.vector(_)

                def _(sy, grp=grp, g4=g4):
                    sDV.wait(sy, snap[("tot", grp)])
                    dst = cin[grp].rearrange("a (c g) -> a c g", g=4)[0]
                    stCIN.inc(sy.dma_start(out=dst, in_=totS.ap()[:, g4]), 16)
                    snap[("cin", grp)] = stCIN.n
                blk.sync(_)

            def _(g, i=i, p=p):
                if p + 2 < NO:
                    load_qkv(g, ORDER[p + 2])
                if p == 8:
                    stW.inc(g.dma_start(out=woch[0][:, :, :], in_=wo_d[0]), 16)
                    snap[("wo", 0)] = stW.n
                if p == NO - 1:
                    sTE.wait(g, snap[("mm", 2, ORDER[NO - 1], 1)])
                    stW.inc(g.dma_start(out=woch[1][:, :, :], in_=wo_d[1]), 16)
                    snap[("wo", 1)] = stW.n
                if i % 4 == 3:
                    emit_cc(g, i // 4)
            blk.gpsimd(_)

            _tails = {5: [("p", 3, None), ("s", 12)],
                      6: [("s", 13), ("s", 14)], 7: [("s", 15)],
                      9: [("p", 0, 3), ("s", 0)],
                      10: [("s", 1), ("s", 2)], 11: [("s", 3)],
                      13: [("p", 1, 0), ("s", 4)],
                      14: [("s", 5), ("s", 6)], 15: [("s", 7)]}
            for task in _tails.get(p, []):
                if task[0] == "p":
                    emit_prefix(task[1], task[2])
                else:
                    emit_scan_ctx(task[1])

        emit_prefix(2, 1)
        for i2 in range(8, 12):
            emit_scan_ctx(i2)

        if DEBUG:
            def _(sy):
                sDV.wait(sy, snap[("ctx", 11)])
                dbg_dump(sy, "d_tot", totS.ap())
                dbg_dump(sy, "d_pref", pref.ap())
                dbg_dump(sy, "d_phi", Cap[:, 7 * 1024:8 * 1024])
                dbg_dump(sy, "d_ukv", Cap[:, 8192 + 7 * 1024: 8192 + 8 * 1024])
                dbg_dump(sy, "d_h1t", A_own.ap())
                dbg_dump(sy, "d_ctx", B_own.ap())
            blk.sync(_)

        def emit_ln23_A(j):
            xslot = E_own.ap() if j % 2 == 0 else STG.ap()
            arow = C3[:, j, :]
            dmp2 = dmp23

            if j >= 2:
                vgate2 = ("dv", snap[("h3b", j - 2)])
            elif j == 1:
                vgate2 = ("dv", snap[("u", NO - 1, 1)])
            else:
                vgate2 = None

            def _(s, j=j, xslot=xslot, vgate2=vgate2):
                if vgate2 is not None:
                    sDV.wait(s, vgate2[1])
                s.dma_start(out=xslot,
                            in_=x_d[j * 128:(j + 1) * 128, :]).then_inc(stX.h, 16)
            SCD(_)
            snap[("xload2", j)] = stX.prebump(16)

            def _(v, j=j):
                sSC.wait(v, snap[("aevac", WO_NC - 1, j)])
                sDV.inc(v.reduce_sum(stc(SC_S, j),
                                     asum.ap()[:, j * 8:j * 8 + WO_NC], AX))
                snap[("ln2_%d" % j, "s", j)] = ("dv", sDV.n)
            blk.vector(_)

            def _(s, j=j):
                sSC.wait(s, snap[("aevac", WO_NC - 1, j)])
            SCD(_)

            tag2 = "ln2_%d" % j

            def _(s, j=j, tag2=tag2):
                s.activation(dmp2, arow, AF.Square,
                             accum_out=stc(SC_Q, j)).then_inc(sSC.h, 1)
            SCD(_)
            snap[(tag2, "q", j)] = ("sc", sSC.prebump())

        def emit_ln23_B(j):
            xslot = E_own.ap() if j % 2 == 0 else STG.ap()
            arow = C3[:, j, :]
            dmp2 = dmp23

            ln_stats_B(j, "ln2_%d" % j)

            if DEBUG and j == NT - 1:
                def _(sy, j=j, arow=arow):
                    sSC.wait(sy, snap[("ln2_%d" % j, "r", j)])
                    dbg_dump(sy, "d_a7", arow)
                    dbg_dump(sy, "d_st7", stats.ap())
                    dbg_dump(sy, "d_asum", asum.ap())
                    dbg_dump(sy, "d_wo3", D_own.ap()[:, 0:2048])
                blk.sync(_)

            def _(v, j=j, xslot=xslot, arow=arow):
                sSC.wait(v, snap[("ln2_%d" % j, "r", j)])
                if DEBUG and j == NT - 1:
                    stDBG.wait(v)
                sDV.inc(v.tensor_scalar(arow, arow, stc(SC_M, j), stc(SC_R, j),
                                        ALU.subtract, ALU.mult))
                stX.wait(v, snap[("xload2", j)])
                sDV.wait(v)
                sDV.inc(v.tensor_tensor(xslot, arow, xslot, ALU.add))
                snap[("x2", j)] = sDV.n
            blk.vector(_)

            vx2 = snap[("x2", j)]

            def _(s, j=j, xslot=xslot, vx2=vx2):
                sDV.wait(s, vx2)
                s.dma_start(out=x2sp[j * 128:(j + 1) * 128, :],
                            in_=xslot).then_inc(stSP.h, 16)
            SCD(_)
            snap[("sp", j)] = stSP.prebump(16)

            if DEBUG and (j == 0 or j == NT - 1):
                def _(sy, j=j, xslot=xslot):
                    sDV.wait(sy, snap[("x2", j)])
                    if j == 0:
                        dbg_dump(sy, "d_x2", xslot)
                    else:
                        dbg_dump(sy, "d_x27", xslot)
                blk.sync(_)

            tag3 = "ln3_%d" % j

            def _(v, j=j, xslot=xslot, vx2=vx2):
                sDV.inc(v.tensor_copy(stc(SC_S, j), xs1.ap()[:, j:j + 1]))
                snap[(tag3, "s", j)] = ("dv", sDV.n)
            blk.vector(_)

            def _(s, j=j, xslot=xslot, vx2=vx2):
                sDV.wait(s, vx2)
                s.activation(dmp2, xslot, AF.Square,
                             accum_out=stc(SC_Q, j)).then_inc(sSC.h, 1)
            SCD(_)
            snap[(tag3, "q", j)] = ("sc", sSC.prebump())

            ln_stats_B(j, "ln3_%d" % j)

            def _(v, j=j, xslot=xslot):
                sSC.wait(v, snap[("ln3_%d" % j, "r", j)])
                if j >= 1:
                    stTP.wait(v, snap[("tp3", j - 1)])
                else:
                    stTP.wait(v, snap[("tp1", NT - 1)])
                sDV.inc(v.tensor_scalar(h1b, xslot, stc(SC_M, j), stc(SC_R, j),
                                        ALU.subtract, ALU.mult))
                snap[("h3b", j)] = sDV.n
            blk.vector(_)

            def _(sy, j=j):
                sDV.wait(sy, snap[("h3b", j)])
                src = h1b.rearrange("p (n t) -> p n t", n=NO)
                dst = h1T[:, :, j * 128:(j + 1) * 128]
                stTP.inc(sy.dma_start(out=dst, in_=src, transpose=True), 16)
                snap[("tp3", j)] = stTP.n
            blk.sync(_)


        bank_prev = [None, None]

        def emit_wo_group(c, j):
            bank = (c + j) % 2
            aps = pb0.ap()[:, 0:512] if bank == 0 else pb0.ap()[:, 512:1024]
            osl = slice(c * 512, (c + 1) * 512)

            def _(t, c=c, j=j, aps=aps, osl=osl, bprev=bank_prev[bank]):
                stW.wait(t, snap[("wo", c)])
                if bprev is not None:
                    sSC.wait(t, bprev)
                inst = t.matmul(aps, onesb, bob[0:1, osl],
                                start=True, stop=False)
                cts = list(range(8)) + list(range(12, 16)) + list(range(8, 12))
                for k_, ct in enumerate(cts):
                    if c == 0 and j == 0:
                        sDV.wait(t, snap[("ctx", ct)])
                    inst = t.matmul(
                        aps, ctxT[:, ct, j * 128:(j + 1) * 128],
                        woch[c % 2][:, ct, :],
                        start=False, stop=(k_ == NO - 1))
                sTE.inc(inst)
                snap[("womm", c, j)] = sTE.n
            blk.tensor(_)

            def _(s, c=c, j=j, aps=aps, osl=osl):
                sTE.wait(s, snap[("womm", c, j)])
                s.activation(C3[:, j, osl], aps, AF.Copy,
                             accum_out=asum.ap()[:, j * 8 + c:j * 8 + c + 1]
                             ).then_inc(sSC.h, 1)
            SCD(_)
            snap[("aevac", c, j)] = sSC.prebump()
            bank_prev[bank] = snap[("aevac", c, j)]

        for c in range(WO_NC - 2):
            for j in range(NT):
                emit_wo_group(c, j)
            if c + 2 < WO_NC:
                def _(g, c=c):
                    sTE.wait(g, snap[("womm", c, NT - 1)])
                    stW.inc(g.dma_start(out=woch[(c + 2) % 2][:, :, :],
                                        in_=wo_d[c + 2]), 16)
                    snap[("wo", c + 2)] = stW.n
                blk.gpsimd(_)

        for j in range(NT):
            emit_wo_group(WO_NC - 2, j)
            emit_wo_group(WO_NC - 1, j)
            emit_ln23_A(j)
            if j >= 1:
                emit_ln23_B(j - 1)

        emit_ln23_B(NT - 1)

        def _(g):
            stTP.wait(g, snap[("tp3", NT - 1)])
            stW.inc(g.dma_start(out=bbv, in_=bb_d), 16)
            snap["bb"] = stW.n
            sTE.wait(g, snap[("womm", WO_NC - 1, NT - 1)])
            stW.inc(g.dma_start(out=wtch[:, 0, :, :], in_=wt_d[0]), 16)
            snap[("wt", 0)] = stW.n
            stW.inc(g.dma_start(out=wtch[:, 1, :, :], in_=wt_d[1]), 16)
            snap[("wt", 1)] = stW.n
        blk.gpsimd(_)

        if DEBUG:
            def _(sy):
                stTP.wait(sy, snap[("tp3", NT - 1)])
                dbg_dump(sy, "d_h3t", A_own.ap())
            blk.sync(_)

        for hh in range(2):
            pslice = pb1.ap()[:, hh * 512:(hh + 1) * 512]

            def _(t, hh=hh, pslice=pslice):
                stTP.wait(t, snap[("tp3", 3 if hh == 0 else NT - 1)])
                if hh == 0:
                    sSC.wait(t, snap[("kacts", NO - 1, 1)])
                inst = None
                for d in range(NO):
                    inst = t.matmul(pslice, arb[:, d, 0:128],
                                    h1T[:, d, hh * 512:(hh + 1) * 512],
                                    start=(d == 0), stop=(d == NO - 1))
                sTE.inc(inst)
                snap[("axmm", hh)] = sTE.n
            blk.tensor(_)

            def _(v, hh=hh, pslice=pslice):
                sTE.wait(v, snap[("axmm", hh)])
                if hh == 0:
                    stSP.wait(v, snap[("sp", NT - 1)])
                sDV.inc(v.tensor_copy(axs[:, hh * 512:(hh + 1) * 512], pslice))
                snap[("axev", hh)] = sDV.n
            blk.vector(_)

            def _(t, hh=hh, pslice=pslice):
                sDV.wait(t, snap[("axev", hh)])
                inst = None
                for d in range(NO):
                    inst = t.matmul(pslice[0:8, :], arb[:, d, 128:136],
                                    h1T[:, d, hh * 512:(hh + 1) * 512],
                                    start=(d == 0), stop=(d == NO - 1))
                sTE.inc(inst)
                snap[("lgmm", hh)] = sTE.n
            blk.tensor(_)

            def _(v, hh=hh, pslice=pslice):
                sTE.wait(v, snap[("lgmm", hh)])
                sDV.inc(v.tensor_copy(lgsb[0:8, hh * 512:(hh + 1) * 512],
                                      pslice[0:8, :]))
                snap[("lgev", hh)] = sDV.n
            blk.vector(_)

        ffn_bank_prev = [("dv", snap[("u", NO - 1, 1)]),
                         ("dv", snap[("u", NO - 1, 1)])]

        def emit_ffn_group(c, j):
            bank = j % 2
            fps = pvv.ap()[:, 0:512] if bank == 0 else pvv.ap()[:, 512:1024]
            osl = slice(c * 512, (c + 1) * 512)

            def _(t, c=c, j=j, fps=fps, osl=osl, bprev=ffn_bank_prev[bank]):
                stW.wait(t, snap[("wt", c)])
                if c == 0:
                    stW.wait(t, snap["bb"])
                    sDV.wait(t, snap[("sx", j)])
                kind, val = bprev
                if kind == "dv":
                    sDV.wait(t, val)
                else:
                    sSC.wait(t, val)
                for d in range(NO):
                    t.matmul(fps, h1T[:, d, j * 128:(j + 1) * 128],
                             wtch[:, c % 2, d, :],
                             start=(d == 0), stop=False)
                inst = t.matmul(fps, sxb[:, j * 128:(j + 1) * 128],
                                bbv[:, osl], start=False, stop=True)
                sTE.inc(inst)
                snap[("fmm", c, j)] = sTE.n
            blk.tensor(_)

            def _(s, c=c, j=j, fps=fps, osl=osl):
                sTE.wait(s, snap[("fmm", c, j)])
                s.activation(C3[:, j, osl], fps, AF.Copy,
                             accum_out=fsum.ap()[:, j * 4 + c:j * 4 + c + 1]
                             ).then_inc(sSC.h, 1)
            SCD(_)
            snap[("fevac", c, j)] = sSC.prebump()
            ffn_bank_prev[bank] = ("sc", snap[("fevac", c, j)])

        for j in range(NT):
            hh = j // 4

            def _(t, j=j, hh=hh):
                sDV.wait(t, snap[("lgev", hh)])
                if j > 0:
                    sDV.wait(t, snap[("sx", j - 1)])
                sTE.inc(t.transpose(psm.ap()[:, 0:8],
                                    lgsb[0:8, j * 128:(j + 1) * 128],
                                    idn.ap()[0:8, 0:8]))
                snap[("lgT", j)] = sTE.n
            blk.tensor(_)

            def _(v, j=j):
                sTE.wait(v, snap[("lgT", j)])
                sDV.inc(v.tensor_copy(ltm.ap(), psm.ap()[:, 0:8]))
                sDV.wait(v)
                sDV.inc(v.reduce_max(rc.ap()[:, 0:1], ltm.ap(), AX))
                sDV.wait(v)
                sDV.inc(v.tensor_single_scalar(r8a.ap(), ltm.ap(),
                                               rc.ap()[:, 0:1], ALU.is_ge))
                sDV.wait(v)
                sDV.inc(v.scalar_tensor_tensor(r8b.ap(), r8a.ap(), -BIG,
                                               ltm.ap(), ALU.mult, ALU.add))
                sDV.wait(v)
                sDV.inc(v.reduce_max(rc.ap()[:, 1:2], r8b.ap(), AX))
                sDV.wait(v)
                sDV.inc(v.tensor_single_scalar(rc.ap()[:, 2:3], rc.ap()[:, 0:1],
                                               -1.0, ALU.mult))
                sDV.wait(v)
                sDV.inc(v.tensor_tensor(rc.ap()[:, 3:4], rc.ap()[:, 1:2],
                                        rc.ap()[:, 0:1], ALU.subtract))
                snap[("rmath", j)] = sDV.n
            blk.vector(_)

            def _(s, j=j):
                sDV.wait(s, snap[("rmath", j)])
                s.activation(r8c.ap(), ltm.ap(), AF.Exp,
                             bias=rc.ap()[:, 2:3]).then_inc(sSC.h, 1)
                s.activation(rc.ap()[:, 4:5], rc.ap()[:, 3:4],
                             AF.Exp).then_inc(sSC.h, 1)
            SCD(_)
            snap[("rexp", j)] = sSC.prebump(2)

            def _(v, j=j):
                sSC.wait(v, snap[("rexp", j)])
                sDV.inc(v.tensor_single_scalar(r8a.ap(), ltm.ap(),
                                               rc.ap()[:, 1:2], ALU.is_ge))
                sDV.inc(v.tensor_single_scalar(rc.ap()[:, 5:6], rc.ap()[:, 4:5],
                                               1.0, ALU.add))
                sDV.wait(v)
                sDV.inc(v.reciprocal(rc.ap()[:, 6:7], rc.ap()[:, 5:6]))
                sDV.wait(v)
                sDV.inc(v.tensor_tensor(r8b.ap(), r8c.ap(), r8a.ap(), ALU.mult))
                sDV.wait(v)
                sDV.inc(v.tensor_single_scalar(wtm.ap(), r8b.ap(),
                                               rc.ap()[:, 6:7], ALU.mult))
                snap[("w", j)] = sDV.n
            blk.vector(_)

            if j >= 1:
                emit_ffn_group(0, j - 1)

            def _(t, j=j):
                sDV.wait(t, snap[("w", j)])
                sTE.inc(t.transpose(psm.ap()[0:8, 128:256], wtm.ap(), idn.ap()))
                snap[("wT", j)] = sTE.n
            blk.tensor(_)

            def _(v, j=j):
                sTE.wait(v, snap[("wT", j)])
                sDV.inc(v.tensor_copy(wfmb[0:8, j * 128:(j + 1) * 128],
                                      psm.ap()[0:8, 128:256]))
                snap[("wev", j)] = sDV.n
            blk.vector(_)

            def _(t, j=j):
                sDV.wait(t, snap[("wev", j)])
                sTE.inc(t.matmul(psm.ap()[:, 256:384], e2b,
                                 wfmb[0:8, j * 128:(j + 1) * 128],
                                 start=True, stop=True))
                snap[("wexpmm", j)] = sTE.n
            blk.tensor(_)

            def _(v, j=j, hh=hh):
                sTE.wait(v, snap[("wexpmm", j)])
                sDV.inc(v.tensor_tensor(sxb[:, j * 128:(j + 1) * 128],
                                        axs[:, j * 128:(j + 1) * 128],
                                        psm.ap()[:, 256:384], ALU.mult))
                snap[("sx", j)] = sDV.n
            blk.vector(_)

        emit_ffn_group(0, NT - 1)

        def _(g):
            sTE.wait(g, snap[("fmm", 0, NT - 1)])
            stW.inc(g.dma_start(out=wtch[:, 0, :, :], in_=wt_d[2]), 16)
            snap[("wt", 2)] = stW.n
        blk.gpsimd(_)

        if DEBUG:
            def _(sy):
                sDV.wait(sy, snap[("lgev", 1)])
                dbg_dump(sy, "d_lg2", lgsb[0:8, :])
                sDV.wait(sy, snap[("sx", NT - 1)])
                dbg_dump(sy, "d_sx", sxb.bitcast(F32))
                dbg_dump(sy, "d_ax", axs)
                dbg_dump(sy, "d_wf", wfmb.bitcast(F32))
            blk.sync(_)

        def emit_p6_A(j):
            frow = C3[:, j, :]
            dmp = dump6[:, 0, :]

            def _(v, j=j):
                sSC.wait(v, snap[("fevac", FF_NC - 1, j)])
            blk.vector(_)

            def _(s, j=j):
                sSC.wait(s, snap[("fevac", FF_NC - 1, j)])
                if j == 0:
                    sTE.wait(s, snap[("womm", WO_NC - 1, NT - 1)])
            SCD(_)

            tag4 = "ln4_%d" % j

            def _(v, j=j):
                sSC.wait(v, snap[("fevac", FF_NC - 1, j)])
                sDV.inc(v.reduce_sum(stc(SC_S, j),
                                     fsum.ap()[:, j * 4:(j + 1) * 4], AX))
                snap[(tag4, "s", j)] = ("dv", sDV.n)
            blk.vector(_)

            def _(s, j=j, frow=frow, dmp=dmp):
                s.activation(dmp, frow, AF.Square,
                             accum_out=stc(SC_Q, j)).then_inc(sSC.h, 1)
            SCD(_)
            snap[(tag4, "q", j)] = ("sc", sSC.prebump())

            vgate6 = snap[("out", j - 2)] if j >= 2 else snap[("h3b", NT - 1)]
            vsp = snap[("sp", j)]

            def _(s, j=j, vgate6=vgate6, vsp=vsp):
                sDV.wait(s, vgate6)
                stSP.wait(s, vsp)
                s.dma_start(
                    out=D_own.ap()[:, (j % 2) * 2048:(j % 2) * 2048 + 2048],
                    in_=x2sp[j * 128:(j + 1) * 128, :]).then_inc(stXR.h, 16)
            SCD(_)
            snap[("x2r", j)] = stXR.prebump(16)

        def emit_p6_B(j):
            frow = C3[:, j, :]
            outstg = dump6[:, 1 + (j % 2), :]
            x2slot = D_own.ap()[:, (j % 2) * 2048:(j % 2) * 2048 + 2048]

            ln_stats_B(j, "ln4_%d" % j)

            if DEBUG and j == NT - 1:
                def _(sy, j=j, frow=frow):
                    sSC.wait(sy, snap[("ln4_%d" % j, "r", j)])
                    dbg_dump(sy, "d_f7", frow)
                    dbg_dump(sy, "d_st47", stats.ap())
                    dbg_dump(sy, "d_fsum", fsum.ap())
                blk.sync(_)

            def _(v, j=j, frow=frow, outstg=outstg, x2slot=x2slot):
                sSC.wait(v, snap[("ln4_%d" % j, "r", j)])
                if DEBUG and j == NT - 1:
                    stDBG.wait(v)
                sDV.inc(v.tensor_scalar(frow, frow, stc(SC_M, j), stc(SC_R, j),
                                        ALU.subtract, ALU.mult))
                snap[("h4", j)] = sDV.n
                stXR.wait(v, snap[("x2r", j)])
                if j < 2:
                    sTE.wait(v, snap[("womm", WO_NC - 1, NT - 1)])
                else:
                    stOUT.wait(v, snap[("outd", j - 2)])
                sDV.wait(v)
                sDV.inc(v.tensor_tensor(outstg, frow, x2slot, ALU.add))
                snap[("out", j)] = sDV.n
            blk.vector(_)

            def _(sy, j=j, outstg=outstg, x2slot=x2slot):
                sDV.wait(sy, snap[("out", j)])
                if DEBUG and j == NT - 1:
                    dbg_dump(sy, "d_xr7", x2slot)
                stOUT.inc(sy.dma_start(out=y_d[j * 128:(j + 1) * 128, :],
                                       in_=outstg), 16)
                snap[("outd", j)] = stOUT.n
            blk.sync(_)


        for c in range(1, FF_NC):
            for j in range(NT):
                emit_ffn_group(c, j)
                if c == FF_NC - 1:
                    emit_p6_A(j)
                    if j >= 1:
                        emit_p6_B(j - 1)
            if c + 2 < FF_NC:
                def _(g, c=c):
                    sTE.wait(g, snap[("fmm", c, NT - 1)])
                    stW.inc(g.dma_start(out=wtch[:, c % 2, :, :],
                                        in_=wt_d[c + 2]), 16)
                    snap[("wt", c + 2)] = stW.n
                blk.gpsimd(_)

        emit_p6_B(NT - 1)

        def _(s):
            for fn in sc_defer:
                fn(s)
        blk.scalar(_)

        def _(sy):
            stOUT.wait(sy)
            if DEBUG:
                stDBG.wait(sy)
        blk.sync(_)

    return nc


def _host_layout(inputs):
    f = np.ascontiguousarray

    def wqkv(W):
        return f(W.T.reshape(NO, 128, NO, 128).transpose(2, 1, 0, 3))

    def wchunk(W, nch, co):
        return f(W.T.reshape(NO, 128, nch, co).transpose(2, 1, 0, 3))

    A = np.asarray(inputs["L0_ffn_lora_A"], np.float32)
    Bl = np.asarray(inputs["L0_ffn_lora_B"], np.float32)
    rw = np.asarray(inputs["L0_ffn_router_w"], np.float32)
    ar_full = np.concatenate(
        [A.transpose(2, 0, 1).reshape(D, 128), rw.T], axis=1)
    out = {
        "wq": wqkv(np.asarray(inputs["L0_attn_Wq"], np.float32)),
        "wk": wqkv(np.asarray(inputs["L0_attn_Wk"], np.float32)),
        "wv": wqkv(np.asarray(inputs["L0_attn_Wv"], np.float32)),
        "wo": wchunk(np.asarray(inputs["L0_attn_Wo"], np.float32), WO_NC, 512),
        "wt": wchunk(np.asarray(inputs["L0_ffn_weight"], np.float32), FF_NC, 512),
        "ar": f(ar_full.reshape(NO, 128, 136).transpose(1, 0, 2)),
        "bb": f(Bl.transpose(0, 2, 1).reshape(128, D)),
        "bq": f(np.asarray(inputs["L0_attn_bq"], np.float32).reshape(NO, 128).T),
        "bk": f(np.asarray(inputs["L0_attn_bk"], np.float32).reshape(NO, 128).T),
        "bv": f(np.asarray(inputs["L0_attn_bv"], np.float32).reshape(NO, 128).T),
        "bo": f(np.asarray(inputs["L0_attn_bo"], np.float32).reshape(1, D)),
        "idn": np.eye(128, dtype=np.float32),
        "epsc": np.full((128, 1), EPS, np.float32),
        "ones1": np.ones((1, 128), np.float32),
    }
    e2 = np.zeros((8, 128), np.float32)
    for e in range(8):
        e2[e, e * 16:(e + 1) * 16] = 2.0
    out["e2"] = e2
    return out


def kernel(**inputs):
    global _dbg_out
    x = np.asarray(inputs["L0_attn_x"], np.float32)
    Bsz, L, _ = x.shape
    xf = np.ascontiguousarray(x.reshape(Bsz * L, D))
    shared = _host_layout(inputs)

    nc = build_program()

    in_maps = []
    for c in range(8):
        m = dict(shared)
        m["x"] = np.ascontiguousarray(xf[c * T:(c + 1) * T])
        mask = np.zeros((8, 1), np.float32)
        s, p = c // 4, c % 4
        for r in range(8):
            if r // 4 == s and r % 4 < p:
                mask[r, 0] = 1.0
        m["maskc"] = mask
        in_maps.append(m)

    res = run_bass_kernel_spmd(nc, in_maps, core_ids=list(range(8)))
    if DEBUG:
        _dbg_out = res.results
    y = np.concatenate([res.results[c]["y"] for c in range(8)], axis=0)
    return y.reshape(Bsz, L, D)


# version 13
# speedup vs baseline: 1.1127x; 1.0426x over previous
import math
import numpy as np
from contextlib import ExitStack

import concourse.bass as bass
import concourse.mybir as mybir
from concourse.bass_utils import run_bass_kernel_spmd

F32 = mybir.dt.float32
BF16 = mybir.dt.bfloat16
ALU = mybir.AluOpType
AF = mybir.ActivationFunctionType
AX = mybir.AxisListType.X

T = 1024
D = 2048
NT = 8
NO = 16
NG = 4
WO_NC = 4
FF_NC = 4
EPS = 1e-5
SCALE = (D // 16) ** -0.5
LNS = math.log(SCALE)
BIG = 1.0e30

DEBUG = False
_dbg_out = {}


class S:

    def __init__(self, nc, ctx, name):
        self.h = ctx.enter_context(nc.semaphore(name))
        self.n = 0

    def inc(self, inst, k=1):
        inst.then_inc(self.h, k)
        self.n += k
        return self.n

    def wait(self, eng, val=None):
        v = self.n if val is None else val
        if v > 0:
            eng.wait_ge(self.h, v)

    def prebump(self, k=1):
        self.n += k
        return self.n


def build_program():
    nc = bass.Bass("TRN2", target_bir_lowering=False, debug=False)

    x_d = nc.dram_tensor("x", [T, D], F32, kind="ExternalInput").ap()
    wq_d = nc.dram_tensor("wq", [NO, 128, NO, 128], F32, kind="ExternalInput").ap()
    wk_d = nc.dram_tensor("wk", [NO, 128, NO, 128], F32, kind="ExternalInput").ap()
    wv_d = nc.dram_tensor("wv", [NO, 128, NO, 128], F32, kind="ExternalInput").ap()
    wo_d = nc.dram_tensor("wo", [WO_NC, 128, NO, 512], F32, kind="ExternalInput").ap()
    wt_d = nc.dram_tensor("wt", [FF_NC, 128, NO, 512], F32, kind="ExternalInput").ap()
    ar_d = nc.dram_tensor("ar", [128, NO, 136], F32, kind="ExternalInput").ap()
    bb_d = nc.dram_tensor("bb", [128, D], F32, kind="ExternalInput").ap()
    bq_d = nc.dram_tensor("bq", [128, NO], F32, kind="ExternalInput").ap()
    bk_d = nc.dram_tensor("bk", [128, NO], F32, kind="ExternalInput").ap()
    bv_d = nc.dram_tensor("bv", [128, NO], F32, kind="ExternalInput").ap()
    bo_d = nc.dram_tensor("bo", [1, D], F32, kind="ExternalInput").ap()
    idn_d = nc.dram_tensor("idn", [128, 128], F32, kind="ExternalInput").ap()
    e2_d = nc.dram_tensor("e2", [8, 128], F32, kind="ExternalInput").ap()
    ones_d = nc.dram_tensor("ones1", [1, 128], F32, kind="ExternalInput").ap()
    mask_d = nc.dram_tensor("maskc", [8, 1], F32, kind="ExternalInput").ap()
    eps_d = nc.dram_tensor("epsc", [128, 1], F32, kind="ExternalInput").ap()
    y_d = nc.dram_tensor("y", [T, D], F32, kind="ExternalOutput").ap()

    x2sp = nc.dram_tensor("x2sp", [T, D], F32, kind="Internal").ap()
    cin = [nc.dram_tensor(f"cin{g}", [1, 512], F32, kind="Internal").ap()
           for g in range(NG)]
    cout = [nc.dram_tensor(f"cout{g}", [8, 512], F32, kind="Internal",
                           addr_space="Shared").ap()
            for g in range(NG)]

    dbg = {}
    if DEBUG:
        for nm, shp in [
            ("d_h1t", [128, NO * T // 2]), ("d_phi", [128, T]),
            ("d_ukv", [128, T]), ("d_ctx", [128, NO * T // 2]),
            ("d_a", [128, D]), ("d_x2", [128, D]),
            ("d_lg", [128, 8]), ("d_w", [128, 8]),
            ("d_sx", [128, T // 2]), ("d_f", [128, D]),
            ("d_pref", [128, NO]), ("d_tot", [128, NO]),
            ("d_h3t", [128, NO * T // 2]), ("d_ax", [128, T]),
            ("d_x27", [128, D]), ("d_h3b7", [128, D // 2]),
            ("d_a7", [128, D]), ("d_st7", [128, 56]), ("d_asum", [128, 64]),
            ("d_f7", [128, D]), ("d_st47", [128, 56]), ("d_fsum", [128, 32]),
            ("d_xr7", [128, D]), ("d_wo3", [128, D]),
            ("d_wf", [8, T // 2]), ("d_lg2", [8, T]),
        ]:
            dbg[nm] = nc.dram_tensor(nm, shp, F32, kind="ExternalOutput").ap()

    ctx_mgr = ExitStack()
    with ctx_mgr as ctx:
        def sb(name, shape, dt=F32):
            return ctx.enter_context(nc.sbuf_tensor(name, shape, dt))

        def psum(name, shape, dt=F32):
            return ctx.enter_context(nc.psum_tensor(name, shape, dt))

        A_own = sb("A_own", [128, 8192])
        B_own = sb("B_own", [128, 8192])
        C_own = sb("C_own", [128, 16384])
        D_own = sb("D_own", [128, 6144])
        WOX = sb("WOX", [128, 6144])
        E_own = sb("E_own", [128, 2048])
        STG = sb("STG", [128, 2048])
        FB = sb("FB", [128, 2112])
        stats = sb("stats", [128, 56])
        asum = sb("asum", [128, 64])
        fsum = sb("fsum", [128, 32])
        xs1 = sb("xs1", [128, 8])
        bqc = sb("bqc", [128, NO]); bkc = sb("bkc", [128, NO]); bvc = sb("bvc", [128, NO])
        bqLn = sb("bqLn", [128, NO]); bqS2 = sb("bqS2", [128, NO])
        idn = sb("idn_s", [128, 128])
        e2f = sb("e2f", [8, 64])
        onesf = sb("onesf", [1, 64])
        bof = sb("bof", [1, 1024])
        maskc = sb("maskc_s", [8, 1])
        epsc = sb("epsc_s", [128, 1])
        tot0 = sb("tot0", [128, NO]); tot1 = sb("tot1", [128, NO])
        totS = sb("totS", [128, NO])
        pref = sb("pref", [128, NO])
        ltm = sb("ltm", [128, 8])
        r8a = sb("r8a", [128, 8]); r8b = sb("r8b", [128, 8]); r8c = sb("r8c", [128, 8])
        rc = sb("rc", [128, 8])
        wtm = sb("wtm", [128, 8])

        gsb = FB.ap()[0:8, 0:512]
        lgsb = E_own.ap()[0:8, 0:1024]
        wfm = E_own.ap()[0:8, 1024:1536]
        h1T = A_own.ap().bitcast(BF16).rearrange("p (n t) -> p n t", n=NO)
        ctxT = B_own.ap().bitcast(BF16).rearrange("p (n t) -> p n t", n=NO)
        wtch = B_own.ap().bitcast(BF16).rearrange("p (c n f) -> p c n f", c=2, n=NO)
        Cap = C_own.ap()
        C3 = Cap.rearrange("p (j f) -> p j f", j=NT)
        qkvw = D_own.ap().bitcast(BF16).rearrange("p (s f) -> p s f", s=6)
        woch = [WOX.ap()[:, 0:4096].bitcast(BF16).rearrange("p (n f) -> p n f", n=NO),
                D_own.ap()[:, 0:4096].bitcast(BF16).rearrange("p (n f) -> p n f", n=NO)]
        dump6 = WOX.ap().rearrange("p (c f) -> p c f", c=3)
        dmp23 = WOX.ap()[:, 4096:6144]
        h1b = FB.ap()[:, 0:1024].bitcast(BF16)
        bbv = FB.ap()[:, 0:1024].bitcast(BF16)
        arb = FB.ap()[:, 1024:2112].bitcast(BF16).rearrange(
            "p (n f) -> p n f", n=NO)
        e2b = e2f.ap().bitcast(BF16)
        onesb = onesf.ap().bitcast(BF16)
        bob = bof.ap().bitcast(BF16)
        axs = STG.ap()[:, 0:1024]
        sxb = STG.ap()[:, 1024:1536].bitcast(BF16)
        wfmb = wfm.bitcast(BF16)

        pb0 = psum("pb0", [128, 1024])
        pb1 = psum("pb1", [128, 1024])
        pvv = psum("pvv", [128, 1024])
        psm = psum("psm", [128, 512])

        stX = S(nc, ctx, "stX")
        stXP = [S(nc, ctx, "stXP%d" % k) for k in range(NT)]
        stTP = S(nc, ctx, "stTP")
        stW = S(nc, ctx, "stW")
        stCIN = S(nc, ctx, "stCIN")
        stGSB = S(nc, ctx, "stGSB")
        stSP = S(nc, ctx, "stSP")
        stXR = S(nc, ctx, "stXR")
        stOUT = S(nc, ctx, "stOUT")
        stDBG = S(nc, ctx, "stDBG")
        sCC = S(nc, ctx, "sCC")
        sPL = S(nc, ctx, "sPL")
        sTE = S(nc, ctx, "sTE")
        sDV = S(nc, ctx, "sDV")
        sSC = S(nc, ctx, "sSC")

        blk = ctx.enter_context(nc.Block())
        snap = {}
        sc_defer = []

        def SCD(fn):
            sc_defer.append(fn)

        def dbg_dump(sy, name, src_ap):
            if DEBUG and name in dbg:
                stDBG.inc(sy.dma_start(out=dbg[name], in_=src_ap), 16)

        def _(g):
            for dst, src in [
                (bqc.ap(), bq_d), (bkc.ap(), bk_d), (bvc.ap(), bv_d),
                (idn.ap(), idn_d), (maskc.ap(), mask_d), (epsc.ap(), eps_d),
            ]:
                stW.inc(g.dma_start(out=dst, in_=src), 16)
            stW.inc(g.dma_start(out=e2b, in_=e2_d), 16)
            stW.inc(g.dma_start(out=onesb, in_=ones_d), 16)
            stW.inc(g.dma_start(out=bob, in_=bo_d), 16)
            stW.inc(g.dma_start(out=arb, in_=ar_d), 16)
            snap["consts"] = stW.n
        blk.gpsimd(_)

        def _(v):
            stW.wait(v, snap["consts"])
            sDV.inc(v.tensor_single_scalar(bqLn.ap(), bqc.ap(), LNS, ALU.add))
            sDV.inc(v.tensor_single_scalar(bqS2.ap(), bqc.ap(), SCALE, ALU.mult))
            snap["dbias"] = sDV.n
        blk.vector(_)

        SC_S, SC_Q, SC_M, SC_M2, SC_VAR, SC_STD, SC_R = 0, 8, 16, 24, 32, 40, 48

        def stc(base, j):
            return stats.ap()[:, base + j:base + j + 1]

        def ln_stats_A(j, src_ap, dump_ap, tag, sum_on_scalar=False):
            if sum_on_scalar:
                def _(s):
                    s.activation(dump_ap, src_ap, AF.Copy,
                                 accum_out=stc(SC_S, j)).then_inc(sSC.h, 1)
                SCD(_)
                snap[(tag, "s", j)] = ("sc", sSC.prebump())
            else:
                def _(v):
                    sDV.inc(v.reduce_sum(stc(SC_S, j), src_ap, AX))
                    snap[(tag, "s", j)] = ("dv", sDV.n)
                blk.vector(_)

            def _(s):
                s.activation(dump_ap, src_ap, AF.Square,
                             accum_out=stc(SC_Q, j)).then_inc(sSC.h, 1)
            SCD(_)
            snap[(tag, "q", j)] = ("sc", sSC.prebump())

        def ln_stats_B(j, tag):
            kind, sval = snap[(tag, "s", j)]
            qk, qval = snap[(tag, "q", j)]

            def _(v):
                if qk == "sc":
                    sSC.wait(v, qval)
                else:
                    sDV.wait(v, qval)
                if kind == "dv":
                    sDV.wait(v, sval)
                sDV.inc(v.tensor_single_scalar(stc(SC_M, j), stc(SC_S, j),
                                               1.0 / D, ALU.mult))
                sDV.wait(v)
                sDV.inc(v.tensor_tensor(stc(SC_M2, j), stc(SC_M, j),
                                        stc(SC_M, j), ALU.mult))
                sDV.wait(v)
                sDV.inc(v.scalar_tensor_tensor(stc(SC_VAR, j), stc(SC_Q, j),
                                               1.0 / D, stc(SC_M2, j),
                                               ALU.mult, ALU.subtract))
                snap[(tag, "var", j)] = sDV.n
            blk.vector(_)

            vvar = snap[(tag, "var", j)]
            vln = sSC.prebump()
            snap[(tag, "r", j)] = sSC.prebump()

            def _(s, vvar=vvar, vln=vln):
                sDV.wait(s, vvar)
                s.activation(stc(SC_STD, j), stc(SC_VAR, j),
                             AF.Ln, bias=epsc.ap()).then_inc(sSC.h, 1)
                s.wait_ge(sSC.h, vln)
                s.activation(stc(SC_R, j), stc(SC_STD, j),
                             AF.Exp, scale=-0.5).then_inc(sSC.h, 1)
            SCD(_)

        def p1_xslot(j):
            return Cap[:, j * 2048:(j + 1) * 2048]

        def _(s):
            for hh in range(2):
                dst = Cap[:, hh * 8192:(hh + 1) * 8192].rearrange(
                    "p (j d) -> p j d", j=4)
                srcv = x_d[hh * 512:(hh + 1) * 512, :].rearrange(
                    "(j p) d -> p j d", p=128)
                s.dma_start(out=dst, in_=srcv).then_inc(stXP[hh].h, 16)
        SCD(_)
        stXP[0].prebump(16)
        stXP[1].prebump(16)

        def emit_ln1_B(j):
            xslot = p1_xslot(j)
            ln_stats_B(j, "ln1")

            def _(v, j=j):
                sDV.inc(v.tensor_copy(xs1.ap()[:, j:j + 1], stc(SC_S, j)))
            blk.vector(_)

            def _(v, j=j, xslot=xslot):
                sSC.wait(v, snap[("ln1", "r", j)])
                if j >= 1:
                    stTP.wait(v, snap[("tp1", j - 1)])
                sDV.inc(v.tensor_scalar(h1b, xslot, stc(SC_M, j), stc(SC_R, j),
                                        ALU.subtract, ALU.mult))
                snap[("ln1h1b", j)] = sDV.n
            blk.vector(_)

            def _(sy, j=j):
                sDV.wait(sy, snap[("ln1h1b", j)])
                src = h1b.rearrange("p (n t) -> p n t", n=NO)
                dst = h1T[:, :, j * 128:(j + 1) * 128]
                stTP.inc(sy.dma_start(out=dst, in_=src, transpose=True), 16)
                snap[("tp1", j)] = stTP.n
            blk.sync(_)

        for j in range(NT):
            xslot = p1_xslot(j)

            def _(v, j=j):
                stXP[j // 4].wait(v, 16)
            blk.vector(_)

            def _(s, j=j):
                stXP[j // 4].wait(s, 16)
            SCD(_)

            ln_stats_A(j, xslot, E_own.ap(), "ln1")
            if j >= 1:
                emit_ln1_B(j - 1)
        emit_ln1_B(NT - 1)

        slot_last_te = [0] * 6

        def load_qkv(g, i):
            for wi, wsrc in enumerate((wq_d, wk_d, wv_d)):
                s6 = (3 * i + wi) % 6
                sTE.wait(g, slot_last_te[s6])
                stW.inc(g.dma_start(out=qkvw[:, s6, :],
                                    in_=wsrc[i].rearrange("p n f -> p (n f)")), 16)
            snap[("wq", i)] = stW.n

        def emit_cc(g, grp):
            stCIN.wait(g, snap[("cin", grp)])
            sCC.inc(g.collective_compute(
                "AllGather", ALU.bypass, replica_groups=[list(range(8))],
                ins=[cin[grp].opt()], outs=[cout[grp].opt()]))
            snap[("cc", grp)] = sCC.n

        def emit_prefix(grp2, prevg):
            def _(sy):
                sCC.wait(sy, snap[("cc", grp2)])
                if prevg is not None:
                    sTE.wait(sy, snap[("prefmm", prevg)])
                stGSB.inc(sy.dma_start(out=gsb, in_=cout[grp2]), 16)
                snap[("gsb", grp2)] = stGSB.n
            blk.sync(_)

            def _(t):
                stGSB.wait(t, snap[("gsb", grp2)])
                if prevg is not None:
                    sDV.wait(t, snap[("prefevac", prevg)])
                gv = gsb.rearrange("r (c g) -> r c g", g=4)
                inst = None
                for q in range(4):
                    inst = t.matmul(psm.ap()[:, q:q + 1], gv[:, :, q],
                                    maskc.ap(), start=True, stop=True)
                sTE.inc(inst)
                snap[("prefmm", grp2)] = sTE.n
            blk.tensor(_)

            def _(v):
                sTE.wait(v, snap[("prefmm", grp2)])
                sDV.inc(v.tensor_copy(pref.ap()[:, grp2 * 4:grp2 * 4 + 4],
                                      psm.ap()[:, 0:4]))
                snap[("prefevac", grp2)] = sDV.n
            blk.vector(_)

        def emit_scan_ctx(i2):
            grp2 = i2 // 4

            def _(v):
                sDV.wait(v, snap[("prefevac", grp2)])
                slot = i2 % 8
                ub = Cap[:, 8192 + slot * 1024: 8192 + (slot + 1) * 1024]
                sDV.inc(v.tensor_tensor_scan(ub, ub, ub,
                                             pref.ap()[:, i2:i2 + 1],
                                             ALU.add, ALU.bypass))
                sDV.wait(v)
                sDV.inc(v.tensor_tensor(ctxT[:, i2, :], ub,
                                        Cap[:, slot * 1024:(slot + 1) * 1024],
                                        ALU.mult))
                snap[("ctx", i2)] = sDV.n
            blk.vector(_)

        ORDER = [12, 13, 14, 15] + list(range(12))

        def _(g):
            stXP[0].wait(g, 16)
            load_qkv(g, ORDER[0])
            stXP[1].wait(g, 16)
            load_qkv(g, ORDER[1])
        blk.gpsimd(_)

        for p in range(NO):
            i = ORDER[p]
            iprev = ORDER[p - 1] if p >= 1 else None
            for h in range(2):
                tsl = slice(h * 512, (h + 1) * 512)

                def _(t, i=i, h=h, tsl=tsl, p=p, iprev=iprev):
                    stW.wait(t, snap[("wq", i)])
                    if p == 0 and h == 0:
                        sDV.wait(t, snap["dbias"])
                    if iprev is not None:
                        sSC.wait(t, snap[("kacts", iprev, h)])
                        sDV.wait(t, snap[("u", iprev, h)])
                    for wi, pst in ((0, pb0), (1, pb1), (2, pvv)):
                        s6 = (3 * i + wi) % 6
                        inst = None
                        if p == 0:
                            for d in range(NO):
                                for j4 in range(4):
                                    jj = h * 4 + j4
                                    if wi == 0 and d == 0:
                                        stTP.wait(t, snap[("tp1", jj)])
                                    inst = t.matmul(
                                        pst.ap()[:, h * 512 + j4 * 128:
                                                 h * 512 + (j4 + 1) * 128],
                                        qkvw[:, s6, d * 128:(d + 1) * 128],
                                        h1T[:, d, jj * 128:(jj + 1) * 128],
                                        start=(d == 0 and j4 == 0),
                                        stop=(d == NO - 1),
                                        skip_group_check=True)
                        else:
                            for d in range(NO):
                                inst = t.matmul(
                                    pst.ap()[:, tsl],
                                    qkvw[:, s6, d * 128:(d + 1) * 128],
                                    h1T[:, d, tsl],
                                    start=(d == 0), stop=(d == NO - 1))
                        sTE.inc(inst)
                        snap[("mm", wi, i, h)] = sTE.n
                    for wi in range(3):
                        slot_last_te[(3 * i + wi) % 6] = sTE.n
                blk.tensor(_)

                vu_prev = snap[("u", iprev, h)] if iprev is not None else 0

                def _(s, i=i, h=h, tsl=tsl, vu_prev=vu_prev, p=p):
                    if vu_prev:
                        sDV.wait(s, vu_prev)
                    if p == 0 and h == 0:
                        sDV.wait(s, snap[("ln1h1b", NT - 1)])
                    sTE.wait(s, snap[("mm", 0, i, h)])
                    s.activation(STG.ap()[:, 0:512], pb0.ap()[:, tsl],
                                 AF.Exp, bias=bqLn.ap()[:, i:i + 1]).then_inc(sSC.h, 1)
                    s.activation(STG.ap()[:, 512:1024], pb0.ap()[:, tsl],
                                 AF.Relu, bias=bqS2.ap()[:, i:i + 1],
                                 scale=SCALE).then_inc(sSC.h, 1)
                    sTE.wait(s, snap[("mm", 1, i, h)])
                    s.activation(STG.ap()[:, 1024:1536], pb1.ap()[:, tsl],
                                 AF.Exp, bias=bkc.ap()[:, i:i + 1]).then_inc(sSC.h, 1)
                    s.activation(STG.ap()[:, 1536:2048], pb1.ap()[:, tsl],
                                 AF.Relu, bias=bkc.ap()[:, i:i + 1]).then_inc(sSC.h, 1)
                SCD(_)
                sSC.prebump(2)
                snap[("qacts", i, h)] = sSC.n
                sSC.prebump(2)
                snap[("kacts", i, h)] = sSC.n

                def _(v, i=i, h=h, tsl=tsl):
                    slot = i % 8
                    sSC.wait(v, snap[("qacts", i, h)])
                    sDV.inc(v.scalar_tensor_tensor(
                        Cap[:, slot * 1024 + h * 512: slot * 1024 + (h + 1) * 512],
                        STG.ap()[:, 0:512], SCALE, STG.ap()[:, 512:1024],
                        ALU.min, ALU.add))
                    sSC.wait(v, snap[("kacts", i, h)])
                    sDV.inc(v.scalar_tensor_tensor(
                        STG.ap()[:, 1024:1536],
                        STG.ap()[:, 1024:1536], 1.0, STG.ap()[:, 1536:2048],
                        ALU.min, ALU.add))
                    sDV.wait(v)
                    sTE.wait(v, snap[("mm", 2, i, h)])
                    tcol = tot0 if h == 0 else tot1
                    sDV.inc(v.scalar_tensor_tensor(
                        Cap[:, 8192 + slot * 1024 + h * 512:
                            8192 + slot * 1024 + (h + 1) * 512],
                        pvv.ap()[:, tsl], bvc.ap()[:, i:i + 1],
                        STG.ap()[:, 1024:1536],
                        ALU.add, ALU.mult,
                        accum_out=tcol.ap()[:, i:i + 1]))
                    snap[("u", i, h)] = sDV.n
                blk.vector(_)

            grp = i // 4
            if i % 4 == 3:
                g4 = slice(grp * 4, grp * 4 + 4)

                def _(v, grp=grp, g4=g4):
                    sDV.wait(v)
                    sDV.inc(v.tensor_tensor(totS.ap()[:, g4], tot0.ap()[:, g4],
                                            tot1.ap()[:, g4], ALU.add))
                    snap[("tot", grp)] = sDV.n
                blk.vector(_)

                def _(sy, grp=grp, g4=g4):
                    sDV.wait(sy, snap[("tot", grp)])
                    dst = cin[grp].rearrange("a (c g) -> a c g", g=4)[0]
                    stCIN.inc(sy.dma_start(out=dst, in_=totS.ap()[:, g4]), 16)
                    snap[("cin", grp)] = stCIN.n
                blk.sync(_)

            def _(g, i=i, p=p):
                if p + 2 < NO:
                    load_qkv(g, ORDER[p + 2])
                if p == 8:
                    stW.inc(g.dma_start(out=woch[0][:, :, :], in_=wo_d[0]), 16)
                    snap[("wo", 0)] = stW.n
                if p == NO - 1:
                    sTE.wait(g, snap[("mm", 2, ORDER[NO - 1], 1)])
                    stW.inc(g.dma_start(out=woch[1][:, :, :], in_=wo_d[1]), 16)
                    snap[("wo", 1)] = stW.n
                if i % 4 == 3:
                    emit_cc(g, i // 4)
            blk.gpsimd(_)

            _tails = {5: [("p", 3, None), ("s", 12)],
                      6: [("s", 13), ("s", 14)], 7: [("s", 15)],
                      9: [("p", 0, 3), ("s", 0)],
                      10: [("s", 1), ("s", 2)], 11: [("s", 3)],
                      13: [("p", 1, 0), ("s", 4)],
                      14: [("s", 5), ("s", 6)], 15: [("s", 7)]}
            for task in _tails.get(p, []):
                if task[0] == "p":
                    emit_prefix(task[1], task[2])
                else:
                    emit_scan_ctx(task[1])

        emit_prefix(2, 1)
        for i2 in range(8, 12):
            emit_scan_ctx(i2)

        if DEBUG:
            def _(sy):
                sDV.wait(sy, snap[("ctx", 11)])
                dbg_dump(sy, "d_tot", totS.ap())
                dbg_dump(sy, "d_pref", pref.ap())
                dbg_dump(sy, "d_phi", Cap[:, 7 * 1024:8 * 1024])
                dbg_dump(sy, "d_ukv", Cap[:, 8192 + 7 * 1024: 8192 + 8 * 1024])
                dbg_dump(sy, "d_h1t", A_own.ap())
                dbg_dump(sy, "d_ctx", B_own.ap())
            blk.sync(_)

        DSP = D_own.ap()[:, 4096:6144]

        def p3_xslot(j):
            return E_own.ap() if j % 2 == 0 else STG.ap()

        def emit_xload2(j):
            def _(s, j=j):
                if j == 1:
                    sDV.wait(s, snap[("u", ORDER[NO - 1], 1)])
                elif j >= 2:
                    sDV.wait(s, snap[("h3b", j - 2)])
                s.dma_start(out=p3_xslot(j),
                            in_=x_d[j * 128:(j + 1) * 128, :]).then_inc(stX.h, 16)
            SCD(_)
            snap[("xload2", j)] = stX.prebump(16)

        def emit_ln23_A(j):
            xslot = p3_xslot(j)
            arow = C3[:, j, :]
            dmp2 = dmp23

            def _(v, j=j):
                sSC.wait(v, snap[("aevac", WO_NC - 1, j)])
                sDV.inc(v.reduce_sum(stc(SC_S, j),
                                     asum.ap()[:, j * 8:j * 8 + WO_NC], AX))
                snap[("ln2_%d" % j, "s", j)] = ("dv", sDV.n)
            blk.vector(_)

            def _(s, j=j):
                sSC.wait(s, snap[("aevac", WO_NC - 1, j)])
            SCD(_)

            tag2 = "ln2_%d" % j

            def _(s, j=j, tag2=tag2):
                s.activation(dmp2, arow, AF.Square,
                             accum_out=stc(SC_Q, j)).then_inc(sSC.h, 1)
            SCD(_)
            snap[(tag2, "q", j)] = ("sc", sSC.prebump())

        def emit_ln23_B(j):
            xslot = p3_xslot(j)
            arow = C3[:, j, :]
            dmp2 = dmp23

            ln_stats_B(j, "ln2_%d" % j)

            if DEBUG and j == NT - 1:
                def _(sy, j=j, arow=arow):
                    sSC.wait(sy, snap[("ln2_%d" % j, "r", j)])
                    dbg_dump(sy, "d_a7", arow)
                    dbg_dump(sy, "d_st7", stats.ap())
                    dbg_dump(sy, "d_asum", asum.ap())
                    dbg_dump(sy, "d_wo3", D_own.ap()[:, 0:2048])
                blk.sync(_)

            def _(v, j=j, xslot=xslot, arow=arow):
                sSC.wait(v, snap[("ln2_%d" % j, "r", j)])
                if DEBUG and j == NT - 1:
                    stDBG.wait(v)
                sDV.inc(v.tensor_scalar(arow, arow, stc(SC_M, j), stc(SC_R, j),
                                        ALU.subtract, ALU.mult))
                stX.wait(v, snap[("xload2", j)])
                sDV.wait(v)
                sDV.inc(v.tensor_tensor(xslot, arow, xslot, ALU.add))
                snap[("x2", j)] = sDV.n
            blk.vector(_)

            vx2 = snap[("x2", j)]

            def _(s, j=j, xslot=xslot, vx2=vx2):
                sDV.wait(s, vx2)
                s.dma_start(out=x2sp[j * 128:(j + 1) * 128, :],
                            in_=xslot).then_inc(stSP.h, 16)
            SCD(_)
            snap[("sp", j)] = stSP.prebump(16)

            if DEBUG and (j == 0 or j == NT - 1):
                def _(sy, j=j, xslot=xslot):
                    sDV.wait(sy, snap[("x2", j)])
                    if j == 0:
                        dbg_dump(sy, "d_x2", xslot)
                    else:
                        dbg_dump(sy, "d_x27", xslot)
                blk.sync(_)

            tag3 = "ln3_%d" % j

            def _(v, j=j, xslot=xslot, vx2=vx2):
                sDV.inc(v.tensor_copy(stc(SC_S, j), xs1.ap()[:, j:j + 1]))
                snap[(tag3, "s", j)] = ("dv", sDV.n)
            blk.vector(_)

            def _(s, j=j, xslot=xslot, vx2=vx2):
                sDV.wait(s, vx2)
                s.activation(dmp2, xslot, AF.Square,
                             accum_out=stc(SC_Q, j)).then_inc(sSC.h, 1)
            SCD(_)
            snap[(tag3, "q", j)] = ("sc", sSC.prebump())

            ln_stats_B(j, "ln3_%d" % j)

            def _(v, j=j, xslot=xslot):
                sSC.wait(v, snap[("ln3_%d" % j, "r", j)])
                if j >= 1:
                    stTP.wait(v, snap[("tp3", j - 1)])
                else:
                    stTP.wait(v, snap[("tp1", NT - 1)])
                sDV.inc(v.tensor_scalar(h1b, xslot, stc(SC_M, j), stc(SC_R, j),
                                        ALU.subtract, ALU.mult))
                snap[("h3b", j)] = sDV.n
            blk.vector(_)

            def _(sy, j=j):
                sDV.wait(sy, snap[("h3b", j)])
                src = h1b.rearrange("p (n t) -> p n t", n=NO)
                dst = h1T[:, :, j * 128:(j + 1) * 128]
                stTP.inc(sy.dma_start(out=dst, in_=src, transpose=True), 16)
                snap[("tp3", j)] = stTP.n
                if j + 2 < NT:
                    stSP.wait(sy, snap[("sp", j)])
                    stX.inc(sy.dma_start(
                        out=p3_xslot(j + 2),
                        in_=x_d[(j + 2) * 128:(j + 3) * 128, :]), 16)
                    snap[("xload2", j + 2)] = stX.n
            blk.sync(_)


        bank_prev = [None, None]

        def emit_wo_group(c, j):
            bank = (c + j) % 2
            aps = pb0.ap()[:, 0:512] if bank == 0 else pb0.ap()[:, 512:1024]
            osl = slice(c * 512, (c + 1) * 512)

            def _(t, c=c, j=j, aps=aps, osl=osl, bprev=bank_prev[bank]):
                stW.wait(t, snap[("wo", c)])
                if bprev is not None:
                    sSC.wait(t, bprev)
                inst = t.matmul(aps, onesb, bob[0:1, osl],
                                start=True, stop=False)
                cts = list(range(8)) + list(range(12, 16)) + list(range(8, 12))
                for k_, ct in enumerate(cts):
                    if c == 0 and j == 0:
                        sDV.wait(t, snap[("ctx", ct)])
                    inst = t.matmul(
                        aps, ctxT[:, ct, j * 128:(j + 1) * 128],
                        woch[c % 2][:, ct, :],
                        start=False, stop=(k_ == NO - 1))
                sTE.inc(inst)
                snap[("womm", c, j)] = sTE.n
            blk.tensor(_)

            def _(s, c=c, j=j, aps=aps, osl=osl):
                sTE.wait(s, snap[("womm", c, j)])
                s.activation(C3[:, j, osl], aps, AF.Copy,
                             accum_out=asum.ap()[:, j * 8 + c:j * 8 + c + 1]
                             ).then_inc(sSC.h, 1)
            SCD(_)
            snap[("aevac", c, j)] = sSC.prebump()
            bank_prev[bank] = snap[("aevac", c, j)]

        for c in range(WO_NC - 2):
            for j in range(NT):
                emit_wo_group(c, j)
            if c + 2 < WO_NC:
                def _(g, c=c):
                    sTE.wait(g, snap[("womm", c, NT - 1)])
                    stW.inc(g.dma_start(out=woch[(c + 2) % 2][:, :, :],
                                        in_=wo_d[c + 2]), 16)
                    snap[("wo", c + 2)] = stW.n
                blk.gpsimd(_)

        for j0 in range(2):
            emit_xload2(j0)

        for j in range(NT):
            emit_wo_group(WO_NC - 2, j)
            emit_wo_group(WO_NC - 1, j)
            emit_ln23_A(j)
            if j >= 1:
                emit_ln23_B(j - 1)

        emit_ln23_B(NT - 1)

        def _(g):
            stTP.wait(g, snap[("tp3", NT - 1)])
            stW.inc(g.dma_start(out=bbv, in_=bb_d), 16)
            snap["bb"] = stW.n
            sTE.wait(g, snap[("womm", WO_NC - 1, NT - 1)])
            stW.inc(g.dma_start(out=wtch[:, 0, :, :], in_=wt_d[0]), 16)
            snap[("wt", 0)] = stW.n
            stW.inc(g.dma_start(out=wtch[:, 1, :, :], in_=wt_d[1]), 16)
            snap[("wt", 1)] = stW.n
        blk.gpsimd(_)

        if DEBUG:
            def _(sy):
                stTP.wait(sy, snap[("tp3", NT - 1)])
                dbg_dump(sy, "d_h3t", A_own.ap())
            blk.sync(_)

        for hh in range(2):
            pslice = pb1.ap()[:, hh * 512:(hh + 1) * 512]

            def _(t, hh=hh, pslice=pslice):
                stTP.wait(t, snap[("tp3", 3 if hh == 0 else NT - 1)])
                if hh == 0:
                    sSC.wait(t, snap[("kacts", NO - 1, 1)])
                inst = None
                for d in range(NO):
                    inst = t.matmul(pslice, arb[:, d, 0:128],
                                    h1T[:, d, hh * 512:(hh + 1) * 512],
                                    start=(d == 0), stop=(d == NO - 1))
                sTE.inc(inst)
                snap[("axmm", hh)] = sTE.n
            blk.tensor(_)

            def _(v, hh=hh, pslice=pslice):
                sTE.wait(v, snap[("axmm", hh)])
                if hh == 0:
                    stSP.wait(v, snap[("sp", NT - 1)])
                sDV.inc(v.tensor_copy(axs[:, hh * 512:(hh + 1) * 512], pslice))
                snap[("axev", hh)] = sDV.n
            blk.vector(_)

            def _(t, hh=hh, pslice=pslice):
                sDV.wait(t, snap[("axev", hh)])
                inst = None
                for d in range(NO):
                    inst = t.matmul(pslice[0:8, :], arb[:, d, 128:136],
                                    h1T[:, d, hh * 512:(hh + 1) * 512],
                                    start=(d == 0), stop=(d == NO - 1))
                sTE.inc(inst)
                snap[("lgmm", hh)] = sTE.n
            blk.tensor(_)

            def _(v, hh=hh, pslice=pslice):
                sTE.wait(v, snap[("lgmm", hh)])
                sDV.inc(v.tensor_copy(lgsb[0:8, hh * 512:(hh + 1) * 512],
                                      pslice[0:8, :]))
                snap[("lgev", hh)] = sDV.n
            blk.vector(_)

        ffn_bank_prev = [("dv", snap[("u", NO - 1, 1)]),
                         ("dv", snap[("u", NO - 1, 1)])]

        def emit_ffn_group(c, j):
            bank = j % 2
            fps = pvv.ap()[:, 0:512] if bank == 0 else pvv.ap()[:, 512:1024]
            osl = slice(c * 512, (c + 1) * 512)

            def _(t, c=c, j=j, fps=fps, osl=osl, bprev=ffn_bank_prev[bank]):
                stW.wait(t, snap[("wt", c)])
                if c == 0:
                    stW.wait(t, snap["bb"])
                    sDV.wait(t, snap[("sx", j)])
                kind, val = bprev
                if kind == "dv":
                    sDV.wait(t, val)
                else:
                    sSC.wait(t, val)
                for d in range(NO):
                    t.matmul(fps, h1T[:, d, j * 128:(j + 1) * 128],
                             wtch[:, c % 2, d, :],
                             start=(d == 0), stop=False)
                inst = t.matmul(fps, sxb[:, j * 128:(j + 1) * 128],
                                bbv[:, osl], start=False, stop=True)
                sTE.inc(inst)
                snap[("fmm", c, j)] = sTE.n
            blk.tensor(_)

            def _(s, c=c, j=j, fps=fps, osl=osl):
                sTE.wait(s, snap[("fmm", c, j)])
                s.activation(C3[:, j, osl], fps, AF.Copy,
                             accum_out=fsum.ap()[:, j * 4 + c:j * 4 + c + 1]
                             ).then_inc(sSC.h, 1)
            SCD(_)
            snap[("fevac", c, j)] = sSC.prebump()
            ffn_bank_prev[bank] = ("sc", snap[("fevac", c, j)])

        for j in range(NT):
            hh = j // 4

            def _(t, j=j, hh=hh):
                sDV.wait(t, snap[("lgev", hh)])
                if j > 0:
                    sDV.wait(t, snap[("sx", j - 1)])
                sTE.inc(t.transpose(psm.ap()[:, 0:8],
                                    lgsb[0:8, j * 128:(j + 1) * 128],
                                    idn.ap()[0:8, 0:8]))
                snap[("lgT", j)] = sTE.n
            blk.tensor(_)

            def _(v, j=j):
                sTE.wait(v, snap[("lgT", j)])
                sDV.inc(v.tensor_copy(ltm.ap(), psm.ap()[:, 0:8]))
                sDV.wait(v)
                sDV.inc(v.reduce_max(rc.ap()[:, 0:1], ltm.ap(), AX))
                sDV.wait(v)
                sDV.inc(v.tensor_single_scalar(r8a.ap(), ltm.ap(),
                                               rc.ap()[:, 0:1], ALU.is_ge))
                sDV.wait(v)
                sDV.inc(v.scalar_tensor_tensor(r8b.ap(), r8a.ap(), -BIG,
                                               ltm.ap(), ALU.mult, ALU.add))
                sDV.wait(v)
                sDV.inc(v.reduce_max(rc.ap()[:, 1:2], r8b.ap(), AX))
                sDV.wait(v)
                sDV.inc(v.tensor_single_scalar(rc.ap()[:, 2:3], rc.ap()[:, 0:1],
                                               -1.0, ALU.mult))
                sDV.wait(v)
                sDV.inc(v.tensor_tensor(rc.ap()[:, 3:4], rc.ap()[:, 1:2],
                                        rc.ap()[:, 0:1], ALU.subtract))
                snap[("rmath", j)] = sDV.n
            blk.vector(_)

            def _(s, j=j):
                sDV.wait(s, snap[("rmath", j)])
                s.activation(r8c.ap(), ltm.ap(), AF.Exp,
                             bias=rc.ap()[:, 2:3]).then_inc(sSC.h, 1)
                s.activation(rc.ap()[:, 4:5], rc.ap()[:, 3:4],
                             AF.Exp).then_inc(sSC.h, 1)
            SCD(_)
            snap[("rexp", j)] = sSC.prebump(2)

            def _(v, j=j):
                sSC.wait(v, snap[("rexp", j)])
                sDV.inc(v.tensor_single_scalar(r8a.ap(), ltm.ap(),
                                               rc.ap()[:, 1:2], ALU.is_ge))
                sDV.inc(v.tensor_single_scalar(rc.ap()[:, 5:6], rc.ap()[:, 4:5],
                                               1.0, ALU.add))
                sDV.wait(v)
                sDV.inc(v.reciprocal(rc.ap()[:, 6:7], rc.ap()[:, 5:6]))
                sDV.wait(v)
                sDV.inc(v.tensor_tensor(r8b.ap(), r8c.ap(), r8a.ap(), ALU.mult))
                sDV.wait(v)
                sDV.inc(v.tensor_single_scalar(wtm.ap(), r8b.ap(),
                                               rc.ap()[:, 6:7], ALU.mult))
                snap[("w", j)] = sDV.n
            blk.vector(_)

            if j >= 1:
                emit_ffn_group(0, j - 1)

            def _(t, j=j):
                sDV.wait(t, snap[("w", j)])
                sTE.inc(t.transpose(psm.ap()[0:8, 128:256], wtm.ap(), idn.ap()))
                snap[("wT", j)] = sTE.n
            blk.tensor(_)

            def _(v, j=j):
                sTE.wait(v, snap[("wT", j)])
                sDV.inc(v.tensor_copy(wfmb[0:8, j * 128:(j + 1) * 128],
                                      psm.ap()[0:8, 128:256]))
                snap[("wev", j)] = sDV.n
            blk.vector(_)

            def _(t, j=j):
                sDV.wait(t, snap[("wev", j)])
                sTE.inc(t.matmul(psm.ap()[:, 256:384], e2b,
                                 wfmb[0:8, j * 128:(j + 1) * 128],
                                 start=True, stop=True))
                snap[("wexpmm", j)] = sTE.n
            blk.tensor(_)

            def _(v, j=j, hh=hh):
                sTE.wait(v, snap[("wexpmm", j)])
                sDV.inc(v.tensor_tensor(sxb[:, j * 128:(j + 1) * 128],
                                        axs[:, j * 128:(j + 1) * 128],
                                        psm.ap()[:, 256:384], ALU.mult))
                snap[("sx", j)] = sDV.n
            blk.vector(_)

        emit_ffn_group(0, NT - 1)

        def _(g):
            sTE.wait(g, snap[("fmm", 0, NT - 1)])
            stW.inc(g.dma_start(out=wtch[:, 0, :, :], in_=wt_d[2]), 16)
            snap[("wt", 2)] = stW.n
        blk.gpsimd(_)

        if DEBUG:
            def _(sy):
                sDV.wait(sy, snap[("lgev", 1)])
                dbg_dump(sy, "d_lg2", lgsb[0:8, :])
                sDV.wait(sy, snap[("sx", NT - 1)])
                dbg_dump(sy, "d_sx", sxb.bitcast(F32))
                dbg_dump(sy, "d_ax", axs)
                dbg_dump(sy, "d_wf", wfmb.bitcast(F32))
            blk.sync(_)

        def emit_p6_A(j):
            frow = C3[:, j, :]
            dmp = dump6[:, 0, :]

            def _(v, j=j):
                sSC.wait(v, snap[("fevac", FF_NC - 1, j)])
            blk.vector(_)

            def _(s, j=j):
                sSC.wait(s, snap[("fevac", FF_NC - 1, j)])
                if j == 0:
                    sTE.wait(s, snap[("womm", WO_NC - 1, NT - 1)])
            SCD(_)

            tag4 = "ln4_%d" % j

            def _(v, j=j):
                sSC.wait(v, snap[("fevac", FF_NC - 1, j)])
                sDV.inc(v.reduce_sum(stc(SC_S, j),
                                     fsum.ap()[:, j * 4:(j + 1) * 4], AX))
                snap[(tag4, "s", j)] = ("dv", sDV.n)
            blk.vector(_)

            def _(s, j=j, frow=frow, dmp=dmp):
                s.activation(dmp, frow, AF.Square,
                             accum_out=stc(SC_Q, j)).then_inc(sSC.h, 1)
            SCD(_)
            snap[(tag4, "q", j)] = ("sc", sSC.prebump())

            vgate6 = snap[("out", j - 2)] if j >= 2 else snap[("h3b", NT - 1)]
            vsp = snap[("sp", j)]

            def _(s, j=j, vgate6=vgate6, vsp=vsp):
                sDV.wait(s, vgate6)
                stSP.wait(s, vsp)
                s.dma_start(
                    out=D_own.ap()[:, (j % 2) * 2048:(j % 2) * 2048 + 2048],
                    in_=x2sp[j * 128:(j + 1) * 128, :]).then_inc(stXR.h, 16)
            SCD(_)
            snap[("x2r", j)] = stXR.prebump(16)

        def emit_p6_B(j):
            frow = C3[:, j, :]
            outstg = dump6[:, 1 + (j % 2), :]
            x2slot = D_own.ap()[:, (j % 2) * 2048:(j % 2) * 2048 + 2048]

            ln_stats_B(j, "ln4_%d" % j)

            if DEBUG and j == NT - 1:
                def _(sy, j=j, frow=frow):
                    sSC.wait(sy, snap[("ln4_%d" % j, "r", j)])
                    dbg_dump(sy, "d_f7", frow)
                    dbg_dump(sy, "d_st47", stats.ap())
                    dbg_dump(sy, "d_fsum", fsum.ap())
                blk.sync(_)

            def _(v, j=j, frow=frow, outstg=outstg, x2slot=x2slot):
                sSC.wait(v, snap[("ln4_%d" % j, "r", j)])
                if DEBUG and j == NT - 1:
                    stDBG.wait(v)
                sDV.inc(v.tensor_scalar(frow, frow, stc(SC_M, j), stc(SC_R, j),
                                        ALU.subtract, ALU.mult))
                snap[("h4", j)] = sDV.n
                stXR.wait(v, snap[("x2r", j)])
                if j < 2:
                    sTE.wait(v, snap[("womm", WO_NC - 1, NT - 1)])
                else:
                    stOUT.wait(v, snap[("outd", j - 2)])
                sDV.wait(v)
                sDV.inc(v.tensor_tensor(outstg, frow, x2slot, ALU.add))
                snap[("out", j)] = sDV.n
            blk.vector(_)

            def _(sy, j=j, outstg=outstg, x2slot=x2slot):
                sDV.wait(sy, snap[("out", j)])
                if DEBUG and j == NT - 1:
                    dbg_dump(sy, "d_xr7", x2slot)
                stOUT.inc(sy.dma_start(out=y_d[j * 128:(j + 1) * 128, :],
                                       in_=outstg), 16)
                snap[("outd", j)] = stOUT.n
            blk.sync(_)


        for c in range(1, FF_NC):
            for j in range(NT):
                emit_ffn_group(c, j)
                if c == FF_NC - 1:
                    emit_p6_A(j)
                    if j >= 1:
                        emit_p6_B(j - 1)
            if c + 2 < FF_NC:
                def _(g, c=c):
                    sTE.wait(g, snap[("fmm", c, NT - 1)])
                    stW.inc(g.dma_start(out=wtch[:, c % 2, :, :],
                                        in_=wt_d[c + 2]), 16)
                    snap[("wt", c + 2)] = stW.n
                blk.gpsimd(_)

        emit_p6_B(NT - 1)

        def _(s):
            for fn in sc_defer:
                fn(s)
        blk.scalar(_)

        def _(sy):
            stOUT.wait(sy)
            if DEBUG:
                stDBG.wait(sy)
        blk.sync(_)

    return nc


def _host_layout(inputs):
    f = np.ascontiguousarray

    def wqkv(W):
        return f(W.T.reshape(NO, 128, NO, 128).transpose(2, 1, 0, 3))

    def wchunk(W, nch, co):
        return f(W.T.reshape(NO, 128, nch, co).transpose(2, 1, 0, 3))

    A = np.asarray(inputs["L0_ffn_lora_A"], np.float32)
    Bl = np.asarray(inputs["L0_ffn_lora_B"], np.float32)
    rw = np.asarray(inputs["L0_ffn_router_w"], np.float32)
    ar_full = np.concatenate(
        [A.transpose(2, 0, 1).reshape(D, 128), rw.T], axis=1)
    out = {
        "wq": wqkv(np.asarray(inputs["L0_attn_Wq"], np.float32)),
        "wk": wqkv(np.asarray(inputs["L0_attn_Wk"], np.float32)),
        "wv": wqkv(np.asarray(inputs["L0_attn_Wv"], np.float32)),
        "wo": wchunk(np.asarray(inputs["L0_attn_Wo"], np.float32), WO_NC, 512),
        "wt": wchunk(np.asarray(inputs["L0_ffn_weight"], np.float32), FF_NC, 512),
        "ar": f(ar_full.reshape(NO, 128, 136).transpose(1, 0, 2)),
        "bb": f(Bl.transpose(0, 2, 1).reshape(128, D)),
        "bq": f(np.asarray(inputs["L0_attn_bq"], np.float32).reshape(NO, 128).T),
        "bk": f(np.asarray(inputs["L0_attn_bk"], np.float32).reshape(NO, 128).T),
        "bv": f(np.asarray(inputs["L0_attn_bv"], np.float32).reshape(NO, 128).T),
        "bo": f(np.asarray(inputs["L0_attn_bo"], np.float32).reshape(1, D)),
        "idn": np.eye(128, dtype=np.float32),
        "epsc": np.full((128, 1), EPS, np.float32),
        "ones1": np.ones((1, 128), np.float32),
    }
    e2 = np.zeros((8, 128), np.float32)
    for e in range(8):
        e2[e, e * 16:(e + 1) * 16] = 2.0
    out["e2"] = e2
    return out


def kernel(**inputs):
    global _dbg_out
    x = np.asarray(inputs["L0_attn_x"], np.float32)
    Bsz, L, _ = x.shape
    xf = np.ascontiguousarray(x.reshape(Bsz * L, D))
    shared = _host_layout(inputs)

    nc = build_program()

    in_maps = []
    for c in range(8):
        m = dict(shared)
        m["x"] = np.ascontiguousarray(xf[c * T:(c + 1) * T])
        mask = np.zeros((8, 1), np.float32)
        s, p = c // 4, c % 4
        for r in range(8):
            if r // 4 == s and r % 4 < p:
                mask[r, 0] = 1.0
        m["maskc"] = mask
        in_maps.append(m)

    res = run_bass_kernel_spmd(nc, in_maps, core_ids=list(range(8)))
    if DEBUG:
        _dbg_out = res.results
    y = np.concatenate([res.results[c]["y"] for c in range(8)], axis=0)
    return y.reshape(Bsz, L, D)
